# version 02
# baseline (speedup 1.0000x reference)
import numpy as np
import ml_dtypes
import concourse.bass as bass
import concourse.mybir as mybir
from concourse.bass_utils import run_bass_kernel_spmd

F32 = mybir.dt.float32
BF16 = mybir.dt.bfloat16
I32 = mybir.dt.int32
U32 = mybir.dt.uint32
AF = mybir.ActivationFunctionType
ALU = mybir.AluOpType
BF16NP = ml_dtypes.bfloat16

B, N, D = 2, 8192, 1024
NC = 8
GR = 512
NG = N // GR
H, DH, HALF = 16, 64, 32
FFN = 4096
KSEL = N // 4
TOKC = 2048
TOKB = 1024


def own_granules(p):
    return [p, NG - 1 - p]


def own_positions(p):
    g0, g1 = own_granules(p)
    return np.concatenate([np.arange(g0 * GR, (g0 + 1) * GR),
                           np.arange(g1 * GR, (g1 + 1) * GR)])


def perm_qk():
    x1 = np.concatenate([np.arange(h * 64, h * 64 + 32) for h in range(H)])
    return np.concatenate([x1, x1 + 32])


PERM_QK = perm_qk()


def rope_tables(positions):
    pos = np.asarray(positions, dtype=np.float64)
    inv = 1.0 / 10000.0 ** (np.arange(HALF) / HALF)
    j = np.arange(128) % HALF
    ang = pos[None, :] * inv[j][:, None]
    return np.cos(ang).astype(np.float32), np.sin(ang).astype(np.float32)


class Sem:
    __slots__ = ("h", "n")

    def __init__(self, handle):
        self.h = handle
        self.n = 0


class Res:
    __slots__ = ("w", "r")

    def __init__(self):
        self.w = None
        self.r = []


class Prog:
    ENGINES = ("sync", "tensor", "vector", "scalar", "gpsimd")

    def __init__(self, nc, sem_pool=None):
        self.nc = nc
        self.ops = {e: [] for e in self.ENGINES}
        self._waited = {e: {} for e in self.ENGINES}
        self.sem_pool = list(sem_pool or [])
        self._res_sem = {}

    def _dma_sem(self, res):
        key = id(res)
        if key not in self._res_sem:
            if not self.sem_pool:
                raise RuntimeError("sem pool exhausted")
            self._res_sem[key] = self.sem_pool.pop()
        return self._res_sem[key]

    def emit(self, blk):
        for e in self.ENGINES:
            dec = getattr(blk, e)
            ops = self.ops[e]

            def body(E, ops=ops):
                for fn in ops:
                    fn(E)
            dec(body)

    def wait(self, eng, sem, val):
        if val is None or val <= 0:
            return
        seen = self._waited[eng].get(id(sem), 0)
        if val <= seen:
            return
        self._waited[eng][id(sem)] = val
        self.ops[eng].append(lambda E, s=sem.h, v=val: E.wait_ge(s, v))

    def _wait_tok(self, eng, tok):
        if tok is not None:
            self.wait(eng, tok[0], tok[1])

    def rd(self, eng, res_list):
        for r in res_list:
            self._wait_tok(eng, r.w)

    def wr(self, eng, res_list):
        for r in res_list:
            for tok in r.r:
                self._wait_tok(eng, tok)
            self._wait_tok(eng, r.w)

    def done_write(self, res_list, sem, val):
        for r in res_list:
            r.w = (sem, val)
            r.r = []

    def done_read(self, res_list, sem, val):
        for r in res_list:
            r.r.append((sem, val))

    def run(self, eng, f, sem, k=1, reads=(), writes=()):
        self.rd(eng, reads)
        self.wr(eng, writes)
        sem.n += k
        val = sem.n

        def g(E, f=f, s=sem.h, k=k):
            f(E).then_inc(s, k)
        self.ops[eng].append(g)
        self.done_write(writes, sem, val)
        self.done_read(reads, sem, val)
        return val

    def run0(self, eng, f, reads=(), writes=(), after=None):
        self.rd(eng, reads)
        self.wr(eng, writes)
        if after is not None:
            self._wait_tok(eng, after)
        self.ops[eng].append(lambda E, f=f: f(E))

    def dma(self, eng, out, in_, sem, reads=(), writes=(), **kw):
        is_store = not writes
        if writes:
            sem = self._dma_sem(writes[0])
        elif reads:
            sem = self._dma_sem(reads[0])
        val = self.run(eng, lambda E, o=out, i=in_, kw=kw: E.dma_start(out=o, i_=i, **kw) if False else E.dma_start(out=o, in_=i, **kw),
                       sem, k=16, reads=reads, writes=writes)
        if is_store:
            if not hasattr(self, "store_toks"):
                self.store_toks = []
            self.store_toks.append((sem, val))
        return val

    def finish(self, eng):
        for sem, val in getattr(self, "store_toks", []):
            self.wait(eng, sem, val)


def build_L1():
    nc = bass.Bass("TRN2", target_bir_lowering=False)
    x_in = nc.dram_tensor("x", [TOKC, D], F32, kind="ExternalInput").ap()
    ln1 = nc.dram_tensor("ln1", [1, D], F32, kind="ExternalInput").ap()
    w_q = nc.dram_tensor("wq", [D, D], F32, kind="ExternalInput").ap()
    w_k = nc.dram_tensor("wk", [D, D], F32, kind="ExternalInput").ap()
    w_v = nc.dram_tensor("wv", [D, D], F32, kind="ExternalInput").ap()
    w_uv = nc.dram_tensor("wuv", [D, 2 * D], F32, kind="ExternalInput").ap()
    cosT = nc.dram_tensor("cosT", [128, TOKC], F32, kind="ExternalInput").ap()
    sinT = nc.dram_tensor("sinT", [128, TOKC], F32, kind="ExternalInput").ap()
    identc = nc.dram_tensor("identc", [128, 128], F32, kind="ExternalInput").ap()
    constv = nc.dram_tensor("constv", [1, 8], F32, kind="ExternalInput").ap()

    xn_out = nc.dram_tensor("xn", [TOKC, D], F32, kind="ExternalOutput").ap()
    qT_out = nc.dram_tensor("qT", [D, TOKC], BF16, kind="ExternalOutput").ap()
    kT_out = nc.dram_tensor("kT", [D, TOKC], BF16, kind="ExternalOutput").ap()
    vT_out = nc.dram_tensor("vT", [D, TOKC], BF16, kind="ExternalOutput").ap()
    uT_out = nc.dram_tensor("uT", [D, TOKC], BF16, kind="ExternalOutput").ap()
    cT_out = nc.dram_tensor("cT", [D, TOKC], BF16, kind="ExternalOutput").ap()

    NT = TOKC // 128

    from contextlib import ExitStack
    with ExitStack() as ctx:
        E_ = ctx.enter_context
        xt = E_(nc.sbuf_tensor("xt", [128, 4, D], F32))
        xnt = E_(nc.sbuf_tensor("xnt", [128, 4, D], F32))
        hb = E_(nc.sbuf_tensor("hb", [128, NT, D], BF16))
        xnb = E_(nc.sbuf_tensor("xnb", [128, NT, D], BF16))
        hT = E_(nc.sbuf_tensor("hT", [128, 8, TOKC], BF16))
        xnT = E_(nc.sbuf_tensor("xnT", [128, 8, TOKC], BF16))
        wt = E_(nc.sbuf_tensor("wt", [128, 8, 2, 128], BF16))
        ident = E_(nc.sbuf_tensor("ident", [128, 128], BF16))
        lnb = E_(nc.sbuf_tensor("lnb", [128, D], F32))
        cosb = E_(nc.sbuf_tensor("cosb", [128, TOKC], F32))
        sinb = E_(nc.sbuf_tensor("sinb", [128, TOKC], F32))
        st6 = E_(nc.sbuf_tensor("st6", [128, NT, 12], F32))
        stat = E_(nc.sbuf_tensor("stat", [128, NT, 4], F32))
        scr = E_(nc.sbuf_tensor("scr", [128, D], BF16))
        cvb = E_(nc.sbuf_tensor("cvb", [128, 8], F32))
        ob = E_(nc.sbuf_tensor("ob", [128, 8, 512], BF16))
        ps = E_(nc.psum_tensor("ps", [128, 2, 512], F32))
        pst = E_(nc.psum_tensor("pst", [128, 2, 1024], BF16))
        _sd = E_(nc.semaphore("sem_d"))
        _st_ = E_(nc.semaphore("sem_t"))
        _sv = E_(nc.semaphore("sem_v"))
        _sa = E_(nc.semaphore("sem_a"))
        _sg = E_(nc.semaphore("sem_g"))
        _so = E_(nc.semaphore("sem_o"))
        sem_pool = [Sem(E_(nc.semaphore("sp%d" % _i))) for _i in range(48)]
        blk = E_(nc.Block())
        P = Prog(nc, sem_pool)
        sd = Sem(_sd)
        st = Sem(_st_)
        sv = Sem(_sv)
        sa = Sem(_sa)
        sg = Sem(_sg)
        so = Sem(_so)

        R_x = [Res() for _ in range(4)]
        R_xn = [Res() for _ in range(4)]
        R_xnb = [Res() for _ in range(NT)]
        R_hb = [Res() for _ in range(NT)]
        R_hT = Res()
        R_xnT = Res()
        R_wt = [[Res() for _ in range(2)] for _ in range(8)]
        R_ps = [Res() for _ in range(2)]
        R_pst = [Res() for _ in range(2)]
        R_ob = [Res() for _ in range(8)]
        R_st6 = [Res() for _ in range(NT)]
        R_stat = [Res() for _ in range(NT)]
        R_ident = Res()
        R_lnb = Res()
        R_cs = Res()
        R_scr = Res()

        P.dma("gpsimd", ident[:, :], identc[:, :], sg, writes=[R_ident])
        P.dma("sync", lnb[:, :], ln1.to_broadcast([128, D]), sd, writes=[R_lnb])
        R_cv = Res()
        P.dma("sync", cvb[:, :], constv.to_broadcast([128, 8]), sd, writes=[R_cv])
        P.dma("sync", cosb[:, :], cosT[:, :], sd, writes=[R_cs])
        P.dma("sync", sinb[:, :], sinT[:, :], sd, writes=[R_cs])
        for i in range(NT):
            s = i % 4
            P.dma("sync", xt[:, s, :], x_in[128 * i:128 * (i + 1), :], sd,
                  writes=[R_x[s]])
            P.run("vector", lambda E, i=i, s=s: E.bn_stats(st6[:, i, 0:6],
                                                           xt[:, s, 0:512]),
                  sv, reads=[R_x[s]], writes=[R_st6[i]])
            P.run("vector", lambda E, i=i, s=s: E.bn_stats(st6[:, i, 6:12],
                                                           xt[:, s, 512:1024]),
                  sv, reads=[R_x[s]], writes=[R_st6[i]])
            P.run("vector", lambda E, i=i: E.bn_aggr(stat[:, i, 0:2], st6[:, i, :]),
                  sv, reads=[R_st6[i]], writes=[R_stat[i]])
            P.run("scalar", lambda E, i=i: E.activation(
                stat[:, i, 2:3], stat[:, i, 1:2], AF.Ln, bias=cvb[:, 0:1]),
                sa, reads=[R_stat[i], R_cv], writes=[R_stat[i]])
            P.run("scalar", lambda E, i=i: E.activation(
                stat[:, i, 2:3], stat[:, i, 2:3], AF.Exp, scale=-0.5),
                sa, reads=[R_stat[i]], writes=[R_stat[i]])
            P.run("vector", lambda E, i=i, s=s: E.tensor_scalar(
                xnt[:, s, :], xt[:, s, :], stat[:, i, 0:1], stat[:, i, 2:3],
                op0=ALU.subtract, op1=ALU.mult),
                sv, reads=[R_x[s], R_stat[i]], writes=[R_xn[s]])
            P.run("vector", lambda E, i=i, s=s: E.tensor_copy(xnb[:, i, :], xnt[:, s, :]),
                  sv, reads=[R_xn[s]], writes=[R_xnb[i]])
            P.run("scalar", lambda E, i=i, s=s: E.activation(
                scr[:, :], xnt[:, s, :], AF.Square, accum_out=stat[:, i, 3:4]),
                sa, reads=[R_xn[s]], writes=[R_stat[i], R_scr])
            P.run("scalar", lambda E, i=i: E.activation(
                stat[:, i, 3:4], stat[:, i, 3:4], AF.Ln, bias=cvb[:, 1:2],
                scale=float(1.0 / D)),
                sa, reads=[R_stat[i], R_cv], writes=[R_stat[i]])
            P.run("scalar", lambda E, i=i: E.activation(
                stat[:, i, 3:4], stat[:, i, 3:4], AF.Exp, scale=-0.5),
                sa, reads=[R_stat[i]], writes=[R_stat[i]])
            P.run("vector", lambda E, i=i, s=s: E.scalar_tensor_tensor(
                hb[:, i, :], xnt[:, s, :], stat[:, i, 3:4], lnb[:, :],
                op0=ALU.mult, op1=ALU.mult),
                sv, reads=[R_xn[s], R_stat[i], R_lnb], writes=[R_hb[i]])
            P.dma("gpsimd", xn_out[128 * i:128 * (i + 1), :], xnt[:, s, :], so,
                  reads=[R_xn[s]])

        def transpose_all(src_tiles, src_res, dstT, dst_res):
            for i in range(NT):
                for c in range(8):
                    pb = (i * 8 + c) % 2
                    P.run("tensor", lambda E, i=i, c=c, pb=pb, s=src_tiles: E.transpose(
                        pst[:, pb, 0:128], s[:, i, 128 * c:128 * (c + 1)], ident[:, :]),
                        st, reads=[src_res[i], R_ident], writes=[R_pst[pb]])
                    P.run("vector", lambda E, i=i, c=c, pb=pb, d=dstT: E.tensor_copy(
                        d[:, c, 128 * i:128 * (i + 1)], pst[:, pb, 0:128]),
                        sv, reads=[R_pst[pb]], writes=[dst_res])

        transpose_all(hb, R_hb, hT, R_hT)
        transpose_all(xnb, R_xnb, xnT, R_xnT)

        def load_wchunks(w_dram, oc, buf):
            for c in range(8):
                P.dma("gpsimd", wt[:, c, buf, :],
                      w_dram[128 * c:128 * (c + 1), 128 * oc:128 * (oc + 1)],
                      sg, writes=[R_wt[c][buf]])
            return [R_wt[c][buf] for c in range(8)]

        def mm_chain(rhsT, rhs_res, wres, pb, ttok, extra_reads=()):
            for c in range(8):
                P.run0("tensor", lambda E, c=c, pb=pb, ttok=ttok, r=rhsT: E.matmul(
                    ps[:, pb, :], wt[:, c, wres[c][1] if False else 0, :],
                    r[:, c, 512 * ttok:512 * (ttok + 1)],
                    start=(c == 0), stop=(c == 7)))
            return None

        def project_plain(w_dram, rhsT, rhs_res, out_dram, silu):
            for oc in range(8):
                buf = oc % 2
                wres = load_wchunks(w_dram, oc, buf)
                for ttok in range(4):
                    pb = ttok % 2
                    P.rd("tensor", wres + [rhs_res])
                    P.wr("tensor", [R_ps[pb]])
                    for c in range(8):
                        last = (c == 7)
                        f = (lambda E, c=c, pb=pb, ttok=ttok, buf=buf, r=rhsT:
                             E.matmul(ps[:, pb, :], wt[:, c, buf, :],
                                      r[:, c, 512 * ttok:512 * (ttok + 1)],
                                      start=(c == 0), stop=(c == 7)))
                        if last:
                            st.n += 1
                            val = st.n
                            P.ops["tensor"].append(
                                lambda E, f=f, v=val: f(E).then_inc(_st_, 1))
                            P.done_write([R_ps[pb]], st, val)
                            P.done_read(wres + [rhs_res], st, val)
                        else:
                            P.ops["tensor"].append(lambda E, f=f: f(E))
                    slot = (oc * 4 + ttok) % 8
                    P.run("scalar", lambda E, pb=pb, slot=slot: E.activation(
                        ob[:, slot, :], ps[:, pb, :],
                        AF.Silu if silu else AF.Identity),
                        sa, reads=[R_ps[pb]], writes=[R_ob[slot]])
                    P.dma("gpsimd",
                          out_dram[128 * oc:128 * (oc + 1), 512 * ttok:512 * (ttok + 1)],
                          ob[:, slot, :], so, reads=[R_ob[slot]])

        def project_rope(w_dram, rhsT, rhs_res, out_dram):
            for ocp in range(4):
                wres0 = load_wchunks(w_dram, ocp, 0)
                wres1 = load_wchunks(w_dram, ocp + 4, 1)
                for ttok in range(4):
                    for half, wres in ((0, wres0), (1, wres1)):
                        P.rd("tensor", wres + [rhs_res])
                        P.wr("tensor", [R_ps[half]])
                        for c in range(8):
                            f = (lambda E, c=c, half=half, ttok=ttok, r=rhsT:
                                 E.matmul(ps[:, half, :], wt[:, c, half, :],
                                          r[:, c, 512 * ttok:512 * (ttok + 1)],
                                          start=(c == 0), stop=(c == 7)))
                            if c == 7:
                                st.n += 1
                                val = st.n
                                P.ops["tensor"].append(
                                    lambda E, f=f: f(E).then_inc(_st_, 1))
                                P.done_write([R_ps[half]], st, val)
                                P.done_read(wres + [rhs_res], st, val)
                            else:
                                P.ops["tensor"].append(lambda E, f=f: f(E))
                    b0, b1, b2, b3 = (R_ob[4], R_ob[5], R_ob[6], R_ob[7])
                    cs_ap = cosb[:, 512 * ttok:512 * (ttok + 1)]
                    sn_ap = sinb[:, 512 * ttok:512 * (ttok + 1)]
                    P.run("vector", lambda E, cs=cs_ap: E.tensor_tensor(
                        ob[:, 4, :], ps[:, 0, :], cs, op=ALU.mult),
                        sv, reads=[R_ps[0], R_cs], writes=[b0])
                    P.run("vector", lambda E, sn=sn_ap: E.tensor_tensor(
                        ob[:, 5, :], ps[:, 1, :], sn, op=ALU.mult),
                        sv, reads=[R_ps[1], R_cs], writes=[b1])
                    P.run("vector", lambda E, cs=cs_ap: E.tensor_tensor(
                        ob[:, 6, :], ps[:, 1, :], cs, op=ALU.mult),
                        sv, reads=[R_ps[1], R_cs], writes=[b2])
                    P.run("vector", lambda E, sn=sn_ap: E.tensor_tensor(
                        ob[:, 7, :], ps[:, 0, :], sn, op=ALU.mult),
                        sv, reads=[R_ps[0], R_cs], writes=[b3])
                    P.run("vector", lambda E: E.tensor_tensor(
                        ob[:, 4, :], ob[:, 4, :], ob[:, 5, :], op=ALU.subtract),
                        sv, reads=[b1], writes=[b0])
                    P.run("vector", lambda E: E.tensor_tensor(
                        ob[:, 6, :], ob[:, 6, :], ob[:, 7, :], op=ALU.add),
                        sv, reads=[b3], writes=[b2])
                    P.dma("gpsimd",
                          out_dram[128 * ocp:128 * (ocp + 1), 512 * ttok:512 * (ttok + 1)],
                          ob[:, 4, :], so, reads=[b0])
                    P.dma("gpsimd",
                          out_dram[128 * (ocp + 4):128 * (ocp + 5), 512 * ttok:512 * (ttok + 1)],
                          ob[:, 6, :], so, reads=[b2])

        project_rope(w_q, hT, R_hT, qT_out)
        project_rope(w_k, hT, R_hT, kT_out)
        project_plain(w_v, hT, R_hT, vT_out, silu=False)
        project_plain(w_uv[:, 0:D], xnT, R_xnT, uT_out, silu=True)
        project_plain(w_uv[:, D:2 * D], xnT, R_xnT, cT_out, silu=True)

        P.finish("gpsimd")
        P.emit(blk)
    return nc


_progs = {}


def _get_prog(name, builder):
    if name not in _progs:
        _progs[name] = builder()
    return _progs[name]


def _f32(a):
    return np.ascontiguousarray(a, dtype=np.float32)


def run_L1(inputs):
    nc = _get_prog("L1", build_L1)
    x = np.asarray(inputs["L0_attn_x"])
    wq = _f32(np.asarray(inputs["L0_ffn_wq1"])[:, PERM_QK])
    wk = _f32(np.asarray(inputs["L0_ffn_wk1"])[:, PERM_QK])
    wv = _f32(inputs["L0_ffn_wv1"])
    wuv = _f32(inputs["L0_attn_W_uv"])
    ln1 = _f32(inputs["L0_ffn_ln1_1"]).reshape(1, D)
    ident = np.eye(128, dtype=np.float32)
    in_maps = []
    for p in range(NC):
        posi = own_positions(p)
        x_own = np.concatenate([np.asarray(x[b])[posi] for b in range(B)], axis=0)
        cosT, sinT = rope_tables(np.concatenate([posi, posi]))
        consts = np.array([[1e-5, 1e-6, 0, 0, 0, 0, 0, 0]], np.float32)
        in_maps.append(dict(x=_f32(x_own), ln1=ln1, wq=wq, wk=wk, wv=wv, wuv=wuv,
                            cosT=cosT, sinT=sinT, identc=ident, constv=consts))
    res = run_bass_kernel_spmd(nc, in_maps, core_ids=list(range(NC)))
    return res.results




NITER = 32
SPAN = 68
EXPB = 24.0


def l2a_iter_meta(p):
    meta = []
    for i in range(NITER):
        a, u = i // 4, i % 4
        m = ((p + a) % 8) + 8 * u
        meta.append((a, m // 16, m % 16))
    return meta


def build_L2a():
    nc = bass.Bass("TRN2", target_bir_lowering=False)
    qseq = nc.dram_tensor("qseq", [NITER, 2, 64, GR], BF16, kind="ExternalInput").ap()
    kseq = nc.dram_tensor("kseq", [NITER, 64, SPAN * 128], BF16, kind="ExternalInput").ap()
    vswz = nc.dram_tensor("vswz", [NITER, 128, SPAN * 65], BF16, kind="ExternalInput").ap()
    maskc = nc.dram_tensor("maskc", [4 * 128, GR], F32, kind="ExternalInput").ap()
    constv2 = nc.dram_tensor("constv2", [1, 8], F32, kind="ExternalInput").ap()
    o_out = nc.dram_tensor("oat", [NITER, 2, 64, GR], BF16, kind="ExternalOutput").ap()

    from contextlib import ExitStack
    with ExitStack() as ctx:
        E_ = ctx.enter_context
        kb = E_(nc.sbuf_tensor("kb", [64, 2, SPAN * 128], BF16))
        vb = E_(nc.sbuf_tensor("vb", [128, 2, SPAN * 65], BF16))
        qb = E_(nc.sbuf_tensor("qb", [64, 2, 2, GR], BF16))
        mask = E_(nc.sbuf_tensor("mask", [128, 4, GR], BF16))
        pt = E_(nc.sbuf_tensor("pt", [128, 2, GR], BF16))
        rb = E_(nc.sbuf_tensor("rb", [64, GR], F32))
        rd = E_(nc.sbuf_tensor("rd", [1, GR], F32))
        ones1 = E_(nc.sbuf_tensor("ones1", [1, 64], F32))
        cvb2 = E_(nc.sbuf_tensor("cvb2", [128, 8], F32))
        ob = E_(nc.sbuf_tensor("ob2", [64, 2, GR], BF16))
        psS = E_(nc.psum_tensor("psS", [128, 2, 512], F32))
        psO = E_(nc.psum_tensor("psO", [128, 2, 512], F32))
        psB = E_(nc.psum_tensor("psB", [128, 512], F32))
        _sd = E_(nc.semaphore("sem_d"))
        _st_ = E_(nc.semaphore("sem_t"))
        _sv = E_(nc.semaphore("sem_v"))
        _sa = E_(nc.semaphore("sem_a"))
        _so = E_(nc.semaphore("sem_o"))
        _sg = E_(nc.semaphore("sem_g"))
        sem_pool = [Sem(E_(nc.semaphore("sp%d" % _i))) for _i in range(24)]
        blk = E_(nc.Block())

        P = Prog(nc, sem_pool)
        sd, st, sv, sa, so = Sem(_sd), Sem(_st_), Sem(_sv), Sem(_sa), Sem(_so)
        sg = Sem(_sg)

        R_kb = [Res() for _ in range(2)]
        R_vb = [Res() for _ in range(2)]
        R_qb = [Res() for _ in range(2)]
        R_mask = Res()
        R_pt = [Res() for _ in range(2)]
        R_psS = [Res() for _ in range(2)]
        R_psO = [Res() for _ in range(2)]
        R_psB = Res()
        R_rb = Res()
        R_rd = Res()
        R_ones = Res()
        R_ob = [Res() for _ in range(2)]

        P.dma("gpsimd", mask[:, :, :], maskc.rearrange("(a p) q -> p a q", p=128),
              sg, writes=[R_mask])
        P.run("vector", lambda E: E.memset(ones1[:, :], 1.0), sv, writes=[R_ones])
        R_cv2 = Res()
        P.dma("sync", cvb2[:, :], constv2.to_broadcast([128, 8]), sd, writes=[R_cv2])

        for i in range(NITER):
            a = i // 4
            nA, nB = 4 * a + 4, 64 - 4 * a
            ib = i % 2
            P.dma("sync", kb[:, ib, :], kseq[i, :, :], sd, writes=[R_kb[ib]])
            P.dma("sync", vb[:, ib, :], vswz[i, :, :], sd, writes=[R_vb[ib]])
            P.dma("sync", qb[:, ib, 0, :], qseq[i, 0, :, :], sd, writes=[R_qb[ib]])
            P.dma("sync", qb[:, ib, 1, :], qseq[i, 1, :, :], sd, writes=[R_qb[ib]])
            for gs, (base, nchunk) in enumerate(((0, nA), (nA, nB))):
                for c in range(nchunk):
                    sb = c % 2
                    col = (base + c) * 128
                    P.run("tensor", lambda E, ib=ib, gs=gs, sb=sb, col=col: E.matmul(
                        psS[:, sb, :], kb[:, ib, col:col + 128], qb[:, ib, gs, :],
                        start=True, stop=True),
                        st, reads=[R_kb[ib], R_qb[ib]], writes=[R_psS[sb]])
                    P.run("scalar", lambda E, sb=sb: E.activation(
                        pt[:, sb, :], psS[:, sb, :], AF.Exp, scale=0.125,
                        bias=cvb2[:, 0:1]),
                        sa, reads=[R_psS[sb], R_cv2], writes=[R_pt[sb]])
                    dd = c - (nchunk - 4)
                    if dd >= 0:
                        P.run("vector", lambda E, sb=sb, dd=dd: E.tensor_tensor(
                            pt[:, sb, :], pt[:, sb, :], mask[:, dd, :], op=ALU.mult),
                            sv, reads=[R_mask, R_pt[sb]], writes=[R_pt[sb]])
                    vcol = (base + c) * 65
                    P.run("tensor", lambda E, ib=ib, gs=gs, sb=sb, vcol=vcol, c=c,
                          nchunk=nchunk: E.matmul(
                        psO[0:65, gs, :], vb[:, ib, vcol:vcol + 65], pt[:, sb, :],
                        start=(c == 0), stop=(c == nchunk - 1)),
                        st, reads=[R_vb[ib], R_pt[sb]], writes=[R_psO[gs]])
                P.run("vector", lambda E, gs=gs: E.reciprocal(
                    rd[:, :], psO[64:65, gs, :]),
                    sv, reads=[R_psO[gs]], writes=[R_rd])
                P.run("tensor", lambda E: E.matmul(
                    psB[0:64, :], ones1[:, :], rd[:, :], start=True, stop=True),
                    st, reads=[R_rd, R_ones], writes=[R_psB])
                P.run("vector", lambda E: E.tensor_copy(rb[:, :], psB[0:64, :]),
                      sv, reads=[R_psB], writes=[R_rb])
                P.run("vector", lambda E, gs=gs: E.tensor_tensor(
                    ob[:, gs, :], psO[0:64, gs, :], rb[:, :], op=ALU.mult),
                    sv, reads=[R_psO[gs], R_rb], writes=[R_ob[gs]])
                P.dma("gpsimd", o_out[i, gs, :, :], ob[:, gs, :], so,
                      reads=[R_ob[gs]])
        P.finish("gpsimd")
        P.emit(blk)
    return nc


def host_prep_L2a(l1_results):
    kT = [np.zeros((D, N), BF16NP) for _ in range(B)]
    vfull = [np.zeros((N, D), BF16NP) for _ in range(B)]
    qT = []
    for p in range(NC):
        r = l1_results[p]
        posi = own_positions(p)
        for b in range(B):
            kT[b][:, posi] = np.asarray(r["kT"])[:, b * TOKB:(b + 1) * TOKB]
            vfull[b][posi, :] = np.asarray(r["vT"])[:, b * TOKB:(b + 1) * TOKB].T
        qT.append(np.asarray(r["qT"]))
    in_maps = []
    for p in range(NC):
        meta = l2a_iter_meta(p)
        qseq = np.zeros((NITER, 2, 64, GR), BF16NP)
        kseq = np.zeros((NITER, 64, SPAN * 128), BF16NP)
        vswz = np.ones((NITER, 128, SPAN * 65), BF16NP)
        for i, (a, b, h) in enumerate(meta):
            rows = np.concatenate([np.arange(32 * h, 32 * h + 32),
                                   np.arange(512 + 32 * h, 512 + 32 * h + 32)])
            for gs, g in enumerate((a, NG - 1 - a)):
                owner = g if g < 8 else NG - 1 - g
                slot = 0 if g == owner else 1
                cols = b * TOKB + slot * GR
                qseq[i, gs] = qT[owner][rows, cols:cols + GR]
            nA = 4 * a + 4
            nB = 64 - 4 * a
            kseq[i, :, 0:nA * 128] = kT[b][rows, 0:nA * 128]
            kseq[i, :, nA * 128:SPAN * 128] = kT[b][rows, 0:nB * 128]
            vA = vfull[b][0:nA * 128, 64 * h:64 * h + 64].reshape(nA, 128, 64)
            vB = vfull[b][0:nB * 128, 64 * h:64 * h + 64].reshape(nB, 128, 64)
            vs = vswz[i].reshape(128, SPAN, 65)
            vs[:, 0:nA, 0:64] = vA.transpose(1, 0, 2)
            vs[:, nA:SPAN, 0:64] = vB.transpose(1, 0, 2)
        in_maps.append(dict(qseq=qseq, kseq=kseq, vswz=vswz, maskc=l2a_masks(),
                            constv2=np.array([[-EXPB, 0, 0, 0, 0, 0, 0, 0]], np.float32)))
    return in_maps


def l2a_masks():
    kk = np.arange(128)[:, None]
    qq = np.arange(GR)[None, :]
    m = np.zeros((4, 128, GR), np.float32)
    for d_ in range(4):
        m[d_] = (128 * d_ + kk <= qq).astype(np.float32)
    return m.reshape(4 * 128, GR)


def run_L2a(l1_results):
    nc = _get_prog("L2a", build_L2a)
    in_maps = host_prep_L2a(l1_results)
    res = run_bass_kernel_spmd(nc, in_maps, core_ids=list(range(NC)))
    return res.results


def build_L2b():
    nc = bass.Bass("TRN2", target_bir_lowering=False)
    OT_in = nc.dram_tensor("OT", [D, TOKC], BF16, kind="ExternalInput").ap()
    xn_in = nc.dram_tensor("xn", [TOKC, D], F32, kind="ExternalInput").ap()
    wo1t = nc.dram_tensor("wo1t", [8 * 128, 8 * 128], F32, kind="ExternalInput").ap()
    wgt = nc.dram_tensor("wgt", [32 * 128, 8 * 128], F32, kind="ExternalInput").ap()
    wut = nc.dram_tensor("wut", [32 * 128, 8 * 128], F32, kind="ExternalInput").ap()
    wdt = nc.dram_tensor("wdt", [8 * 128, 32 * 128], F32, kind="ExternalInput").ap()
    prwt = nc.dram_tensor("prwt", [16 * 128, 8 * 128], F32, kind="ExternalInput").ap()
    ln2 = nc.dram_tensor("ln2", [1, D], F32, kind="ExternalInput").ap()
    pnw = nc.dram_tensor("pnw", [1, D], F32, kind="ExternalInput").ap()
    identc = nc.dram_tensor("identc", [128, 128], F32, kind="ExternalInput").ap()
    constv = nc.dram_tensor("constv", [1, 8], F32, kind="ExternalInput").ap()

    proc_out = nc.dram_tensor("proc", [TOKC, D], F32, kind="ExternalOutput").ap()
    dst_out = nc.dram_tensor("dstch", [TOKC, 2], F32, kind="ExternalOutput").ap()

    wob = nc.dram_tensor("wob", [8 * 128, 8 * 128], BF16).ap()
    wgb = nc.dram_tensor("wgb", [32 * 128, 8 * 128], BF16).ap()
    wub = nc.dram_tensor("wub", [32 * 128, 8 * 128], BF16).ap()
    wdb = nc.dram_tensor("wdb", [8 * 128, 32 * 128], BF16).ap()
    prb = nc.dram_tensor("prb", [16 * 128, 8 * 128], BF16).ap()
    mlv = nc.dram_tensor("mlv", [TOKC, 2 * D], BF16).ap()

    from contextlib import ExitStack
    with ExitStack() as ctx:
        E_ = ctx.enter_context
        big1 = E_(nc.sbuf_tensor("big1", [128, 8, TOKC], BF16))
        big2 = E_(nc.sbuf_tensor("big2", [128, 8, TOKC], BF16))
        gub = E_(nc.sbuf_tensor("gub", [128, 32, 512], BF16))
        mlvb = E_(nc.sbuf_tensor("mlvb", [128, 4, 2048], BF16))
        x1b = E_(nc.sbuf_tensor("x1b", [128, 4, D], F32))
        xnb4 = E_(nc.sbuf_tensor("xnb4", [128, 4, D], F32))
        wt = E_(nc.sbuf_tensor("wt2", [128, 2, 8, 128], BF16))
        wtd = E_(nc.sbuf_tensor("wtd", [128, 2, 32, 128], BF16))
        ob = E_(nc.sbuf_tensor("ob3", [128, 6, 512], BF16))
        hb2 = E_(nc.sbuf_tensor("hb2", [128, 2, D], BF16))
        resb = E_(nc.sbuf_tensor("resb", [128, D], F32))
        eb = E_(nc.sbuf_tensor("eb", [128, D], F32))
        sqb = E_(nc.sbuf_tensor("sqb", [128, D], F32))
        lnb2 = E_(nc.sbuf_tensor("lnb2", [128, D], F32))
        pnb = E_(nc.sbuf_tensor("pnb", [128, D], F32))
        cvb = E_(nc.sbuf_tensor("cvb3", [128, 8], F32))
        ident = E_(nc.sbuf_tensor("ident2", [128, 128], BF16))
        stat = E_(nc.sbuf_tensor("stat2", [128, 16, 4], F32))
        dacc = E_(nc.sbuf_tensor("dacc", [128, 16, 2], F32))
        scr = E_(nc.sbuf_tensor("scr2", [128, D], BF16))
        ps = E_(nc.psum_tensor("ps2", [128, 4, 512], F32))
        pst = E_(nc.psum_tensor("pst2", [128, 2, 1024], BF16))
        _sd = E_(nc.semaphore("sem_d"))
        _st_ = E_(nc.semaphore("sem_t"))
        _sv = E_(nc.semaphore("sem_v"))
        _sa = E_(nc.semaphore("sem_a"))
        _sg = E_(nc.semaphore("sem_g"))
        _so = E_(nc.semaphore("sem_o"))
        sem_pool = [Sem(E_(nc.semaphore("sp%d" % _i))) for _i in range(24)]
        blk = E_(nc.Block())

        P = Prog(nc, sem_pool)
        sd, st, sv, sa, sg, so = (Sem(_sd), Sem(_st_), Sem(_sv), Sem(_sa),
                                  Sem(_sg), Sem(_so))

        R_big1 = Res()
        R_big2 = Res()
        R_gub = [Res() for _ in range(32)]
        R_mlvb = [Res() for _ in range(4)]
        R_x1 = [Res() for _ in range(4)]
        R_xn4 = [Res() for _ in range(4)]
        R_wt = [Res() for _ in range(2)]
        R_wtd = [Res() for _ in range(2)]
        R_ob = [Res() for _ in range(6)]
        R_hb2 = [Res() for _ in range(2)]
        R_res = Res()
        R_eb = Res()
        R_sqb = Res()
        R_ln2 = Res()
        R_pnb = Res()
        R_cv = Res()
        R_ident = Res()
        R_stat = [Res() for _ in range(16)]
        R_dacc = Res()
        R_scr = Res()
        R_ps = [Res() for _ in range(4)]
        R_pst = [Res() for _ in range(2)]
        R_cast = Res()

        P.dma("gpsimd", ident[:, :], identc[:, :], sg, writes=[R_ident])
        for srcap, dstap in ((wo1t, wob), (wgt, wgb), (wut, wub), (wdt, wdb),
                             (prwt, prb)):
            P.dma("gpsimd", dstap[:, :], srcap[:, :], sg, writes=[R_cast])
        P.dma("sync", lnb2[:, :], ln2.to_broadcast([128, D]), sd, writes=[R_ln2])
        P.dma("sync", pnb[:, :], pnw.to_broadcast([128, D]), sd, writes=[R_pnb])
        P.dma("sync", cvb[:, :], constv.to_broadcast([128, 8]), sd, writes=[R_cv])
        for c in range(8):
            P.dma("sync", big1[:, c, :], OT_in[128 * c:128 * (c + 1), :], sd,
                  writes=[R_big1])

        def transpose128(src_ap, dst_ap, reads, writes, pb):
            P.run("tensor", lambda E, s=src_ap, pb=pb: E.transpose(
                pst[:, pb, 0:128], s, ident[:, :]),
                st, reads=list(reads) + [R_ident], writes=[R_pst[pb]])
            P.run("vector", lambda E, d=dst_ap, pb=pb: E.tensor_copy(d, pst[:, pb, 0:128]),
                  sv, reads=[R_pst[pb]], writes=list(writes))

        tpb = 0
        for ti in range(16):
            sl = ti % 4
            P.dma("sync", xnb4[:, sl, :], xn_in[128 * ti:128 * (ti + 1), :], sd,
                  writes=[R_xn4[sl]])
            P.run("scalar", lambda E, sl=sl, ti=ti: E.activation(
                scr[:, :], xnb4[:, sl, :], AF.Square, accum_out=stat[:, ti, 0:1]),
                sa, reads=[R_xn4[sl]], writes=[R_stat[ti], R_scr])
            P.run("scalar", lambda E, ti=ti: E.activation(
                stat[:, ti, 0:1], stat[:, ti, 0:1], AF.Ln, bias=cvb[:, 1:2],
                scale=float(1.0 / D)),
                sa, reads=[R_stat[ti], R_cv], writes=[R_stat[ti]])
            P.run("scalar", lambda E, ti=ti: E.activation(
                stat[:, ti, 0:1], stat[:, ti, 0:1], AF.Exp, scale=-0.5),
                sa, reads=[R_stat[ti]], writes=[R_stat[ti]])
            hsl = ti % 2
            P.run("vector", lambda E, sl=sl, ti=ti, hsl=hsl: E.scalar_tensor_tensor(
                hb2[:, hsl, :], xnb4[:, sl, :], stat[:, ti, 0:1], pnb[:, :],
                op0=ALU.mult, op1=ALU.mult),
                sv, reads=[R_xn4[sl], R_stat[ti], R_pnb], writes=[R_hb2[hsl]])
            for c in range(8):
                transpose128(hb2[:, hsl, 128 * c:128 * (c + 1)],
                             big2[:, c, 128 * ti:128 * (ti + 1)],
                             [R_hb2[hsl]], [R_big2], tpb % 2)
                tpb += 1
        for ttok in range(4):
            for oc in range(16):
                wb = oc % 2
                P.dma("sync", wt[:, wb, :, :],
                      prb[128 * oc:128 * (oc + 1), :].rearrange("p (c j) -> p c j", c=8),
                      sd, writes=[R_wt[wb]])
                pb = oc % 2
                P.rd("tensor", [R_wt[wb], R_big2])
                P.wr("tensor", [R_ps[pb]])
                for c in range(8):
                    f = (lambda E, c=c, oc=oc, ttok=ttok, wb=wb, pb=pb:
                         E.matmul(ps[:, pb, :], wt[:, wb, c, :],
                                  big2[:, c, 512 * ttok:512 * (ttok + 1)],
                                  start=(c == 0), stop=(c == 7)))
                    if c == 7:
                        st.n += 1
                        P.ops["tensor"].append(lambda E, f=f: f(E).then_inc(_st_, 1))
                        P.done_write([R_ps[pb]], st, st.n)
                        P.done_read([R_wt[wb], R_big2], st, st.n)
                    else:
                        P.ops["tensor"].append(lambda E, f=f: f(E))
                slot = oc % 4
                P.run("scalar", lambda E, pb=pb, slot=slot: E.activation(
                    ob[:, slot, :], ps[:, pb, :], AF.Identity),
                    sa, reads=[R_ps[pb]], writes=[R_ob[slot]])
                for sub in range(4):
                    transpose128(ob[:, slot, 128 * sub:128 * (sub + 1)],
                                 mlvb[:, sub, 128 * oc:128 * (oc + 1)],
                                 [R_ob[slot]], [R_mlvb[sub]], tpb % 2)
                    tpb += 1
            for sub in range(4):
                ti = 4 * ttok + sub
                P.dma("gpsimd", mlv[128 * ti:128 * (ti + 1), :], mlvb[:, sub, :],
                      sg, reads=[R_mlvb[sub]])

        for ttok in range(4):
            for sub in range(4):
                ti = 4 * ttok + sub
                sl = sub
                P.dma("sync", xnb4[:, sl, :], xn_in[128 * ti:128 * (ti + 1), :],
                      sd, writes=[R_xn4[sl]])
            for oc in range(8):
                wb = oc % 2
                P.dma("sync", wt[:, wb, :, :],
                      wob[128 * oc:128 * (oc + 1), :].rearrange("p (c j) -> p c j", c=8),
                      sd, writes=[R_wt[wb]])
                pb = oc % 2
                P.rd("tensor", [R_wt[wb], R_big1])
                P.wr("tensor", [R_ps[pb]])
                for c in range(8):
                    f = (lambda E, c=c, wb=wb, pb=pb, ttok=ttok:
                         E.matmul(ps[:, pb, :], wt[:, wb, c, :],
                                  big1[:, c, 512 * ttok:512 * (ttok + 1)],
                                  start=(c == 0), stop=(c == 7)))
                    if c == 7:
                        st.n += 1
                        P.ops["tensor"].append(lambda E, f=f: f(E).then_inc(_st_, 1))
                        P.done_write([R_ps[pb]], st, st.n)
                        P.done_read([R_wt[wb], R_big1], st, st.n)
                    else:
                        P.ops["tensor"].append(lambda E, f=f: f(E))
                slot = 4 + (oc % 2)
                P.run("scalar", lambda E, pb=pb, slot=slot: E.activation(
                    ob[:, slot, :], ps[:, pb, :], AF.Identity),
                    sa, reads=[R_ps[pb]], writes=[R_ob[slot]])
                for sub in range(4):
                    pb2 = tpb % 2
                    P.run("tensor", lambda E, slot=slot, sub=sub, pb2=pb2: E.transpose(
                        pst[:, pb2, 0:128], ob[:, slot, 128 * sub:128 * (sub + 1)],
                        ident[:, :]),
                        st, reads=[R_ob[slot], R_ident], writes=[R_pst[pb2]])
                    P.run("vector", lambda E, sub=sub, oc=oc, pb2=pb2: E.tensor_tensor(
                        x1b[:, sub, 128 * oc:128 * (oc + 1)],
                        xnb4[:, sub, 128 * oc:128 * (oc + 1)],
                        pst[:, pb2, 0:128], op=ALU.add),
                        sv, reads=[R_pst[pb2], R_xn4[sub]], writes=[R_x1[sub]])
                    tpb += 1
            for sub in range(4):
                ti = 4 * ttok + sub
                P.run("scalar", lambda E, sub=sub, ti=ti: E.activation(
                    scr[:, :], x1b[:, sub, :], AF.Square, accum_out=stat[:, ti, 1:2]),
                    sa, reads=[R_x1[sub]], writes=[R_stat[ti], R_scr])
                P.run("scalar", lambda E, ti=ti: E.activation(
                    stat[:, ti, 1:2], stat[:, ti, 1:2], AF.Ln, bias=cvb[:, 1:2],
                    scale=float(1.0 / D)),
                    sa, reads=[R_stat[ti], R_cv], writes=[R_stat[ti]])
                P.run("scalar", lambda E, ti=ti: E.activation(
                    stat[:, ti, 1:2], stat[:, ti, 1:2], AF.Exp, scale=-0.5),
                    sa, reads=[R_stat[ti]], writes=[R_stat[ti]])
                hsl = sub % 2
                P.run("vector", lambda E, sub=sub, ti=ti, hsl=hsl: E.scalar_tensor_tensor(
                    hb2[:, hsl, :], x1b[:, sub, :], stat[:, ti, 1:2], lnb2[:, :],
                    op0=ALU.mult, op1=ALU.mult),
                    sv, reads=[R_x1[sub], R_stat[ti], R_ln2], writes=[R_hb2[hsl]])
                for c in range(8):
                    transpose128(hb2[:, hsl, 128 * c:128 * (c + 1)],
                                 big2[:, c, 128 * (4 * ttok + sub):128 * (4 * ttok + sub + 1)],
                                 [R_hb2[hsl]], [R_big2], tpb % 2)
                    tpb += 1
            for nch in range(32):
                wb = nch % 2
                P.dma("sync", wt[:, wb, :, :],
                      wgb[128 * nch:128 * (nch + 1), :].rearrange("p (c j) -> p c j", c=8),
                      sd, writes=[R_wt[wb]])
                P.rd("tensor", [R_wt[wb], R_big2])
                P.wr("tensor", [R_ps[2]])
                for c in range(8):
                    f = (lambda E, c=c, wb=wb, ttok=ttok:
                         E.matmul(ps[:, 2, :], wt[:, wb, c, :],
                                  big2[:, c, 512 * ttok:512 * (ttok + 1)],
                                  start=(c == 0), stop=(c == 7)))
                    if c == 7:
                        st.n += 1
                        P.ops["tensor"].append(lambda E, f=f: f(E).then_inc(_st_, 1))
                        P.done_write([R_ps[2]], st, st.n)
                        P.done_read([R_wt[wb], R_big2], st, st.n)
                    else:
                        P.ops["tensor"].append(lambda E, f=f: f(E))
                wb2 = nch % 2
                P.dma("gpsimd", wtd[:, wb2, 0:8, :],
                      wub[128 * nch:128 * (nch + 1), :].rearrange("p (c j) -> p c j", c=8),
                      sg, writes=[R_wtd[wb2]])
                P.rd("tensor", [R_wtd[wb2], R_big2])
                P.wr("tensor", [R_ps[3]])
                for c in range(8):
                    f = (lambda E, c=c, wb2=wb2, ttok=ttok:
                         E.matmul(ps[:, 3, :], wtd[:, wb2, c, :],
                                  big2[:, c, 512 * ttok:512 * (ttok + 1)],
                                  start=(c == 0), stop=(c == 7)))
                    if c == 7:
                        st.n += 1
                        P.ops["tensor"].append(lambda E, f=f: f(E).then_inc(_st_, 1))
                        P.done_write([R_ps[3]], st, st.n)
                        P.done_read([R_wtd[wb2], R_big2], st, st.n)
                    else:
                        P.ops["tensor"].append(lambda E, f=f: f(E))
                P.run("scalar", lambda E: E.activation(
                    ob[:, 0, :], ps[:, 2, :], AF.Silu),
                    sa, reads=[R_ps[2]], writes=[R_ob[0]])
                P.run("vector", lambda E, nch=nch: E.tensor_tensor(
                    gub[:, nch, :], ob[:, 0, :], ps[:, 3, :], op=ALU.mult),
                    sv, reads=[R_ob[0], R_ps[3]], writes=[R_gub[nch]])
            for oc in range(8):
                wb = oc % 2
                P.dma("sync", wtd[:, wb, :, :],
                      wdb[128 * oc:128 * (oc + 1), :].rearrange("p (n j) -> p n j", n=32),
                      sd, writes=[R_wtd[wb]])
                pb = oc % 2
                P.rd("tensor", [R_wtd[wb]] + R_gub)
                P.wr("tensor", [R_ps[pb]])
                for n in range(32):
                    f = (lambda E, n=n, wb=wb, pb=pb:
                         E.matmul(ps[:, pb, :], wtd[:, wb, n, :], gub[:, n, :],
                                  start=(n == 0), stop=(n == 31)))
                    if n == 31:
                        st.n += 1
                        P.ops["tensor"].append(lambda E, f=f: f(E).then_inc(_st_, 1))
                        P.done_write([R_ps[pb]], st, st.n)
                        P.done_read([R_wtd[wb]] + R_gub, st, st.n)
                    else:
                        P.ops["tensor"].append(lambda E, f=f: f(E))
                slot = 4 + (oc % 2)
                P.run("scalar", lambda E, pb=pb, slot=slot: E.activation(
                    ob[:, slot, :], ps[:, pb, :], AF.Identity),
                    sa, reads=[R_ps[pb]], writes=[R_ob[slot]])
                for sub in range(4):
                    pb2 = tpb % 2
                    P.run("tensor", lambda E, slot=slot, sub=sub, pb2=pb2: E.transpose(
                        pst[:, pb2, 0:128], ob[:, slot, 128 * sub:128 * (sub + 1)],
                        ident[:, :]),
                        st, reads=[R_ob[slot], R_ident], writes=[R_pst[pb2]])
                    P.run("vector", lambda E, sub=sub, oc=oc, pb2=pb2: E.tensor_tensor(
                        x1b[:, sub, 128 * oc:128 * (oc + 1)],
                        x1b[:, sub, 128 * oc:128 * (oc + 1)],
                        pst[:, pb2, 0:128], op=ALU.add),
                        sv, reads=[R_pst[pb2]], writes=[R_x1[sub]])
                    tpb += 1
            for sub in range(4):
                ti = 4 * ttok + sub
                P.dma("gpsimd", proc_out[128 * ti:128 * (ti + 1), :], x1b[:, sub, :],
                      so, reads=[R_x1[sub]])
                P.dma("sync", mlvb[:, 0, :], mlv[128 * ti:128 * (ti + 1), :], sd,
                      writes=[R_mlvb[0]])
                P.run("vector", lambda E, sub=sub: E.tensor_tensor(
                    resb[:, :], x1b[:, sub, :], xnb4[:, sub, :], op=ALU.subtract),
                    sv, reads=[R_x1[sub], R_xn4[sub]], writes=[R_res])
                P.run("scalar", lambda E, ti=ti: E.activation(
                    scr[:, :], resb[:, :], AF.Square, accum_out=dacc[:, ti, 0:1]),
                    sa, reads=[R_res], writes=[R_dacc, R_scr])
                P.run("scalar", lambda E: E.activation(
                    eb[:, :], mlvb[:, 0, 1024:2048], AF.Exp, scale=-1.0),
                    sa, reads=[R_mlvb[0]], writes=[R_eb])
                P.run("vector", lambda E: E.tensor_tensor(
                    sqb[:, :], resb[:, :], mlvb[:, 0, 0:1024], op=ALU.subtract),
                    sv, reads=[R_res, R_mlvb[0]], writes=[R_sqb])
                P.run("scalar", lambda E: E.activation(
                    sqb[:, :], sqb[:, :], AF.Square),
                    sa, reads=[R_sqb], writes=[R_sqb])
                P.run("vector", lambda E: E.tensor_tensor(
                    sqb[:, :], sqb[:, :], eb[:, :], op=ALU.mult),
                    sv, reads=[R_sqb, R_eb], writes=[R_sqb])
                P.run("vector", lambda E: E.tensor_tensor(
                    sqb[:, :], sqb[:, :], eb[:, :], op=ALU.add),
                    sv, reads=[R_sqb, R_eb], writes=[R_sqb])
                P.run("vector", lambda E, ti=ti: E.scalar_tensor_tensor(
                    scr[:, :], sqb[:, :], -1.0, mlvb[:, 0, 1024:2048],
                    op0=ALU.add, op1=ALU.add, accum_out=dacc[:, ti, 1:2]),
                    sv, reads=[R_sqb, R_mlvb[0]], writes=[R_scr, R_dacc])
        P.run("vector", lambda E: E.tensor_scalar(
            dacc[:, :, 0:1].rearrange("p t c -> p (t c)"),
            dacc[:, :, 0:1].rearrange("p t c -> p (t c)"),
            float(1.0 / D), None, op0=ALU.mult),
            sv, reads=[R_dacc], writes=[R_dacc])
        P.run("vector", lambda E: E.tensor_scalar(
            dacc[:, :, 1:2].rearrange("p t c -> p (t c)"),
            dacc[:, :, 1:2].rearrange("p t c -> p (t c)"),
            float(0.5 / D), None, op0=ALU.mult),
            sv, reads=[R_dacc], writes=[R_dacc])
        P.dma("gpsimd", dst_out.rearrange("(t p) c -> p t c", p=128),
              dacc[:, :, :], so, reads=[R_dacc])
        P.finish("gpsimd")
        P.emit(blk)
    return nc


def _tile_w(w, KCH, OCH):
    w = np.ascontiguousarray(w, dtype=np.float32)
    K, O = w.shape
    assert K == KCH * 128 and O == OCH * 128
    t = w.reshape(KCH, 128, OCH, 128).transpose(2, 1, 0, 3)
    return np.ascontiguousarray(t.reshape(OCH * 128, KCH * 128))


def host_prep_L2b(inputs, l1_results, l2a_results):
    OTs = [np.zeros((D, TOKC), BF16NP) for _ in range(NC)]
    for p in range(NC):
        meta = l2a_iter_meta(p)
        oat = np.asarray(l2a_results[p]["oat"])
        for i, (a, b, h) in enumerate(meta):
            for gs, g in enumerate((a, NG - 1 - a)):
                owner = g if g < 8 else NG - 1 - g
                slot = 0 if g == owner else 1
                OTs[owner][64 * h:64 * h + 64,
                           b * TOKB + slot * GR:b * TOKB + (slot + 1) * GR] = oat[i, gs]
    wo1t = _tile_w(inputs["L0_ffn_wo1"], 8, 8)
    wgt = _tile_w(inputs["L0_ffn_wg1"], 8, 32)
    wut = _tile_w(inputs["L0_ffn_wu1"], 8, 32)
    wdt = _tile_w(inputs["L0_ffn_wd1"], 32, 8)
    prwt = _tile_w(inputs["L0_ffn_prior_w"], 8, 16)
    ln2 = _f32(inputs["L0_ffn_ln2_1"]).reshape(1, D)
    pnw = _f32(inputs["L0_ffn_prior_norm_w"]).reshape(1, D)
    ident = np.eye(128, dtype=np.float32)
    consts = np.array([[1e-5, 1e-6, 0, 0, 0, 0, 0, 0]], np.float32)
    in_maps = []
    for p in range(NC):
        in_maps.append(dict(OT=OTs[p], xn=np.asarray(l1_results[p]["xn"]),
                            wo1t=wo1t, wgt=wgt, wut=wut, wdt=wdt, prwt=prwt,
                            ln2=ln2, pnw=pnw, identc=ident, constv=consts))
    return in_maps


def run_L2b(inputs, l1_results, l2a_results):
    nc = _get_prog("L2b", build_L2b)
    in_maps = host_prep_L2b(inputs, l1_results, l2a_results)
    res = run_bass_kernel_spmd(nc, in_maps, core_ids=list(range(NC)))
    return res.results


NBIS = 48


def build_L3():
    nc = bass.Bass("TRN2", target_bir_lowering=False)
    dsti = nc.dram_tensor("dst", [128, 128], F32, kind="ExternalInput").ap()
    dchi = nc.dram_tensor("dch", [128, 128], F32, kind="ExternalInput").ap()
    procfull = nc.dram_tensor("procfull", [B * N, D], F32, kind="ExternalInput").ap()
    cstabi = nc.dram_tensor("cstab", [N, 64], F32, kind="ExternalInput").ap()
    w_q2 = nc.dram_tensor("wq2", [D, D], F32, kind="ExternalInput").ap()
    w_k2 = nc.dram_tensor("wk2", [D, D], F32, kind="ExternalInput").ap()
    w_v2 = nc.dram_tensor("wv2", [D, D], F32, kind="ExternalInput").ap()
    ln12 = nc.dram_tensor("ln12", [1, D], F32, kind="ExternalInput").ap()
    identc = nc.dram_tensor("identc", [128, 128], F32, kind="ExternalInput").ap()
    constv = nc.dram_tensor("constv", [1, 8], F32, kind="ExternalInput").ap()
    bselc = nc.dram_tensor("bselc", [128, 2], F32, kind="ExternalInput").ap()
    bselTc = nc.dram_tensor("bselTc", [2, 128], F32, kind="ExternalInput").ap()
    lprefc = nc.dram_tensor("lprefc", [128, 128], F32, kind="ExternalInput").ap()
    iotaFc = nc.dram_tensor("iotaFc", [128, 128], F32, kind="ExternalInput").ap()
    fcolc = nc.dram_tensor("fcolc", [128, 1], F32, kind="ExternalInput").ap()
    ccolc = nc.dram_tensor("ccolc", [128, 64], F32, kind="ExternalInput").ap()
    repcc = nc.dram_tensor("repcc", [32, 128], F32, kind="ExternalInput").ap()
    rkbasec = nc.dram_tensor("rkbase", [1, 8], F32, kind="ExternalInput").ap()

    q2T_out = nc.dram_tensor("q2T", [D, 512], BF16, kind="ExternalOutput").ap()
    k2T_out = nc.dram_tensor("k2T", [D, 512], BF16, kind="ExternalOutput").ap()
    v2T_out = nc.dram_tensor("v2T", [D, 512], BF16, kind="ExternalOutput").ap()
    sel_out = nc.dram_tensor("selo", [512, D], F32, kind="ExternalOutput").ap()
    sc_out = nc.dram_tensor("sco", [128, 8], F32, kind="ExternalOutput").ap()
    rkm_out = nc.dram_tensor("rkm", [128, 256], F32, kind="ExternalOutput").ap()

    from contextlib import ExitStack
    with ExitStack() as ctx:
        E_ = ctx.enter_context
        g2 = E_(nc.sbuf_tensor("g2", [128, 128], F32))
        gT = E_(nc.sbuf_tensor("gT", [128, 128], BF16))
        rkT = E_(nc.sbuf_tensor("rkT", [128, 128], F32))
        rkTs = E_(nc.sbuf_tensor("rkTs", [128, 128], F32))
        smT = E_(nc.sbuf_tensor("smT", [128, 128], F32))
        work = E_(nc.sbuf_tensor("work", [128, 12, 128], F32))
        zer = E_(nc.sbuf_tensor("zer", [128, 128], F32))
        sc1 = E_(nc.sbuf_tensor("sc1", [128, 16], F32))
        onr = E_(nc.sbuf_tensor("onr", [1, 128], F32))
        bsel = E_(nc.sbuf_tensor("bsel", [128, 2], F32))
        bselT = E_(nc.sbuf_tensor("bselT", [2, 128], F32))
        lpref = E_(nc.sbuf_tensor("lpref", [128, 128], F32))
        iotaF = E_(nc.sbuf_tensor("iotaF", [128, 128], F32))
        fcol = E_(nc.sbuf_tensor("fcol", [128, 1], BF16))
        ccols = E_(nc.sbuf_tensor("ccols", [128, 64], BF16))
        repcb = E_(nc.sbuf_tensor("repcb", [32, 128], BF16))
        ohb = E_(nc.sbuf_tensor("ohb", [128, 2, 128], BF16))
        idxf = E_(nc.sbuf_tensor("idxf", [128, 8], F32))
        idxu = E_(nc.sbuf_tensor("idxu", [128, 2, 4], U32))
        selb = E_(nc.sbuf_tensor("selb", [128, 4, D], F32))
        csg = E_(nc.sbuf_tensor("csg", [128, 4, 64], F32))
        csrC = E_(nc.sbuf_tensor("csrC", [32, 512], BF16))
        csrS = E_(nc.sbuf_tensor("csrS", [32, 512], BF16))
        cs2 = E_(nc.sbuf_tensor("cs2", [128, 2, 512], F32))
        hb3 = E_(nc.sbuf_tensor("hb3", [128, 4, D], BF16))
        h2T = E_(nc.sbuf_tensor("h2T3", [128, 8, 512], BF16))
        wt = E_(nc.sbuf_tensor("wt3", [128, 8, 2, 128], BF16))
        ob = E_(nc.sbuf_tensor("ob4", [128, 8, 512], BF16))
        lnb = E_(nc.sbuf_tensor("lnb3", [128, D], F32))
        cvb = E_(nc.sbuf_tensor("cvb4", [128, 8], F32))
        rkb = E_(nc.sbuf_tensor("rkb", [128, 8], F32))
        ident = E_(nc.sbuf_tensor("ident3", [128, 128], BF16))
        identF = E_(nc.sbuf_tensor("identF", [128, 128], F32))
        stat = E_(nc.sbuf_tensor("stat3", [128, 4, 4], F32))
        scr = E_(nc.sbuf_tensor("scr3", [128, D], BF16))
        ps = E_(nc.psum_tensor("ps3", [128, 4, 512], F32))
        psF = E_(nc.psum_tensor("psF3", [128, 128], F32))
        pst = E_(nc.psum_tensor("pst3", [128, 2, 1024], BF16))
        _sd = E_(nc.semaphore("sem_d"))
        _st_ = E_(nc.semaphore("sem_t"))
        _sv = E_(nc.semaphore("sem_v"))
        _sa = E_(nc.semaphore("sem_a"))
        _sg = E_(nc.semaphore("sem_g"))
        _so = E_(nc.semaphore("sem_o"))
        sem_pool = [Sem(E_(nc.semaphore("sp%d" % _i))) for _i in range(24)]
        blk = E_(nc.Block())

        P = Prog(nc, sem_pool)
        sd, st, sv, sa, sg, so = (Sem(_sd), Sem(_st_), Sem(_sv), Sem(_sa),
                                  Sem(_sg), Sem(_so))

        Rs = lambda n: [Res() for _ in range(n)]
        R_g2, R_gT, R_rkT, R_rkTs, R_smT = Res(), Res(), Res(), Res(), Res()
        R_work = Rs(12)
        R_sc1, R_const, R_zer = Res(), Res(), Res()
        R_ohb = Rs(2)
        R_idxf, R_idxu = Res(), Res()
        R_selb = Rs(4)
        R_csg, R_csr, R_cs2 = Res(), Res(), Res()
        R_hb3 = Rs(4)
        R_h2T = Res()
        R_wt = Rs(2)
        R_ob = Rs(8)
        R_stat = Rs(4)
        R_scr = Res()
        R_ps = Rs(4)
        R_psF = Res()
        R_pst = Rs(2)
        R_ident, R_identF = Res(), Res()

        P.dma("gpsimd", ident[:, :], identc[:, :], sg, writes=[R_ident])
        P.dma("gpsimd", fcol[:, :], fcolc[:, :], sg, writes=[R_const])
        P.dma("gpsimd", ccols[:, :], ccolc[:, :], sg, writes=[R_const])
        P.dma("gpsimd", repcb[:, :], repcc[:, :], sg, writes=[R_const])
        P.dma("sync", identF[:, :], identc[:, :], sd, writes=[R_identF])
        P.dma("sync", bsel[:, :], bselc[:, :], sd, writes=[R_const])
        P.dma("sync", bselT[:, :], bselTc[:, :], sd, writes=[R_const])
        P.dma("sync", lpref[:, :], lprefc[:, :], sd, writes=[R_const])
        P.dma("sync", iotaF[:, :], iotaFc[:, :], sd, writes=[R_const])
        P.dma("sync", cvb[:, :], constv.to_broadcast([128, 8]), sd, writes=[R_const])
        P.dma("sync", rkb[:, :], rkbasec.to_broadcast([128, 8]), sd, writes=[R_const])
        P.dma("sync", lnb[:, :], ln12.to_broadcast([128, D]), sd, writes=[R_const])
        P.dma("sync", work[:, 0, :], dsti[:, :], sd, writes=[R_work[0]])
        P.dma("sync", work[:, 1, :], dchi[:, :], sd, writes=[R_work[1]])
        P.run("vector", lambda E: E.memset(zer[:, :], 0.0), sv, writes=[R_zer])
        R_onr = Res()
        P.run("vector", lambda E: E.memset(onr[:, :], 1.0), sv, writes=[R_onr])

        P.run("vector", lambda E: E.tensor_scalar(
            work[:, 2, :], work[:, 0, :], 0.0, None, op0=ALU.add, op1=ALU.add,
            accum_out=sc1[:, 0:1]),
            sv, reads=[R_work[0]], writes=[R_work[2], R_sc1])
        P.run("vector", lambda E: E.tensor_scalar(
            work[:, 2, :], work[:, 1, :], 0.0, None, op0=ALU.add, op1=ALU.add,
            accum_out=sc1[:, 1:2]),
            sv, reads=[R_work[1]], writes=[R_work[2], R_sc1])
        P.run("vector", lambda E: E.tensor_tensor(
            sc1[:, 2:3], sc1[:, 0:1], sc1[:, 1:2], op=ALU.add),
            sv, reads=[R_sc1], writes=[R_sc1])
        P.run("vector", lambda E: E.tensor_tensor(
            sc1[:, 8:9], bsel[:, 0:1], bsel[:, 1:2], op=ALU.add),
            sv, reads=[R_const], writes=[R_sc1])
        P.run("tensor", lambda E: E.matmul(
            psF[0:1, 0:1], sc1[:, 2:3], sc1[:, 8:9], start=True, stop=True),
            st, reads=[R_sc1], writes=[R_psF])
        P.run("vector", lambda E: E.tensor_scalar(
            sc1[0:1, 3:4], psF[0:1, 0:1], float(-1.0 / (B * N)), None, op0=ALU.mult),
            sv, reads=[R_psF], writes=[R_sc1])
        P.run("tensor", lambda E: E.matmul(
            psF[:, 0:1], onr[:, :], sc1[0:1, 3:4], start=True, stop=True),
            st, reads=[R_sc1, R_onr], writes=[R_psF])
        P.run("vector", lambda E: E.tensor_copy(sc1[:, 4:5], psF[:, 0:1]),
              sv, reads=[R_psF], writes=[R_sc1])
        P.run("vector", lambda E: E.tensor_tensor(
            g2[:, :], work[:, 0, :], work[:, 1, :], op=ALU.add),
            sv, reads=[R_work[0], R_work[1]], writes=[R_g2])
        P.run("scalar", lambda E: E.activation(
            g2[:, :], g2[:, :], AF.Sigmoid, bias=sc1[:, 4:5]),
            sa, reads=[R_g2, R_sc1], writes=[R_g2])

        P.run("vector", lambda E: E.memset(work[:, 2, 0:1], -1.0), sv,
              writes=[R_work[2]])
        P.run("vector", lambda E: E.memset(work[:, 3, 0:1], 2.0), sv,
              writes=[R_work[3]])

        def count_gt(thr_col, Rthr):
            P.run("vector", lambda E, t=thr_col: E.tensor_scalar(
                work[:, 5, :], g2[:, :], t, None, op0=ALU.is_gt,
                op1=ALU.add, accum_out=sc1[:, 5:6]),
                sv, reads=[R_g2, Rthr], writes=[R_work[5], R_sc1])
            P.run("tensor", lambda E: E.matmul(
                psF[0:2, 0:1], bsel[:, :], sc1[:, 5:6], start=True, stop=True),
                st, reads=[R_sc1, R_const], writes=[R_psF])
            P.run("vector", lambda E: E.tensor_copy(sc1[0:2, 6:7], psF[0:2, 0:1]),
                  sv, reads=[R_psF], writes=[R_sc1])
            P.run("tensor", lambda E: E.matmul(
                psF[:, 0:1], bselT[:, :], sc1[0:2, 6:7], start=True, stop=True),
                st, reads=[R_sc1, R_const], writes=[R_psF])
            P.run("vector", lambda E: E.tensor_copy(sc1[:, 7:8], psF[:, 0:1]),
                  sv, reads=[R_psF], writes=[R_sc1])

        for it in range(NBIS):
            P.run("vector", lambda E: E.tensor_tensor(
                work[:, 4, 0:1], work[:, 2, 0:1], work[:, 3, 0:1], op=ALU.add),
                sv, reads=[R_work[2], R_work[3]], writes=[R_work[4]])
            P.run("vector", lambda E: E.tensor_scalar(
                work[:, 4, 0:1], work[:, 4, 0:1], 0.5, None, op0=ALU.mult),
                sv, reads=[R_work[4]], writes=[R_work[4]])
            count_gt(work[:, 4, 0:1], R_work[4])
            P.run("vector", lambda E: E.tensor_scalar(
                work[:, 6, 0:1], sc1[:, 7:8], float(KSEL), None, op0=ALU.is_ge),
                sv, reads=[R_sc1], writes=[R_work[6]])
            P.run("vector", lambda E: E.tensor_tensor(
                work[:, 7, 0:1], work[:, 4, 0:1], work[:, 2, 0:1], op=ALU.subtract),
                sv, reads=[R_work[4], R_work[2]], writes=[R_work[7]])
            P.run("vector", lambda E: E.scalar_tensor_tensor(
                work[:, 2, 0:1], work[:, 7, 0:1], work[:, 6, 0:1], work[:, 2, 0:1],
                op0=ALU.mult, op1=ALU.add),
                sv, reads=[R_work[7], R_work[6], R_work[2]], writes=[R_work[2]])
            P.run("vector", lambda E: E.tensor_tensor(
                work[:, 7, 0:1], work[:, 4, 0:1], work[:, 3, 0:1], op=ALU.subtract),
                sv, reads=[R_work[4], R_work[3]], writes=[R_work[7]])
            P.run("vector", lambda E: E.tensor_scalar(
                work[:, 8, 0:1], work[:, 6, 0:1], float(-1.0), float(1.0),
                op0=ALU.mult, op1=ALU.add),
                sv, reads=[R_work[6]], writes=[R_work[8]])
            P.run("vector", lambda E: E.scalar_tensor_tensor(
                work[:, 3, 0:1], work[:, 7, 0:1], work[:, 8, 0:1], work[:, 3, 0:1],
                op0=ALU.mult, op1=ALU.add),
                sv, reads=[R_work[7], R_work[8], R_work[3]], writes=[R_work[3]])
        count_gt(work[:, 3, 0:1], R_work[3])
        P.run("vector", lambda E: E.tensor_scalar(
            work[:, 6, 0:1], sc1[:, 7:8], float(-1.0), float(KSEL),
            op0=ALU.mult, op1=ALU.add),
            sv, reads=[R_sc1], writes=[R_work[6]])
        P.run("vector", lambda E: E.tensor_scalar(
            work[:, 7, :], g2[:, :], work[:, 3, 0:1], None, op0=ALU.is_equal),
            sv, reads=[R_g2, R_work[3]], writes=[R_work[7]])

        def hier_scan(src, dst, tmp):
            P.run("vector", lambda E, s=src, t=tmp: E.tensor_tensor_scan(
                work[:, t, :], work[:, s, :], zer[:, :],
                initial=0.0, op0=ALU.add, op1=ALU.bypass),
                sv, reads=[R_work[src], R_zer], writes=[R_work[tmp]])
            P.run("vector", lambda E, s=src, d=dst, t=tmp: E.tensor_tensor(
                work[:, d, :], work[:, t, :], work[:, s, :], op=ALU.subtract),
                sv, reads=[R_work[tmp], R_work[src]], writes=[R_work[dst]])
            P.run("tensor", lambda E, t=tmp: E.matmul(
                psF[:, 0:1], lpref[:, :], work[:, t, 127:128],
                start=True, stop=True),
                st, reads=[R_work[tmp], R_const], writes=[R_psF])
            P.run("vector", lambda E: E.tensor_copy(sc1[:, 9:10], psF[:, 0:1]),
                  sv, reads=[R_psF], writes=[R_sc1])
            P.run("vector", lambda E, d=dst: E.tensor_scalar(
                work[:, d, :], work[:, d, :], sc1[:, 9:10], None, op0=ALU.add),
                sv, reads=[R_sc1, R_work[dst]], writes=[R_work[dst]])

        hier_scan(7, 8, 9)
        P.run("vector", lambda E: E.tensor_scalar(
            work[:, 9, :], work[:, 8, :], work[:, 6, 0:1], None, op0=ALU.is_lt),
            sv, reads=[R_work[8], R_work[6]], writes=[R_work[9]])
        P.run("vector", lambda E: E.tensor_tensor(
            work[:, 9, :], work[:, 9, :], work[:, 7, :], op=ALU.mult),
            sv, reads=[R_work[9], R_work[7]], writes=[R_work[9]])
        P.run("vector", lambda E: E.tensor_tensor(
            work[:, 9, :], work[:, 9, :], work[:, 5, :], op=ALU.add),
            sv, reads=[R_work[9], R_work[5]], writes=[R_work[9]])
        hier_scan(9, 10, 4)
        P.dma("gpsimd", rkm_out[:, 0:128], work[:, 10, :], so, reads=[R_work[10]])
        P.dma("gpsimd", rkm_out[:, 128:256], work[:, 9, :], so, reads=[R_work[9]])

        P.run("vector", lambda E: E.tensor_copy(scr[:, 0:128], g2[:, :]),
              sv, reads=[R_g2], writes=[R_scr])
        P.run("tensor", lambda E: E.transpose(
            pst[:, 0, 0:128], scr[:, 0:128], ident[:, :]),
            st, reads=[R_scr, R_ident], writes=[R_pst[0]])
        P.run("vector", lambda E: E.tensor_copy(gT[:, :], pst[:, 0, 0:128]),
              sv, reads=[R_pst[0]], writes=[R_gT])
        for src, dst, dres, sres in ((10, rkT, R_rkT, R_work[10]),
                                     (9, smT, R_smT, R_work[9])):
            P.run("tensor", lambda E, s=src: E.transpose(
                psF[:, :], work[:, s, :], identF[:, :]),
                st, reads=[sres, R_identF], writes=[R_psF])
            P.run("vector", lambda E, d=dst: E.tensor_copy(d[:, :], psF[:, :]),
                  sv, reads=[R_psF], writes=[dres])

        for tl in range(4):
            bb = tl // 2
            P.run("vector", lambda E, tl=tl: E.tensor_scalar(
                rkTs[:, :], rkT[:, :], rkb[:, tl:tl + 1], None, op0=ALU.subtract),
                sv, reads=[R_rkT, R_const], writes=[R_rkTs])
            for c in range(64):
                cc = bb * 64 + c
                hb_ = c % 2
                P.run("vector", lambda E, cc=cc, hb_=hb_: E.tensor_scalar(
                    ohb[:, hb_, :], iotaF[:, :], rkTs[:, cc:cc + 1], None,
                    op0=ALU.is_equal),
                    sv, reads=[R_const, R_rkTs], writes=[R_ohb[hb_]])
                P.run("vector", lambda E, cc=cc, hb_=hb_: E.tensor_scalar(
                    ohb[:, hb_, :], ohb[:, hb_, :], smT[:, cc:cc + 1], None,
                    op0=ALU.mult),
                    sv, reads=[R_smT, R_ohb[hb_]], writes=[R_ohb[hb_]])
                P.run("tensor", lambda E, c=c, hb_=hb_: E.matmul(
                    ps[:, 0, 0:1], ohb[:, hb_, :], fcol[:, :],
                    start=(c == 0), stop=(c == 63)),
                    st, reads=[R_ohb[hb_], R_const], writes=[R_ps[0]])
                P.run("tensor", lambda E, c=c, cc=cc, hb_=hb_: E.matmul(
                    ps[:, 1, 0:1], ohb[:, hb_, :], gT[:, cc:cc + 1],
                    start=(c == 0), stop=(c == 63)),
                    st, reads=[R_ohb[hb_], R_gT], writes=[R_ps[1]])
                P.run("tensor", lambda E, c=c, hb_=hb_: E.matmul(
                    ps[:, 2, 0:1], ohb[:, hb_, :], ccols[:, c:c + 1],
                    start=(c == 0), stop=(c == 63)),
                    st, reads=[R_ohb[hb_], R_const], writes=[R_ps[2]])
            P.run("vector", lambda E, tl=tl: E.tensor_scalar(
                idxf[:, tl:tl + 1], ps[:, 2, 0:1], 128.0, None, op0=ALU.mult),
                sv, reads=[R_ps[2]], writes=[R_idxf])
            P.run("vector", lambda E, tl=tl: E.tensor_tensor(
                idxf[:, tl:tl + 1], idxf[:, tl:tl + 1], ps[:, 0, 0:1], op=ALU.add),
                sv, reads=[R_ps[0], R_idxf], writes=[R_idxf])
            P.run("vector", lambda E, tl=tl: E.tensor_copy(
                idxf[:, 4 + tl:5 + tl], ps[:, 1, 0:1]),
                sv, reads=[R_ps[1]], writes=[R_idxf])
        P.dma("gpsimd", sc_out[:, :], idxf[:, :], so, reads=[R_idxf])

        P.run("vector", lambda E: E.tensor_copy(idxu[:, 0, :], idxf[:, 0:4]),
              sv, reads=[R_idxf], writes=[R_idxu])
        P.run("vector", lambda E: E.tensor_copy(work[:, 4, 0:4], idxf[:, 0:4]),
              sv, reads=[R_idxf], writes=[R_work[4]])
        P.run("vector", lambda E: E.tensor_scalar(
            work[:, 4, 2:4], work[:, 4, 2:4], float(N), None, op0=ALU.add),
            sv, reads=[R_work[4]], writes=[R_work[4]])
        P.run("vector", lambda E: E.tensor_copy(idxu[:, 1, :], work[:, 4, 0:4]),
              sv, reads=[R_work[4]], writes=[R_idxu])

        for tl in range(4):
            P.run("gpsimd", lambda E, tl=tl: E.indirect_dma_start(
                out=selb[:, tl, :], out_offset=None, in_=procfull[:, :],
                in_offset=bass.IndirectOffsetOnAxis(ap=idxu[:, 1, tl:tl + 1], axis=0),
                bounds_check=B * N - 1, oob_is_err=True),
                P._dma_sem(R_selb[tl]), k=16, reads=[R_idxu], writes=[R_selb[tl]])
            P.run("gpsimd", lambda E, tl=tl: E.indirect_dma_start(
                out=csg[:, tl, :], out_offset=None, in_=cstabi[:, :],
                in_offset=bass.IndirectOffsetOnAxis(ap=idxu[:, 0, tl:tl + 1], axis=0),
                bounds_check=N - 1, oob_is_err=True),
                P._dma_sem(R_csg), k=16, reads=[R_idxu], writes=[R_csg])
        for tl in range(4):
            P.dma("gpsimd", sel_out[128 * tl:128 * (tl + 1), :], selb[:, tl, :],
                  so, reads=[R_selb[tl]])

        for tl in range(4):
            P.run("tensor", lambda E, tl=tl: E.transpose(
                psF[0:64, 0:128], csg[:, tl, :], identF[:, :]),
                st, reads=[R_csg, R_identF], writes=[R_psF])
            P.run("vector", lambda E, tl=tl: E.tensor_copy(
                csrC[:, 128 * tl:128 * (tl + 1)], psF[0:32, 0:128]),
                sv, reads=[R_psF], writes=[R_csr])
            P.run("vector", lambda E, tl=tl: E.tensor_copy(
                csrS[:, 128 * tl:128 * (tl + 1)], psF[32:64, 0:128]),
                sv, reads=[R_psF], writes=[R_csr])
        P.run("tensor", lambda E: E.matmul(
            ps[:, 3, :], repcb[:, :], csrC[:, :], start=True, stop=True),
            st, reads=[R_csr, R_const], writes=[R_ps[3]])
        P.run("vector", lambda E: E.tensor_copy(cs2[:, 0, :], ps[:, 3, :]),
              sv, reads=[R_ps[3]], writes=[R_cs2])
        P.run("tensor", lambda E: E.matmul(
            ps[:, 3, :], repcb[:, :], csrS[:, :], start=True, stop=True),
            st, reads=[R_csr, R_const], writes=[R_ps[3]])
        P.run("vector", lambda E: E.tensor_copy(cs2[:, 1, :], ps[:, 3, :]),
              sv, reads=[R_ps[3]], writes=[R_cs2])

        for tl in range(4):
            P.run("scalar", lambda E, tl=tl: E.activation(
                scr[:, :], selb[:, tl, :], AF.Square, accum_out=stat[:, tl, 0:1]),
                sa, reads=[R_selb[tl]], writes=[R_stat[tl], R_scr])
            P.run("scalar", lambda E, tl=tl: E.activation(
                stat[:, tl, 0:1], stat[:, tl, 0:1], AF.Ln, bias=cvb[:, 1:2],
                scale=float(1.0 / D)),
                sa, reads=[R_stat[tl], R_const], writes=[R_stat[tl]])
            P.run("scalar", lambda E, tl=tl: E.activation(
                stat[:, tl, 0:1], stat[:, tl, 0:1], AF.Exp, scale=-0.5),
                sa, reads=[R_stat[tl]], writes=[R_stat[tl]])
            P.run("vector", lambda E, tl=tl: E.scalar_tensor_tensor(
                hb3[:, tl, :], selb[:, tl, :], stat[:, tl, 0:1], lnb[:, :],
                op0=ALU.mult, op1=ALU.mult),
                sv, reads=[R_selb[tl], R_stat[tl], R_const], writes=[R_hb3[tl]])
            for c in range(8):
                pb = (tl * 8 + c) % 2
                P.run("tensor", lambda E, tl=tl, c=c, pb=pb: E.transpose(
                    pst[:, pb, 0:128], hb3[:, tl, 128 * c:128 * (c + 1)], ident[:, :]),
                    st, reads=[R_hb3[tl], R_ident], writes=[R_pst[pb]])
                P.run("vector", lambda E, tl=tl, c=c, pb=pb: E.tensor_copy(
                    h2T[:, c, 128 * tl:128 * (tl + 1)], pst[:, pb, 0:128]),
                    sv, reads=[R_pst[pb]], writes=[R_h2T])

        def load_w(w_dram, oc, buf):
            for c in range(8):
                P.dma("gpsimd", wt[:, c, buf, :],
                      w_dram[128 * c:128 * (c + 1), 128 * oc:128 * (oc + 1)],
                      sg, writes=[R_wt[buf]])
            return R_wt[buf]

        def mmchain(wres, buf, bank):
            P.rd("tensor", [wres, R_h2T])
            P.wr("tensor", [R_ps[bank]])
            for c in range(8):
                f = (lambda E, c=c, buf=buf, bank=bank:
                     E.matmul(ps[:, bank, :], wt[:, c, buf, :], h2T[:, c, :],
                              start=(c == 0), stop=(c == 7)))
                if c == 7:
                    st.n += 1
                    P.ops["tensor"].append(lambda E, f=f: f(E).then_inc(_st_, 1))
                    P.done_write([R_ps[bank]], st, st.n)
                    P.done_read([wres, R_h2T], st, st.n)
                else:
                    P.ops["tensor"].append(lambda E, f=f: f(E))

        def rope_proj(w_dram, out_dram):
            for ocp in range(4):
                w0 = load_w(w_dram, ocp, 0)
                w1 = load_w(w_dram, ocp + 4, 1)
                mmchain(w0, 0, 0)
                mmchain(w1, 1, 1)
                b0, b1, b2, b3 = R_ob[4], R_ob[5], R_ob[6], R_ob[7]
                P.run("vector", lambda E: E.tensor_tensor(
                    ob[:, 4, :], ps[:, 0, :], cs2[:, 0, :], op=ALU.mult),
                    sv, reads=[R_ps[0], R_cs2], writes=[b0])
                P.run("vector", lambda E: E.tensor_tensor(
                    ob[:, 5, :], ps[:, 1, :], cs2[:, 1, :], op=ALU.mult),
                    sv, reads=[R_ps[1], R_cs2], writes=[b1])
                P.run("vector", lambda E: E.tensor_tensor(
                    ob[:, 6, :], ps[:, 1, :], cs2[:, 0, :], op=ALU.mult),
                    sv, reads=[R_ps[1], R_cs2], writes=[b2])
                P.run("vector", lambda E: E.tensor_tensor(
                    ob[:, 7, :], ps[:, 0, :], cs2[:, 1, :], op=ALU.mult),
                    sv, reads=[R_ps[0], R_cs2], writes=[b3])
                P.run("vector", lambda E: E.tensor_tensor(
                    ob[:, 4, :], ob[:, 4, :], ob[:, 5, :], op=ALU.subtract),
                    sv, reads=[b1], writes=[b0])
                P.run("vector", lambda E: E.tensor_tensor(
                    ob[:, 6, :], ob[:, 6, :], ob[:, 7, :], op=ALU.add),
                    sv, reads=[b3], writes=[b2])
                P.dma("gpsimd", out_dram[128 * ocp:128 * (ocp + 1), :], ob[:, 4, :],
                      so, reads=[b0])
                P.dma("gpsimd", out_dram[128 * (ocp + 4):128 * (ocp + 5), :],
                      ob[:, 6, :], so, reads=[b2])

        rope_proj(w_q2, q2T_out)
        rope_proj(w_k2, k2T_out)
        for oc in range(8):
            w0 = load_w(w_v2, oc, oc % 2)
            mmchain(w0, oc % 2, 2)
            slot = oc % 4
            P.run("scalar", lambda E, slot=slot: E.activation(
                ob[:, slot, :], ps[:, 2, :], AF.Identity),
                sa, reads=[R_ps[2]], writes=[R_ob[slot]])
            P.dma("gpsimd", v2T_out[128 * oc:128 * (oc + 1), :], ob[:, slot, :],
                  so, reads=[R_ob[slot]])

        P.finish("gpsimd")
        P.emit(blk)
    return nc


def host_prep_L3(inputs, l2b_results):
    dst = np.zeros(B * N, np.float32)
    dch = np.zeros(B * N, np.float32)
    for p in range(NC):
        posi = own_positions(p)
        d = np.asarray(l2b_results[p]["dstch"])
        for b in range(B):
            dst[b * N + posi] = d[b * TOKB:(b + 1) * TOKB, 0]
            dch[b * N + posi] = d[b * TOKB:(b + 1) * TOKB, 1]
    dst = dst.reshape(128, 128)
    dch = dch.reshape(128, 128)
    procfull = np.zeros((B * N, D), np.float32)
    for p in range(NC):
        posi = own_positions(p)
        pr = np.asarray(l2b_results[p]["proc"])
        for b in range(B):
            procfull[b * N + posi] = pr[b * TOKB:(b + 1) * TOKB]
    pos = np.arange(N, dtype=np.float64)
    inv = 1.0 / 10000.0 ** (np.arange(HALF) / HALF)
    ang = pos[:, None] * inv[None, :]
    cstab = np.concatenate([np.cos(ang), np.sin(ang)], 1).astype(np.float32)
    ident = np.eye(128, dtype=np.float32)
    consts = np.array([[1e-5, 1e-6, 0, 0, 0, 0, 0, 0]], np.float32)
    pp = np.arange(128)
    bselc = np.stack([(pp < 64).astype(np.float32),
                      (pp >= 64).astype(np.float32)], 1)
    bselTc = bselc.T.copy()
    lprefc = ((pp[:, None] < pp[None, :]) &
              ((pp[:, None] // 64) == (pp[None, :] // 64))).astype(np.float32)
    iotaFc = np.tile(np.arange(128, dtype=np.float32)[None, :], (128, 1))
    fcolc = np.arange(128, dtype=np.float32)[:, None].copy()
    ccolc = np.tile(np.arange(64, dtype=np.float32)[None, :], (128, 1))
    repcc = np.zeros((32, 128), np.float32)
    for i in range(32):
        repcc[i, np.arange(128) % 32 == i] = 1.0
    wq2 = _f32(np.asarray(inputs["L0_ffn_wq2"])[:, PERM_QK])
    wk2 = _f32(np.asarray(inputs["L0_ffn_wk2"])[:, PERM_QK])
    wv2 = _f32(inputs["L0_ffn_wv2"])
    ln12 = _f32(inputs["L0_ffn_ln1_2"]).reshape(1, D)
    in_maps = []
    for p in range(NC):
        rkbase = np.array([[256 * p, 256 * p + 128,
                            256 * (7 - p), 256 * (7 - p) + 128, 0, 0, 0, 0]],
                          np.float32)
        in_maps.append(dict(dst=dst, dch=dch, procfull=procfull, cstab=cstab,
                            wq2=wq2, wk2=wk2, wv2=wv2, ln12=ln12,
                            identc=ident, constv=consts, bselc=bselc,
                            bselTc=bselTc, lprefc=lprefc, iotaFc=iotaFc,
                            fcolc=fcolc, ccolc=ccolc, repcc=repcc,
                            rkbase=rkbase))
    return in_maps


def run_L3(inputs, l2b_results):
    nc = _get_prog("L3", build_L3)
    in_maps = host_prep_L3(inputs, l2b_results)
    res = run_bass_kernel_spmd(nc, in_maps, core_ids=list(range(NC)))
    return res.results


def build_L4():
    nc = bass.Bass("TRN2", target_bir_lowering=False)
    q2T_in = nc.dram_tensor("q2T", [D, 512], BF16, kind="ExternalInput").ap()
    k2f = nc.dram_tensor("k2f", [D, B * KSEL], BF16, kind="ExternalInput").ap()
    vswz2 = nc.dram_tensor("vswz2", [B * H, 128, 16 * 65], BF16, kind="ExternalInput").ap()
    sel_in = nc.dram_tensor("seli", [512, D], F32, kind="ExternalInput").ap()
    sc_in = nc.dram_tensor("sci", [128, 8], F32, kind="ExternalInput").ap()
    mask2c = nc.dram_tensor("mask2c", [128, 2 * 16 * 256], F32, kind="ExternalInput").ap()
    wo2t = nc.dram_tensor("wo2t", [8 * 128, 8 * 128], F32, kind="ExternalInput").ap()
    wg2t = nc.dram_tensor("wg2t", [32 * 128, 8 * 128], F32, kind="ExternalInput").ap()
    wu2t = nc.dram_tensor("wu2t", [32 * 128, 8 * 128], F32, kind="ExternalInput").ap()
    wd2t = nc.dram_tensor("wd2t", [8 * 128, 32 * 128], F32, kind="ExternalInput").ap()
    ln22 = nc.dram_tensor("ln22", [1, D], F32, kind="ExternalInput").ap()
    identc = nc.dram_tensor("identc", [128, 128], F32, kind="ExternalInput").ap()
    constv = nc.dram_tensor("constv", [1, 8], F32, kind="ExternalInput").ap()

    upd_out = nc.dram_tensor("updo", [512, D], F32, kind="ExternalOutput").ap()

    wob = nc.dram_tensor("wob4", [8 * 128, 8 * 128], BF16).ap()
    wgb = nc.dram_tensor("wgb4", [32 * 128, 8 * 128], BF16).ap()
    wub = nc.dram_tensor("wub4", [32 * 128, 8 * 128], BF16).ap()
    wdb = nc.dram_tensor("wdb4", [8 * 128, 32 * 128], BF16).ap()

    from contextlib import ExitStack
    with ExitStack() as ctx:
        E_ = ctx.enter_context
        kb = E_(nc.sbuf_tensor("kb4", [64, 2, 2048], BF16))
        vb = E_(nc.sbuf_tensor("vb4", [128, 2, 16 * 65], BF16))
        qb = E_(nc.sbuf_tensor("qb4", [64, 2, 256], BF16))
        mk2 = E_(nc.sbuf_tensor("mk2", [128, 8192], BF16))
        pt = E_(nc.sbuf_tensor("pt4", [128, 2, 256], BF16))
        rd = E_(nc.sbuf_tensor("rd4", [1, 256], F32))
        rb = E_(nc.sbuf_tensor("rb4", [64, 256], F32))
        on1 = E_(nc.sbuf_tensor("on14", [1, 64], F32))
        OT2 = E_(nc.sbuf_tensor("OT2", [128, 8, 512], BF16))
        selb = E_(nc.sbuf_tensor("selb4", [128, 4, D], F32))
        x2b = E_(nc.sbuf_tensor("x2b", [128, 4, D], F32))
        gub = E_(nc.sbuf_tensor("gub4", [128, 32, 512], BF16))
        h2T = E_(nc.sbuf_tensor("h2T4", [128, 8, 512], BF16))
        hb4 = E_(nc.sbuf_tensor("hb4", [128, 2, D], BF16))
        wt = E_(nc.sbuf_tensor("wt4", [128, 2, 8, 128], BF16))
        wtd = E_(nc.sbuf_tensor("wtd4", [128, 2, 32, 128], BF16))
        ob = E_(nc.sbuf_tensor("ob5", [128, 6, 512], BF16))
        scb = E_(nc.sbuf_tensor("scb", [128, 8], F32))
        lnb = E_(nc.sbuf_tensor("lnb4", [128, D], F32))
        cvb = E_(nc.sbuf_tensor("cvb5", [128, 8], F32))
        ident = E_(nc.sbuf_tensor("ident4", [128, 128], BF16))
        stat = E_(nc.sbuf_tensor("stat4", [128, 4, 4], F32))
        scr = E_(nc.sbuf_tensor("scr4", [128, D], BF16))
        tmpb = E_(nc.sbuf_tensor("tmpb", [128, D], F32))
        ps = E_(nc.psum_tensor("ps4", [128, 4, 512], F32))
        psB = E_(nc.psum_tensor("psB4", [128, 512], F32))
        pst = E_(nc.psum_tensor("pst4", [128, 2, 1024], BF16))
        _sd = E_(nc.semaphore("sem_d"))
        _st_ = E_(nc.semaphore("sem_t"))
        _sv = E_(nc.semaphore("sem_v"))
        _sa = E_(nc.semaphore("sem_a"))
        _sg = E_(nc.semaphore("sem_g"))
        _so = E_(nc.semaphore("sem_o"))
        sem_pool = [Sem(E_(nc.semaphore("sp%d" % _i))) for _i in range(24)]
        blk = E_(nc.Block())

        P = Prog(nc, sem_pool)
        sd, st, sv, sa, sg, so = (Sem(_sd), Sem(_st_), Sem(_sv), Sem(_sa),
                                  Sem(_sg), Sem(_so))
        Rs = lambda n: [Res() for _ in range(n)]
        R_kb, R_vb, R_qb = Rs(2), Rs(2), Rs(2)
        R_mk2, R_pt = Res(), Rs(2)
        R_rd, R_rb, R_on1 = Res(), Res(), Res()
        R_OT2 = Res()
        R_selb = Rs(4)
        R_x2 = Rs(4)
        R_gub = Rs(32)
        R_h2T = Res()
        R_hb4 = Rs(2)
        R_wt, R_wtd = Rs(2), Rs(2)
        R_ob = Rs(6)
        R_scb, R_ln, R_cv, R_ident = Res(), Res(), Res(), Res()
        R_stat = Rs(4)
        R_scr, R_tmp = Res(), Res()
        R_ps = Rs(4)
        R_psB = Res()
        R_pst = Rs(2)
        R_cast = Res()

        P.dma("gpsimd", ident[:, :], identc[:, :], sg, writes=[R_ident])
        P.dma("gpsimd", mk2[:, :], mask2c[:, :], sg, writes=[R_mk2])
        for s_, d_ in ((wo2t, wob), (wg2t, wgb), (wu2t, wub), (wd2t, wdb)):
            P.dma("gpsimd", d_[:, :], s_[:, :], sg, writes=[R_cast])
        P.dma("sync", lnb[:, :], ln22.to_broadcast([128, D]), sd, writes=[R_ln])
        P.dma("sync", cvb[:, :], constv.to_broadcast([128, 8]), sd, writes=[R_cv])
        P.dma("sync", scb[:, :], sc_in[:, :], sd, writes=[R_scb])
        for tl in range(4):
            P.dma("sync", selb[:, tl, :], sel_in[128 * tl:128 * (tl + 1), :], sd,
                  writes=[R_selb[tl]])
        P.run("vector", lambda E: E.memset(on1[:, :], 1.0), sv, writes=[R_on1])

        for u in range(2):
            for h in range(H):
                ib = (u * H + h) % 2
                bb_cols = u * KSEL
                P.dma("sync", kb[0:32, ib, :],
                      k2f[32 * h:32 * h + 32, bb_cols:bb_cols + KSEL], sd,
                      writes=[R_kb[ib]])
                P.dma("sync", kb[32:64, ib, :],
                      k2f[512 + 32 * h:512 + 32 * h + 32, bb_cols:bb_cols + KSEL],
                      sd, writes=[R_kb[ib]])
                P.dma("sync", vb[:, ib, :], vswz2[u * H + h, :, :], sd,
                      writes=[R_vb[ib]])
                P.dma("sync", qb[0:32, ib, :],
                      q2T_in[32 * h:32 * h + 32, 256 * u:256 * (u + 1)], sd,
                      writes=[R_qb[ib]])
                P.dma("sync", qb[32:64, ib, :],
                      q2T_in[512 + 32 * h:512 + 32 * h + 32, 256 * u:256 * (u + 1)],
                      sd, writes=[R_qb[ib]])
                for c in range(16):
                    sb = c % 2
                    P.run("tensor", lambda E, ib=ib, sb=sb, c=c: E.matmul(
                        ps[:, sb, 0:256], kb[:, ib, 128 * c:128 * (c + 1)],
                        qb[:, ib, :], start=True, stop=True),
                        st, reads=[R_kb[ib], R_qb[ib]], writes=[R_ps[sb]])
                    P.run("scalar", lambda E, sb=sb: E.activation(
                        pt[:, sb, :], ps[:, sb, 0:256], AF.Exp, scale=0.125,
                        bias=cvb[:, 0:1]),
                        sa, reads=[R_ps[sb], R_cv], writes=[R_pt[sb]])
                    P.run("vector", lambda E, sb=sb, u=u, c=c: E.tensor_tensor(
                        pt[:, sb, :], pt[:, sb, :],
                        mk2[:, (u * 16 + c) * 256:(u * 16 + c + 1) * 256],
                        op=ALU.mult),
                        sv, reads=[R_mk2, R_pt[sb]], writes=[R_pt[sb]])
                    P.run("tensor", lambda E, ib=ib, sb=sb, c=c: E.matmul(
                        ps[0:65, 2, 0:256], vb[:, ib, 65 * c:65 * (c + 1)],
                        pt[:, sb, :], start=(c == 0), stop=(c == 15)),
                        st, reads=[R_vb[ib], R_pt[sb]], writes=[R_ps[2]])
                P.run("vector", lambda E: E.reciprocal(
                    rd[:, :], ps[64:65, 2, 0:256]),
                    sv, reads=[R_ps[2]], writes=[R_rd])
                P.run("tensor", lambda E: E.matmul(
                    psB[0:64, 0:256], on1[:, :], rd[:, :], start=True, stop=True),
                    st, reads=[R_rd, R_on1], writes=[R_psB])
                P.run("vector", lambda E: E.tensor_copy(rb[:, :], psB[0:64, 0:256]),
                      sv, reads=[R_psB], writes=[R_rb])
                P.run("vector", lambda E, h=h, u=u: E.tensor_tensor(
                    OT2[64 * (h % 2):64 * (h % 2) + 64, h // 2, 256 * u:256 * (u + 1)],
                    ps[0:64, 2, 0:256], rb[:, :], op=ALU.mult),
                    sv, reads=[R_ps[2], R_rb], writes=[R_OT2])

        def mmchain8(wres, buf, bank, rhs_tiles, rhs_res, coloff):
            P.rd("tensor", [wres, rhs_res])
            P.wr("tensor", [R_ps[bank]])
            for c in range(8):
                f = (lambda E, c=c, buf=buf, bank=bank, coloff=coloff:
                     E.matmul(ps[:, bank, :], wt[:, buf, c, :],
                              rhs_tiles[:, c, coloff:coloff + 512],
                              start=(c == 0), stop=(c == 7)))
                if c == 7:
                    st.n += 1
                    P.ops["tensor"].append(lambda E, f=f: f(E).then_inc(_st_, 1))
                    P.done_write([R_ps[bank]], st, st.n)
                    P.done_read([wres, rhs_res], st, st.n)
                else:
                    P.ops["tensor"].append(lambda E, f=f: f(E))

        def loadw8(w_dram, oc, buf):
            P.dma("sync", wt[:, buf, :, :],
                  w_dram[128 * oc:128 * (oc + 1), :].rearrange("p (c j) -> p c j", c=8),
                  sd, writes=[R_wt[buf]])
            return R_wt[buf]

        tpb = 0
        for oc in range(8):
            wres = loadw8(wob, oc, oc % 2)
            mmchain8(wres, oc % 2, oc % 2, OT2, R_OT2, 0)
            slot = 4 + (oc % 2)
            P.run("scalar", lambda E, pb=oc % 2, slot=slot: E.activation(
                ob[:, slot, :], ps[:, pb, :], AF.Identity),
                sa, reads=[R_ps[oc % 2]], writes=[R_ob[slot]])
            for sub in range(4):
                pb2 = tpb % 2
                P.run("tensor", lambda E, slot=slot, sub=sub, pb2=pb2: E.transpose(
                    pst[:, pb2, 0:128], ob[:, slot, 128 * sub:128 * (sub + 1)],
                    ident[:, :]),
                    st, reads=[R_ob[slot], R_ident], writes=[R_pst[pb2]])
                P.run("vector", lambda E, sub=sub, oc=oc, pb2=pb2: E.tensor_tensor(
                    x2b[:, sub, 128 * oc:128 * (oc + 1)],
                    selb[:, sub, 128 * oc:128 * (oc + 1)],
                    pst[:, pb2, 0:128], op=ALU.add),
                    sv, reads=[R_pst[pb2], R_selb[sub]], writes=[R_x2[sub]])
                tpb += 1

        for sub in range(4):
            P.run("scalar", lambda E, sub=sub: E.activation(
                scr[:, :], x2b[:, sub, :], AF.Square, accum_out=stat[:, sub, 0:1]),
                sa, reads=[R_x2[sub]], writes=[R_stat[sub], R_scr])
            P.run("scalar", lambda E, sub=sub: E.activation(
                stat[:, sub, 0:1], stat[:, sub, 0:1], AF.Ln, bias=cvb[:, 1:2],
                scale=float(1.0 / D)),
                sa, reads=[R_stat[sub], R_cv], writes=[R_stat[sub]])
            P.run("scalar", lambda E, sub=sub: E.activation(
                stat[:, sub, 0:1], stat[:, sub, 0:1], AF.Exp, scale=-0.5),
                sa, reads=[R_stat[sub]], writes=[R_stat[sub]])
            hsl = sub % 2
            P.run("vector", lambda E, sub=sub, hsl=hsl: E.scalar_tensor_tensor(
                hb4[:, hsl, :], x2b[:, sub, :], stat[:, sub, 0:1], lnb[:, :],
                op0=ALU.mult, op1=ALU.mult),
                sv, reads=[R_x2[sub], R_stat[sub], R_ln], writes=[R_hb4[hsl]])
            for c in range(8):
                pb2 = tpb % 2
                P.run("tensor", lambda E, hsl=hsl, c=c, pb2=pb2: E.transpose(
                    pst[:, pb2, 0:128], hb4[:, hsl, 128 * c:128 * (c + 1)],
                    ident[:, :]),
                    st, reads=[R_hb4[hsl], R_ident], writes=[R_pst[pb2]])
                P.run("vector", lambda E, sub=sub, c=c, pb2=pb2: E.tensor_copy(
                    h2T[:, c, 128 * sub:128 * (sub + 1)], pst[:, pb2, 0:128]),
                    sv, reads=[R_pst[pb2]], writes=[R_h2T])
                tpb += 1

        for nch in range(32):
            wb = nch % 2
            P.dma("sync", wt[:, wb, :, :],
                  wgb[128 * nch:128 * (nch + 1), :].rearrange("p (c j) -> p c j", c=8),
                  sd, writes=[R_wt[wb]])
            mmchain8(R_wt[wb], wb, 2, h2T, R_h2T, 0)
            P.dma("gpsimd", wtd[:, wb, 0:8, :],
                  wub[128 * nch:128 * (nch + 1), :].rearrange("p (c j) -> p c j", c=8),
                  sg, writes=[R_wtd[wb]])
            P.rd("tensor", [R_wtd[wb], R_h2T])
            P.wr("tensor", [R_ps[3]])
            for c in range(8):
                f = (lambda E, c=c, wb=wb:
                     E.matmul(ps[:, 3, :], wtd[:, wb, c, :], h2T[:, c, :],
                              start=(c == 0), stop=(c == 7)))
                if c == 7:
                    st.n += 1
                    P.ops["tensor"].append(lambda E, f=f: f(E).then_inc(_st_, 1))
                    P.done_write([R_ps[3]], st, st.n)
                    P.done_read([R_wtd[wb], R_h2T], st, st.n)
                else:
                    P.ops["tensor"].append(lambda E, f=f: f(E))
            P.run("scalar", lambda E: E.activation(
                ob[:, 0, :], ps[:, 2, :], AF.Silu),
                sa, reads=[R_ps[2]], writes=[R_ob[0]])
            P.run("vector", lambda E, nch=nch: E.tensor_tensor(
                gub[:, nch, :], ob[:, 0, :], ps[:, 3, :], op=ALU.mult),
                sv, reads=[R_ob[0], R_ps[3]], writes=[R_gub[nch]])
        for oc in range(8):
            wb = oc % 2
            P.dma("sync", wtd[:, wb, :, :],
                  wdb[128 * oc:128 * (oc + 1), :].rearrange("p (n j) -> p n j", n=32),
                  sd, writes=[R_wtd[wb]])
            pb = oc % 2
            P.rd("tensor", [R_wtd[wb]] + R_gub)
            P.wr("tensor", [R_ps[pb]])
            for n in range(32):
                f = (lambda E, n=n, wb=wb, pb=pb:
                     E.matmul(ps[:, pb, :], wtd[:, wb, n, :], gub[:, n, :],
                              start=(n == 0), stop=(n == 31)))
                if n == 31:
                    st.n += 1
                    P.ops["tensor"].append(lambda E, f=f: f(E).then_inc(_st_, 1))
                    P.done_write([R_ps[pb]], st, st.n)
                    P.done_read([R_wtd[wb]] + R_gub, st, st.n)
                else:
                    P.ops["tensor"].append(lambda E, f=f: f(E))
            slot = 4 + (oc % 2)
            P.run("scalar", lambda E, pb=pb, slot=slot: E.activation(
                ob[:, slot, :], ps[:, pb, :], AF.Identity),
                sa, reads=[R_ps[pb]], writes=[R_ob[slot]])
            for sub in range(4):
                pb2 = tpb % 2
                P.run("tensor", lambda E, slot=slot, sub=sub, pb2=pb2: E.transpose(
                    pst[:, pb2, 0:128], ob[:, slot, 128 * sub:128 * (sub + 1)],
                    ident[:, :]),
                    st, reads=[R_ob[slot], R_ident], writes=[R_pst[pb2]])
                P.run("vector", lambda E, sub=sub, oc=oc, pb2=pb2: E.tensor_tensor(
                    x2b[:, sub, 128 * oc:128 * (oc + 1)],
                    x2b[:, sub, 128 * oc:128 * (oc + 1)],
                    pst[:, pb2, 0:128], op=ALU.add),
                    sv, reads=[R_pst[pb2]], writes=[R_x2[sub]])
                tpb += 1
        for sub in range(4):
            P.run("vector", lambda E, sub=sub: E.tensor_tensor(
                tmpb[:, :], x2b[:, sub, :], selb[:, sub, :], op=ALU.subtract),
                sv, reads=[R_x2[sub], R_selb[sub]], writes=[R_tmp])
            P.run("vector", lambda E, sub=sub: E.scalar_tensor_tensor(
                x2b[:, sub, :], tmpb[:, :], scb[:, 4 + sub:5 + sub], selb[:, sub, :],
                op0=ALU.mult, op1=ALU.add),
                sv, reads=[R_tmp, R_scb, R_selb[sub]], writes=[R_x2[sub]])
            P.dma("gpsimd", upd_out[128 * sub:128 * (sub + 1), :], x2b[:, sub, :],
                  so, reads=[R_x2[sub]])
        P.finish("gpsimd")
        P.emit(blk)
    return nc


def host_prep_L4(inputs, l3_results):
    k2f = np.zeros((D, B * KSEL), BF16NP)
    v2full = np.zeros((B * KSEL, D), BF16NP)
    for p in range(NC):
        r = l3_results[p]
        bases = [(0, 256 * p), (1, 256 * (7 - p))]
        for ui, (b, base) in enumerate(bases):
            cols = slice(256 * ui, 256 * (ui + 1))
            k2f[:, b * KSEL + base:b * KSEL + base + 256] = np.asarray(r["k2T"])[:, cols]
            v2full[b * KSEL + base:b * KSEL + base + 256, :] = np.asarray(r["v2T"])[:, cols].T
    vswz2 = np.ones((B * H, 128, 16 * 65), BF16NP)
    for b in range(B):
        for h in range(H):
            vv = v2full[b * KSEL:(b + 1) * KSEL, 64 * h:64 * h + 64].reshape(16, 128, 64)
            vs = vswz2[b * H + h].reshape(128, 16, 65)
            vs[:, :, 0:64] = vv.transpose(1, 0, 2)
    wo2t = _tile_w(inputs["L0_ffn_wo2"], 8, 8)
    wg2t = _tile_w(inputs["L0_ffn_wg2"], 8, 32)
    wu2t = _tile_w(inputs["L0_ffn_wu2"], 8, 32)
    wd2t = _tile_w(inputs["L0_ffn_wd2"], 32, 8)
    ln22 = _f32(inputs["L0_ffn_ln2_2"]).reshape(1, D)
    ident = np.eye(128, dtype=np.float32)
    consts = np.array([[-EXPB, 1e-6, 0, 0, 0, 0, 0, 0]], np.float32)
    in_maps = []
    kk = np.arange(128)
    qq = np.arange(256)
    for p in range(NC):
        ubase = [256 * p, 256 * (7 - p)]
        m = np.zeros((2, 16, 128, 256), np.float32)
        for u in range(2):
            for c in range(16):
                m[u, c] = (128 * c + kk[:, None] <= ubase[u] + qq[None, :])
        mask2c = np.ascontiguousarray(
            m.transpose(2, 0, 1, 3).reshape(128, 2 * 16 * 256)).astype(np.float32)
        r = l3_results[p]
        in_maps.append(dict(q2T=np.asarray(r["q2T"]), k2f=k2f, vswz2=vswz2,
                            seli=np.asarray(r["selo"]), sci=np.asarray(r["sco"]),
                            mask2c=mask2c, wo2t=wo2t, wg2t=wg2t, wu2t=wu2t,
                            wd2t=wd2t, ln22=ln22, identc=ident, constv=consts))
    return in_maps


def run_L4(inputs, l3_results):
    nc = _get_prog("L4", build_L4)
    in_maps = host_prep_L4(inputs, l3_results)
    res = run_bass_kernel_spmd(nc, in_maps, core_ids=list(range(NC)))
    return res.results


CCH = 128


def build_LcA():
    nc = bass.Bass("TRN2", target_bir_lowering=False)
    vch = nc.dram_tensor("vch", [B, CCH, N], BF16, kind="ExternalInput").ap()
    rpe_in = nc.dram_tensor("rpein", [128, 1], F32, kind="ExternalInput").ap()
    pos4c = nc.dram_tensor("pos4c", [128, N // 4], F32, kind="ExternalInput").ap()
    w4l = nc.dram_tensor("w4l", [3 * 128, 128], F32, kind="ExternalInput").ap()
    wog = nc.dram_tensor("wog", [4 * 128, 128], F32, kind="ExternalInput").ap()
    gb4 = nc.dram_tensor("gb4", [3 * 128, 2], F32, kind="ExternalInput").ap()
    gam = nc.dram_tensor("gam", [1, CCH], F32, kind="ExternalInput").ap()
    posr = nc.dram_tensor("posr", [1, N], F32, kind="ExternalInput").ap()
    blk1 = nc.dram_tensor("blk1", [128, 4], F32, kind="ExternalInput").ap()
    blk2 = nc.dram_tensor("blk2", [4, 128], F32, kind="ExternalInput").ap()
    dfc = nc.dram_tensor("dfc", [64, 256], F32, kind="ExternalInput").ap()
    constv = nc.dram_tensor("constv", [1, 8], F32, kind="ExternalInput").ap()

    ya_out = nc.dram_tensor("ya", [3, 2, 128, CCH * 128], BF16, kind="ExternalOutput").ap()
    tdr = nc.dram_tensor("tdr", [CCH, N], BF16).ap()

    from contextlib import ExitStack
    with ExitStack() as ctx:
        E_ = ctx.enter_context
        h4 = E_(nc.sbuf_tensor("h4", [128, N // 4], F32))
        hn4 = E_(nc.sbuf_tensor("hn4", [128, N // 4], F32))
        gs4 = E_(nc.sbuf_tensor("gs4", [4, 8, 512], F32))
        gsb = E_(nc.sbuf_tensor("gsb", [128, 2, N // 4], F32))
        w4b = E_(nc.sbuf_tensor("w4b", [128, 8, 128], BF16))
        gbb = E_(nc.sbuf_tensor("gbb", [128, 3, 2], F32))
        rpeb = E_(nc.sbuf_tensor("rpeb", [128, 1], F32))
        pos4b = E_(nc.sbuf_tensor("pos4b", [128, N // 4], F32))
        posb = E_(nc.sbuf_tensor("posb", [1, N], F32))
        blk1b = E_(nc.sbuf_tensor("blk1b", [128, 4], F32))
        blk2b = E_(nc.sbuf_tensor("blk2b", [4, 128], F32))
        gamb = E_(nc.sbuf_tensor("gamb", [128, 2], F32))
        iob = E_(nc.sbuf_tensor("iob", [128, N], F32))
        ttb = E_(nc.sbuf_tensor("ttb", [128, N // 4], BF16))
        dfb = E_(nc.sbuf_tensor("dfb", [64, 256], BF16))
        vt = E_(nc.sbuf_tensor("vt", [64, 2, 512], BF16))
        oeb = E_(nc.sbuf_tensor("oeb", [128, 4, 512], BF16))
        cvb = E_(nc.sbuf_tensor("cvb6", [128, 8], F32))
        ps = E_(nc.psum_tensor("psA", [128, 4, 512], F32))
        _sd = E_(nc.semaphore("sem_d"))
        _st_ = E_(nc.semaphore("sem_t"))
        _sv = E_(nc.semaphore("sem_v"))
        _sa = E_(nc.semaphore("sem_a"))
        _sg = E_(nc.semaphore("sem_g"))
        _so = E_(nc.semaphore("sem_o"))
        sem_pool = [Sem(E_(nc.semaphore("sp%d" % _i))) for _i in range(24)]
        blk = E_(nc.Block())
        P = Prog(nc, sem_pool)
        sd, st, sv, sa, sg, so = (Sem(_sd), Sem(_st_), Sem(_sv), Sem(_sa),
                                  Sem(_sg), Sem(_so))
        Rs = lambda n: [Res() for _ in range(n)]
        R_h4, R_hn4, R_gs4, R_gsb = Res(), Res(), Res(), Res()
        R_const, R_ttb, R_io, R_gam = Res(), Res(), Res(), Res()
        R_tdr = Res()
        R_vt = Rs(2)
        R_oeb = Rs(4)
        R_ps = Rs(4)

        P.dma("gpsimd", w4b[:, 0:3, :], w4l.rearrange("(l p) j -> p l j", p=128),
              sg, writes=[R_const])
        P.dma("gpsimd", w4b[:, 4:8, :], wog.rearrange("(g p) j -> p g j", p=128),
              sg, writes=[R_const])
        P.dma("gpsimd", dfb[:, :], dfc[:, :], sg, writes=[R_const])
        P.dma("gpsimd", gbb[:, :, :],
              gb4.rearrange("(l p) c -> p l c", p=128), sg, writes=[R_const])
        P.dma("sync", rpeb[:, :], rpe_in[:, :], sd, writes=[R_const])
        P.dma("sync", pos4b[:, :], pos4c[:, :], sd, writes=[R_const])
        P.dma("sync", posb[:, :], posr[:, :], sd, writes=[R_const])
        P.dma("sync", blk1b[:, :], blk1[:, :], sd, writes=[R_const])
        P.dma("sync", blk2b[:, :], blk2[:, :], sd, writes=[R_const])
        P.dma("sync", gamb[:, 0:1], gam.rearrange("a c -> c a"), sd, writes=[R_gam])
        P.dma("sync", cvb[:, :], constv.to_broadcast([128, 8]), sd, writes=[R_const])

        P.run("vector", lambda E: E.tensor_scalar(
            h4[:, :], pos4b[:, :], rpeb[:, 0:1], None, op0=ALU.mult),
            sv, reads=[R_const], writes=[R_h4])
        P.run("scalar", lambda E: E.activation(h4[:, :], h4[:, :], AF.Silu),
              sa, reads=[R_h4], writes=[R_h4])
        for li in range(3):
            P.run("scalar", lambda E: E.activation(
                hn4[:, :], h4[:, :], AF.Square),
                sa, reads=[R_h4], writes=[R_hn4])
            for ch in range(4):
                P.run("tensor", lambda E, ch=ch: E.matmul(
                    ps[0:4, 0, :], blk1b[:, :],
                    h4[:, 512 * ch:512 * (ch + 1)], start=True, stop=True),
                    st, reads=[R_h4, R_const], writes=[R_ps[0]])
                P.run("vector", lambda E, ch=ch: E.tensor_copy(
                    gs4[0:4, 2 * ch, :], ps[0:4, 0, :]),
                    sv, reads=[R_ps[0]], writes=[R_gs4])
                P.run("tensor", lambda E, ch=ch: E.matmul(
                    ps[0:4, 1, :], blk1b[:, :], hn4[:, 512 * ch:512 * (ch + 1)],
                    start=True, stop=True),
                    st, reads=[R_hn4, R_const], writes=[R_ps[1]])
                P.run("vector", lambda E, ch=ch: E.tensor_copy(
                    gs4[0:4, 1 + 2 * ch, :], ps[0:4, 1, :]),
                    sv, reads=[R_ps[1]], writes=[R_gs4])
            for ch in range(4):
                P.run("vector", lambda E, ch=ch: E.tensor_scalar(
                    gs4[0:4, 2 * ch, :], gs4[0:4, 2 * ch, :], float(1.0 / 32), None,
                    op0=ALU.mult),
                    sv, reads=[R_gs4], writes=[R_gs4])
                P.run("vector", lambda E, ch=ch: E.tensor_scalar(
                    gs4[0:4, 1 + 2 * ch, :], gs4[0:4, 1 + 2 * ch, :],
                    float(1.0 / 32), None, op0=ALU.mult),
                    sv, reads=[R_gs4], writes=[R_gs4])
                P.run("vector", lambda E, ch=ch: E.tensor_tensor(
                    gsb[0:4, 0, 512 * ch:512 * (ch + 1)], gs4[0:4, 2 * ch, :],
                    gs4[0:4, 2 * ch, :], op=ALU.mult),
                    sv, reads=[R_gs4], writes=[R_gsb])
                P.run("vector", lambda E, ch=ch: E.tensor_tensor(
                    gsb[0:4, 0, 512 * ch:512 * (ch + 1)],
                    gs4[0:4, 1 + 2 * ch, :],
                    gsb[0:4, 0, 512 * ch:512 * (ch + 1)], op=ALU.subtract),
                    sv, reads=[R_gs4, R_gsb], writes=[R_gsb])
            P.run("scalar", lambda E: E.activation(
                gsb[0:4, 0, :], gsb[0:4, 0, :], AF.Ln, bias=cvb[0:4, 0:1]),
                sa, reads=[R_gsb, R_const], writes=[R_gsb])
            P.run("scalar", lambda E: E.activation(
                gsb[0:4, 0, :], gsb[0:4, 0, :], AF.Exp, scale=-0.5),
                sa, reads=[R_gsb], writes=[R_gsb])
            for ch in range(4):
                P.run("tensor", lambda E, ch=ch: E.matmul(
                    ps[:, 2, :], blk2b[:, :], gs4[0:4, 2 * ch, :],
                    start=True, stop=True),
                    st, reads=[R_gs4, R_const], writes=[R_ps[2]])
                P.run("vector", lambda E, ch=ch: E.tensor_copy(
                    gsb[:, 1, 512 * ch:512 * (ch + 1)], ps[:, 2, :]),
                    sv, reads=[R_ps[2]], writes=[R_gsb])
                P.run("tensor", lambda E, ch=ch: E.matmul(
                    ps[:, 3, :], blk2b[:, :], gsb[0:4, 0, 512 * ch:512 * (ch + 1)],
                    start=True, stop=True),
                    st, reads=[R_gsb, R_const], writes=[R_ps[3]])
                P.run("vector", lambda E, ch=ch: E.tensor_tensor(
                    hn4[:, 512 * ch:512 * (ch + 1)], h4[:, 512 * ch:512 * (ch + 1)],
                    gsb[:, 1, 512 * ch:512 * (ch + 1)], op=ALU.subtract),
                    sv, reads=[R_h4, R_gsb], writes=[R_hn4])
                P.run("vector", lambda E, ch=ch: E.tensor_tensor(
                    hn4[:, 512 * ch:512 * (ch + 1)], hn4[:, 512 * ch:512 * (ch + 1)],
                    ps[:, 3, :], op=ALU.mult),
                    sv, reads=[R_hn4, R_ps[3]], writes=[R_hn4])
            P.run("vector", lambda E, li=li: E.tensor_scalar(
                hn4[:, :], hn4[:, :], gbb[:, li, 0:1], gbb[:, li, 1:2],
                op0=ALU.mult, op1=ALU.add),
                sv, reads=[R_hn4, R_const], writes=[R_hn4])
            P.run("vector", lambda E: E.tensor_copy(ttb[:, :], hn4[:, :]),
                  sv, reads=[R_hn4], writes=[R_ttb])
            for ch in range(4):
                P.run("tensor", lambda E, li=li, ch=ch: E.matmul(
                    ps[:, 0, :], w4b[:, li, :], ttb[:, 512 * ch:512 * (ch + 1)],
                    start=True, stop=True),
                    st, reads=[R_ttb, R_const], writes=[R_ps[0]])
                P.run("scalar", lambda E, ch=ch: E.activation(
                    h4[:, 512 * ch:512 * (ch + 1)], ps[:, 0, :], AF.Silu),
                    sa, reads=[R_ps[0]], writes=[R_h4])

        P.run("gpsimd", lambda E: E.iota(iob[:, :], [[1, N]], base=0,
                                         channel_multiplier=0,
                                         allow_small_or_imprecise_dtypes=True),
              sg, k=1, writes=[R_io])
        P.run("scalar", lambda E: E.activation(
            gamb[:, 1:2], gamb[:, 0:1], AF.Sigmoid),
            sa, reads=[R_gam], writes=[R_gam])
        P.run("vector", lambda E: E.tensor_scalar(
            gamb[:, 1:2], gamb[:, 1:2], 0.01, 0.99, op0=ALU.mult, op1=ALU.add),
            sv, reads=[R_gam], writes=[R_gam])
        P.run("scalar", lambda E: E.activation(
            gamb[:, 1:2], gamb[:, 1:2], AF.Ln),
            sa, reads=[R_gam], writes=[R_gam])
        P.run("vector", lambda E: E.tensor_scalar(
            iob[:, :], iob[:, :], gamb[:, 1:2], None, op0=ALU.mult),
            sv, reads=[R_io, R_gam], writes=[R_io])
        P.run("scalar", lambda E: E.activation(
            iob[:, :], iob[:, :], AF.Exp),
            sa, reads=[R_io], writes=[R_io])
        P.run("vector", lambda E: E.tensor_copy(ttb[:, :], h4[:, :]),
              sv, reads=[R_h4], writes=[R_ttb])
        for g in range(4):
            for ch in range(4):
                P.run("tensor", lambda E, g=g, ch=ch: E.matmul(
                    ps[:, 1, :], w4b[:, 4 + g, :], ttb[:, 512 * ch:512 * (ch + 1)],
                    start=True, stop=True),
                    st, reads=[R_ttb, R_const], writes=[R_ps[1]])
                P.run("vector", lambda E, g=g, ch=ch: E.tensor_tensor(
                    oeb[:, 0, :], ps[:, 1, :],
                    iob[:, 2048 * g + 512 * ch:2048 * g + 512 * (ch + 1)],
                    op=ALU.mult),
                    sv, reads=[R_ps[1], R_io], writes=[R_oeb[0]])
                P.dma("gpsimd",
                      tdr[:, 2048 * g + 512 * ch:2048 * g + 512 * (ch + 1)],
                      oeb[:, 0, :], so, reads=[R_oeb[0]], writes=[R_tdr])

        def stageA(sig_idx, src_ap, src_res):
            for chunk in range(32):
                ib = chunk % 2
                c0 = chunk * 4
                P.dma("sync", vt[:, ib, :].rearrange("p (c j) -> p c j", c=4),
                      src_ap[c0:c0 + 4, :].rearrange("c (j1 j2) -> j1 c j2", j1=64),
                      sd, reads=[src_res] if src_res else (),
                      writes=[R_vt[ib]])
                P.run("tensor", lambda E, ib=ib: E.matmul(
                    ps[:, 2, :], dfb[:, 0:128], vt[:, ib, :], start=True, stop=True),
                    st, reads=[R_vt[ib], R_const], writes=[R_ps[2]])
                P.run("tensor", lambda E, ib=ib: E.matmul(
                    ps[:, 3, :], dfb[:, 128:256], vt[:, ib, :], start=True, stop=True),
                    st, reads=[R_vt[ib], R_const], writes=[R_ps[3]])
                ob0 = (2 * chunk) % 4
                ob1 = (2 * chunk + 1) % 4
                P.run("vector", lambda E, ob0=ob0: E.tensor_copy(
                    oeb[:, ob0, :], ps[:, 2, :]),
                    sv, reads=[R_ps[2]], writes=[R_oeb[ob0]])
                P.dma("gpsimd", ya_out[sig_idx, 0, :, 512 * chunk:512 * (chunk + 1)],
                      oeb[:, ob0, :], so, reads=[R_oeb[ob0]])
                P.run("vector", lambda E, ob1=ob1: E.tensor_copy(
                    oeb[:, ob1, :], ps[:, 3, :]),
                    sv, reads=[R_ps[3]], writes=[R_oeb[ob1]])
                P.dma("gpsimd", ya_out[sig_idx, 1, :, 512 * chunk:512 * (chunk + 1)],
                      oeb[:, ob1, :], so, reads=[R_oeb[ob1]])

        stageA(0, tdr, R_tdr)
        stageA(1, vch[0], None)
        stageA(2, vch[1], None)
        P.finish("gpsimd")
        P.emit(blk)
    return nc


def build_LcB():
    nc = bass.Bass("TRN2", target_bir_lowering=False)
    yb = nc.dram_tensor("yb", [3, 2, 128, 128 * CCH], BF16, kind="ExternalInput").ap()
    mbc = nc.dram_tensor("mbc", [128, 3 * 128 * 128], BF16, kind="ExternalInput").ap()
    p_out = nc.dram_tensor("pp", [B, 2, 128, 128 * CCH], BF16, kind="ExternalOutput").ap()
    xdr = nc.dram_tensor("xdr", [3, 2, 128, 128 * CCH], BF16).ap()

    from contextlib import ExitStack
    with ExitStack() as ctx:
        E_ = ctx.enter_context
        yt = E_(nc.sbuf_tensor("yt", [128, 2, 2, 512], BF16))
        mb = E_(nc.sbuf_tensor("mb", [128, 2, 3, 128], BF16))
        xe = E_(nc.sbuf_tensor("xe", [128, 4, 512], BF16))
        pv = E_(nc.sbuf_tensor("pv", [128, 4, 512], BF16))
        pt_ = E_(nc.sbuf_tensor("pt_", [128, 4, 512], BF16))
        po = E_(nc.sbuf_tensor("po", [128, 4, 512], BF16))
        ps = E_(nc.psum_tensor("psB", [128, 6, 512], F32))
        _sd = E_(nc.semaphore("sem_d"))
        _st_ = E_(nc.semaphore("sem_t"))
        _sv = E_(nc.semaphore("sem_v"))
        _sg = E_(nc.semaphore("sem_g"))
        _so = E_(nc.semaphore("sem_o"))
        sem_pool = [Sem(E_(nc.semaphore("sp%d" % _i))) for _i in range(24)]
        blk = E_(nc.Block())
        P = Prog(nc, sem_pool)
        sd, st, sv, sg, so = Sem(_sd), Sem(_st_), Sem(_sv), Sem(_sg), Sem(_so)
        Rs = lambda n: [Res() for _ in range(n)]
        R_yt = Rs(2)
        R_mb = Rs(2)
        R_xe = Rs(4)
        R_pv, R_pt, R_po = Rs(4), Rs(4), Rs(4)
        R_ps = Rs(6)
        R_xdr = Res()

        for k1 in range(128):
            wb = k1 % 2
            P.dma("sync", mb[:, wb, :, :],
                  mbc[:, k1 * 384:(k1 + 1) * 384].rearrange("p (m j) -> p m j", m=3),
                  sd, writes=[R_mb[wb]])
            for s in range(3):
                ib = (k1 * 3 + s) % 2
                P.dma("sync", yt[:, ib, 0, 0:128], yb[s, 0, k1, :].rearrange("(j c) -> j c", j=128),
                      sd, writes=[R_yt[ib]])
                P.dma("sync", yt[:, ib, 1, 0:128], yb[s, 1, k1, :].rearrange("(j c) -> j c", j=128),
                      sd, writes=[R_yt[ib]])
                qk = k1 % 4
                br = 2 * s
                P.rd("tensor", [R_mb[wb], R_yt[ib]])
                P.wr("tensor", [R_ps[br]])
                P.wr("tensor", [R_ps[br + 1]])
                for (bank, lhs_i, pl) in ((br, 0, 0), (br, 2, 1), (br + 1, 1, 0), (br + 1, 0, 1)):
                    f = (lambda E, bank=bank, lhs_i=lhs_i, pl=pl, wb=wb, ib=ib, qk=qk:
                         E.matmul(ps[:, bank, 128 * qk:128 * (qk + 1)],
                                  mb[:, wb, lhs_i, :], yt[:, ib, pl, 0:128],
                                  start=(pl == 0), stop=(pl == 1)))
                    st.n += 1
                    P.ops["tensor"].append(lambda E, f=f: f(E).then_inc(_st_, 1))
                    P.done_write([R_ps[bank]], st, st.n)
                    P.done_read([R_mb[wb], R_yt[ib]], st, st.n)
                if qk == 3:
                    for pl, bank in ((0, br), (1, br + 1)):
                        sl = (k1 // 4 + pl) % 4
                        P.run("vector", lambda E, bank=bank, sl=sl: E.tensor_copy(
                            xe[:, sl, :], ps[:, bank, :]),
                            sv, reads=[R_ps[bank]], writes=[R_xe[sl]])
                        P.dma("gpsimd",
                              xdr[s, pl, :, 512 * (k1 // 4):512 * (k1 // 4 + 1)],
                              xe[:, sl, :], so, reads=[R_xe[sl]], writes=[R_xdr])
        for b in range(B):
            for ch in range(32):
                ib = ch % 2
                cs = slice(512 * ch, 512 * (ch + 1))
                P.dma("sync", pv[:, 2 * ib, :], xdr[1 + b, 0, :, cs], sd,
                      reads=[R_xdr], writes=[R_pv[2 * ib]])
                P.dma("sync", pv[:, 2 * ib + 1, :], xdr[1 + b, 1, :, cs], sd,
                      reads=[R_xdr], writes=[R_pv[2 * ib + 1]])
                P.dma("sync", pt_[:, 2 * ib, :], xdr[0, 0, :, cs], sd,
                      reads=[R_xdr], writes=[R_pt[2 * ib]])
                P.dma("sync", pt_[:, 2 * ib + 1, :], xdr[0, 1, :, cs], sd,
                      reads=[R_xdr], writes=[R_pt[2 * ib + 1]])
                o0, o1, o2, o3 = 2 * ib, 2 * ib + 1, 2 * (1 - ib), 2 * (1 - ib) + 1
                P.run("vector", lambda E, o0=o0, ib=ib: E.tensor_tensor(
                    po[:, o0, :], pv[:, 2 * ib, :], pt_[:, 2 * ib, :], op=ALU.mult),
                    sv, reads=[R_pv[2 * ib], R_pt[2 * ib]], writes=[R_po[o0]])
                P.run("vector", lambda E, o1=o1, ib=ib: E.tensor_tensor(
                    po[:, o1, :], pv[:, 2 * ib + 1, :], pt_[:, 2 * ib + 1, :],
                    op=ALU.mult),
                    sv, reads=[R_pv[2 * ib + 1], R_pt[2 * ib + 1]], writes=[R_po[o1]])
                P.run("vector", lambda E, o0=o0, o1=o1: E.tensor_tensor(
                    po[:, o0, :], po[:, o0, :], po[:, o1, :], op=ALU.subtract),
                    sv, reads=[R_po[o1]], writes=[R_po[o0]])
                P.dma("gpsimd", p_out[b, 0, :, cs], po[:, o0, :], so,
                      reads=[R_po[o0]])
                P.run("vector", lambda E, o1=o1, ib=ib: E.tensor_tensor(
                    po[:, o1, :], pv[:, 2 * ib, :], pt_[:, 2 * ib + 1, :],
                    op=ALU.mult),
                    sv, reads=[R_pv[2 * ib], R_pt[2 * ib + 1]], writes=[R_po[o1]])
                P.run("vector", lambda E, o1=o1, o3=o3, ib=ib: E.tensor_tensor(
                    po[:, o3, :], pv[:, 2 * ib + 1, :], pt_[:, 2 * ib, :],
                    op=ALU.mult),
                    sv, reads=[R_pv[2 * ib + 1], R_pt[2 * ib]], writes=[R_po[o3]])
                P.run("vector", lambda E, o1=o1, o3=o3: E.tensor_tensor(
                    po[:, o1, :], po[:, o1, :], po[:, o3, :], op=ALU.add),
                    sv, reads=[R_po[o3]], writes=[R_po[o1]])
                P.dma("gpsimd", p_out[b, 1, :, cs], po[:, o1, :], so,
                      reads=[R_po[o1]])
        P.finish("gpsimd")
        P.emit(blk)
    return nc


def build_LcC():
    nc = bass.Bass("TRN2", target_bir_lowering=False)
    pc = nc.dram_tensor("pc", [B, 2, 128, 128 * CCH], BF16, kind="ExternalInput").ap()
    mcc = nc.dram_tensor("mcc", [128, 3 * 128 * 128], BF16, kind="ExternalInput").ap()
    z_out = nc.dram_tensor("zz", [B, 2, 128, 128 * CCH], BF16, kind="ExternalOutput").ap()

    from contextlib import ExitStack
    with ExitStack() as ctx:
        E_ = ctx.enter_context
        yt = E_(nc.sbuf_tensor("ytC", [128, 2, 2, 128], BF16))
        mb = E_(nc.sbuf_tensor("mbC", [128, 2, 3, 128], BF16))
        xe = E_(nc.sbuf_tensor("xeC", [128, 4, 512], BF16))
        ps = E_(nc.psum_tensor("psC", [128, 4, 512], F32))
        _sd = E_(nc.semaphore("sem_d"))
        _st_ = E_(nc.semaphore("sem_t"))
        _sv = E_(nc.semaphore("sem_v"))
        _sg = E_(nc.semaphore("sem_g"))
        _so = E_(nc.semaphore("sem_o"))
        sem_pool = [Sem(E_(nc.semaphore("sp%d" % _i))) for _i in range(24)]
        blk = E_(nc.Block())
        P = Prog(nc, sem_pool)
        sd, st, sv, sg, so = Sem(_sd), Sem(_st_), Sem(_sv), Sem(_sg), Sem(_so)
        Rs = lambda n: [Res() for _ in range(n)]
        R_yt = Rs(2)
        R_mb = Rs(2)
        R_xe = Rs(4)
        R_ps = Rs(4)

        for k1 in range(128):
            wb = k1 % 2
            P.dma("sync", mb[:, wb, :, :],
                  mcc[:, k1 * 384:(k1 + 1) * 384].rearrange("p (m j) -> p m j", m=3),
                  sd, writes=[R_mb[wb]])
            for b in range(B):
                ib = (k1 * 2 + b) % 2
                P.dma("sync", yt[:, ib, 0, :], pc[b, 0, k1, :].rearrange("(j c) -> j c", j=128),
                      sd, writes=[R_yt[ib]])
                P.dma("sync", yt[:, ib, 1, :], pc[b, 1, k1, :].rearrange("(j c) -> j c", j=128),
                      sd, writes=[R_yt[ib]])
                qk = k1 % 4
                br = 2 * b
                P.rd("tensor", [R_mb[wb], R_yt[ib]])
                P.wr("tensor", [R_ps[br]])
                P.wr("tensor", [R_ps[br + 1]])
                for (bank, lhs_i, pl) in ((br, 0, 0), (br, 2, 1), (br + 1, 1, 0), (br + 1, 0, 1)):
                    f = (lambda E, bank=bank, lhs_i=lhs_i, pl=pl, wb=wb, ib=ib, qk=qk:
                         E.matmul(ps[:, bank, 128 * qk:128 * (qk + 1)],
                                  mb[:, wb, lhs_i, :], yt[:, ib, pl, :],
                                  start=(pl == 0), stop=(pl == 1)))
                    st.n += 1
                    P.ops["tensor"].append(lambda E, f=f: f(E).then_inc(_st_, 1))
                    P.done_write([R_ps[bank]], st, st.n)
                    P.done_read([R_mb[wb], R_yt[ib]], st, st.n)
                if qk == 3:
                    for pl, bank in ((0, br), (1, br + 1)):
                        sl = (k1 // 4 + pl + 2 * b) % 4
                        P.run("vector", lambda E, bank=bank, sl=sl: E.tensor_copy(
                            xe[:, sl, :], ps[:, bank, :]),
                            sv, reads=[R_ps[bank]], writes=[R_xe[sl]])
                        P.dma("gpsimd",
                              z_out[b, pl, :, 512 * (k1 // 4):512 * (k1 // 4 + 1)],
                              xe[:, sl, :], so, reads=[R_xe[sl]])
        P.finish("gpsimd")
        P.emit(blk)
    return nc


def build_LcD():
    nc = bass.Bass("TRN2", target_bir_lowering=False)
    zd = nc.dram_tensor("zd", [B, 2, 128, 128 * CCH], BF16, kind="ExternalInput").ap()
    ddc = nc.dram_tensor("ddc", [128, 128], F32, kind="ExternalInput").ap()
    o_out = nc.dram_tensor("oo", [B, N, CCH], BF16, kind="ExternalOutput").ap()

    from contextlib import ExitStack
    with ExitStack() as ctx:
        E_ = ctx.enter_context
        zt = E_(nc.sbuf_tensor("zt", [128, 2, 2, 512], BF16))
        ddb = E_(nc.sbuf_tensor("ddb", [128, 128], BF16))
        oe = E_(nc.sbuf_tensor("oeD", [64, 4, 512], BF16))
        ps = E_(nc.psum_tensor("psD", [128, 4, 512], F32))
        _sd = E_(nc.semaphore("sem_d"))
        _st_ = E_(nc.semaphore("sem_t"))
        _sv = E_(nc.semaphore("sem_v"))
        _sg = E_(nc.semaphore("sem_g"))
        _so = E_(nc.semaphore("sem_o"))
        sem_pool = [Sem(E_(nc.semaphore("sp%d" % _i))) for _i in range(24)]
        blk = E_(nc.Block())
        P = Prog(nc, sem_pool)
        sd, st, sv, sg, so = Sem(_sd), Sem(_st_), Sem(_sv), Sem(_sg), Sem(_so)
        Rs = lambda n: [Res() for _ in range(n)]
        R_zt = Rs(2)
        R_dd = Res()
        R_oe = Rs(4)
        R_ps = Rs(4)

        P.dma("gpsimd", ddb[:, :], ddc[:, :], sg, writes=[R_dd])
        for b in range(B):
            for ch in range(32):
                ib = ch % 2
                cs = slice(512 * ch, 512 * (ch + 1))
                P.dma("sync", zt[:, ib, 0, :], zd[b, 0, :, cs], sd, writes=[R_zt[ib]])
                P.dma("sync", zt[:, ib, 1, :], zd[b, 1, :, cs], sd, writes=[R_zt[ib]])
                bank = ib
                P.rd("tensor", [R_zt[ib], R_dd])
                P.wr("tensor", [R_ps[bank]])
                for pl in range(2):
                    f = (lambda E, pl=pl, ib=ib, bank=bank:
                         E.matmul(ps[0:64, bank, :], ddb[:, 64 * pl:64 * (pl + 1)],
                                  zt[:, ib, pl, :], start=(pl == 0), stop=(pl == 1)))
                    st.n += 1
                    P.ops["tensor"].append(lambda E, f=f: f(E).then_inc(_st_, 1))
                    P.done_write([R_ps[bank]], st, st.n)
                    P.done_read([R_zt[ib], R_dd], st, st.n)
                sl = ch % 4
                P.run("vector", lambda E, bank=bank, sl=sl: E.tensor_copy(
                    oe[:, sl, :], ps[0:64, bank, :]),
                    sv, reads=[R_ps[bank]], writes=[R_oe[sl]])
                m2o = 4 * ch
                P.dma("gpsimd",
                      o_out[b, :, :].rearrange("(m1 m2) c -> m1 m2 c", m2=128)[:, m2o:m2o + 4, :].rearrange("m1 m2 c -> m1 (m2 c)"),
                      oe[:, sl, :], so, reads=[R_oe[sl]])
        P.finish("gpsimd")
        P.emit(blk)
    return nc


def _conv_consts():
    j1 = np.arange(64)
    k1 = np.arange(128)
    j2 = np.arange(128)
    k2 = np.arange(128)
    m1 = np.arange(64)
    m2 = np.arange(128)
    w = np.exp(-2j * np.pi / (2 * N))
    w128 = np.exp(-2j * np.pi / 128)
    wi = np.conj(w)
    wi128 = np.conj(w128)
    DA = w128 ** np.outer(j1, k1)
    dfc = np.concatenate([DA.real, DA.imag], 1).astype(np.float32)
    MB = (w ** np.outer(j2, k1)).T[:, :, None] * (w128 ** np.outer(j2, k2))[None, :, :]
    mbc2 = np.zeros((128, 128, 3, 128), np.float32)
    mbc2[:, :, 0] = MB.real.transpose(1, 0, 2)
    mbc2[:, :, 1] = MB.imag.transpose(1, 0, 2)
    mbc2[:, :, 2] = -MB.imag.transpose(1, 0, 2)
    mbc = mbc2.reshape(128, 3 * 128 * 128)
    MC = (wi128 ** np.outer(k2, m2))[None, :, :] * (wi ** np.outer(k1, m2))[:, None, :] / (2.0 * N)
    mcc2 = np.zeros((128, 128, 3, 128), np.float32)
    mcc2[:, :, 0] = MC.real.transpose(1, 0, 2)
    mcc2[:, :, 1] = MC.imag.transpose(1, 0, 2)
    mcc2[:, :, 2] = -MC.imag.transpose(1, 0, 2)
    mcc = mcc2.reshape(128, 3 * 128 * 128)
    DD = wi128 ** np.outer(np.arange(128), m1)
    ddc = np.concatenate([DD.real, -DD.imag], 1).astype(np.float32)
    return dfc, mbc.astype(BF16NP), mcc.astype(BF16NP), ddc


def host_prep_LcA(inputs, l1_results):
    vchs = [np.zeros((B, CCH, N), BF16NP) for _ in range(NC)]
    for p in range(NC):
        posi = own_positions(p)
        cT = np.asarray(l1_results[p]["cT"])
        for q in range(NC):
            for b in range(B):
                vchs[q][b, :, posi] = cT[CCH * q:CCH * (q + 1),
                                         b * TOKB:(b + 1) * TOKB].T
    dfc, _, _, _ = _conv_consts()
    W = [np.asarray(inputs["L0_attn_rpe_hid_w"][i], np.float32) for i in range(3)]
    w4l = np.zeros((3 * 128, 128), np.float32)
    for li in range(3):
        for g in range(4):
            w4l[128 * li + 32 * g:128 * li + 32 * (g + 1),
                32 * g:32 * (g + 1)] = W[li]
    gb4 = np.zeros((3 * 128, 2), np.float32)
    for li in range(3):
        gb4[128 * li:128 * (li + 1), 0] = np.tile(np.asarray(inputs["L0_attn_rpe_ln_g"][li]), 4)
        gb4[128 * li:128 * (li + 1), 1] = np.tile(np.asarray(inputs["L0_attn_rpe_ln_b"][li]), 4)
    blk1 = np.zeros((128, 4), np.float32)
    for g in range(4):
        blk1[32 * g:32 * (g + 1), g] = 1.0
    blk2 = np.ascontiguousarray(blk1.T)
    posr = np.arange(N, dtype=np.float32).reshape(1, N)
    consts = np.array([[1e-5, 1e-6, 0, 0, 0, 0, 0, 0]], np.float32)
    rpein = np.tile(_f32(inputs["L0_attn_rpe_in_w"]).reshape(32), 4).reshape(128, 1)
    pos4c = np.zeros((128, N // 4), np.float32)
    for g in range(4):
        pos4c[32 * g:32 * (g + 1), :] = np.arange(2048 * g, 2048 * (g + 1),
                                                  dtype=np.float32)[None, :]
    wout = _f32(inputs["L0_attn_rpe_out_w"])
    in_maps = []
    for p in range(NC):
        wog = np.zeros((4 * 128, 128), np.float32)
        wsl = wout[:, CCH * p:CCH * (p + 1)]
        for g in range(4):
            wog[128 * g + 32 * g:128 * g + 32 * (g + 1), :] = wsl
        gamsl = _f32(inputs["L0_attn_decay_gamma"])[CCH * p:CCH * (p + 1)].reshape(1, CCH)
        in_maps.append(dict(vch=vchs[p], rpein=rpein, pos4c=pos4c, w4l=w4l,
                            wog=wog, gb4=gb4, gam=gamsl, posr=posr, blk1=blk1,
                            blk2=blk2, dfc=dfc, constv=consts))
    return in_maps


def run_conv(inputs, l1_results):
    dfc, mbc, mcc, ddc = _conv_consts()
    ncA = _get_prog("LcA", build_LcA)
    resA = run_bass_kernel_spmd(ncA, host_prep_LcA(inputs, l1_results),
                                core_ids=list(range(NC))).results
    ncB = _get_prog("LcB", build_LcB)
    in_mapsB = []
    for p in range(NC):
        ya = np.asarray(resA[p]["ya"]).reshape(3, 2, 128, CCH, 128)
        ybt = np.ascontiguousarray(ya.transpose(0, 1, 2, 4, 3)).reshape(3, 2, 128, 128 * CCH)
        in_mapsB.append(dict(yb=ybt, mbc=mbc))
    resB = run_bass_kernel_spmd(ncB, in_mapsB, core_ids=list(range(NC))).results
    ncC = _get_prog("LcC", build_LcC)
    in_mapsC = []
    for p in range(NC):
        pp = np.asarray(resB[p]["pp"]).reshape(B, 2, 128, 128, CCH)
        pct = np.ascontiguousarray(pp.transpose(0, 1, 3, 2, 4)).reshape(B, 2, 128, 128 * CCH)
        in_mapsC.append(dict(pc=pct, mcc=mcc))
    resC = run_bass_kernel_spmd(ncC, in_mapsC, core_ids=list(range(NC))).results
    ncD = _get_prog("LcD", build_LcD)
    in_mapsD = []
    for p in range(NC):
        zz = np.asarray(resC[p]["zz"]).reshape(B, 2, 128, 128, CCH)
        zdt = np.ascontiguousarray(zz.transpose(0, 1, 3, 2, 4)).reshape(B, 2, 128, 128 * CCH)
        in_mapsD.append(dict(zd=zdt, ddc=ddc))
    resD = run_bass_kernel_spmd(ncD, in_mapsD, core_ids=list(range(NC))).results
    return resD


def build_L5():
    nc = bass.Bass("TRN2", target_bir_lowering=False)
    x_in = nc.dram_tensor("x5", [TOKC, D], F32, kind="ExternalInput").ap()
    proc_in = nc.dram_tensor("proc5", [TOKC, D], F32, kind="ExternalInput").ap()
    updf = nc.dram_tensor("updf", [B * KSEL, D], F32, kind="ExternalInput").ap()
    rks_in = nc.dram_tensor("rks", [TOKC, 2], F32, kind="ExternalInput").ap()
    u_in = nc.dram_tensor("u5", [TOKC, D], BF16, kind="ExternalInput").ap()
    o_in = nc.dram_tensor("o5", [TOKC, D], BF16, kind="ExternalInput").ap()
    wot = nc.dram_tensor("wot", [8 * 128, 8 * 128], F32, kind="ExternalInput").ap()
    identc = nc.dram_tensor("identc", [128, 128], F32, kind="ExternalInput").ap()

    y_out = nc.dram_tensor("y", [TOKC, D], F32, kind="ExternalOutput").ap()

    from contextlib import ExitStack
    with ExitStack() as ctx:
        E_ = ctx.enter_context
        yb = E_(nc.sbuf_tensor("yb5", [128, 4, D], F32))
        pb = E_(nc.sbuf_tensor("pb5", [128, 4, D], F32))
        ub = E_(nc.sbuf_tensor("ub5", [128, 2, D], BF16))
        ob_ = E_(nc.sbuf_tensor("ob5x", [128, 2, D], BF16))
        uo = E_(nc.sbuf_tensor("uo5", [128, 2, D], BF16))
        uoT = E_(nc.sbuf_tensor("uoT", [128, 8, 512], BF16))
        rkb = E_(nc.sbuf_tensor("rkb5", [128, 4, 2], F32))
        gix = E_(nc.sbuf_tensor("gix", [128, 4], F32))
        giu = E_(nc.sbuf_tensor("giu", [128, 1], U32))
        wt = E_(nc.sbuf_tensor("wt5", [128, 2, 8, 128], BF16))
        oev = E_(nc.sbuf_tensor("oev", [128, 2, 512], BF16))
        ident = E_(nc.sbuf_tensor("ident5", [128, 128], BF16))
        ps = E_(nc.psum_tensor("ps5", [128, 2, 512], F32))
        pst = E_(nc.psum_tensor("pst5", [128, 2, 1024], BF16))
        sem_pool = [Sem(E_(nc.semaphore("sp%d" % _i))) for _i in range(24)]
        _st_ = E_(nc.semaphore("sem_t"))
        _sv = E_(nc.semaphore("sem_v"))
        _sa = E_(nc.semaphore("sem_a"))
        _so = E_(nc.semaphore("sem_o"))
        blk = E_(nc.Block())
        P = Prog(nc, sem_pool)
        st, sv, sa, so = Sem(_st_), Sem(_sv), Sem(_sa), Sem(_so)
        Rs = lambda n: [Res() for _ in range(n)]
        R_yb = Rs(4)
        R_pb = Rs(4)
        R_ub, R_ob, R_uo = Rs(2), Rs(2), Rs(2)
        R_uoT = Res()
        R_rkb = Rs(4)
        R_gix, R_giu = Res(), Res()
        R_wt = Rs(2)
        R_oev = Rs(2)
        R_ident = Res()
        R_ps = Rs(2)
        R_pst = Rs(2)

        P.dma("gpsimd", ident[:, :], identc[:, :], None, writes=[R_ident])

        for ttok in range(4):
            for sub in range(4):
                ti = 4 * ttok + sub
                bb = ti // 8
                P.dma("gpsimd", pb[:, sub, :], proc_in[128 * ti:128 * (ti + 1), :],
                      None, writes=[R_pb[sub]])
                P.dma("sync", rkb[:, sub, :], rks_in[128 * ti:128 * (ti + 1), :],
                      None, writes=[R_rkb[sub]])
                P.run("vector", lambda E, sub=sub, bb=bb: E.tensor_scalar(
                    gix[:, sub:sub + 1], rkb[:, sub, 1:2], float(-1e9), float(1e9),
                    op0=ALU.mult, op1=ALU.add),
                    sv, reads=[R_rkb[sub]], writes=[R_gix])
                P.run("vector", lambda E, sub=sub, bb=bb: E.scalar_tensor_tensor(
                    gix[:, sub:sub + 1], rkb[:, sub, 0:1], float(1.0),
                    gix[:, sub:sub + 1], op0=ALU.mult, op1=ALU.add),
                    sv, reads=[R_rkb[sub], R_gix], writes=[R_gix])
                P.run("vector", lambda E, sub=sub, bb=bb: E.tensor_scalar(
                    gix[:, sub:sub + 1], gix[:, sub:sub + 1], float(bb * KSEL), None,
                    op0=ALU.add),
                    sv, reads=[R_gix], writes=[R_gix])
                P.run("vector", lambda E, sub=sub: E.tensor_copy(
                    giu[:, :], gix[:, sub:sub + 1]),
                    sv, reads=[R_gix], writes=[R_giu])
                P.run("gpsimd", lambda E, sub=sub: E.indirect_dma_start(
                    out=pb[:, sub, :], out_offset=None, in_=updf[:, :],
                    in_offset=bass.IndirectOffsetOnAxis(ap=giu[:, :], axis=0),
                    bounds_check=B * KSEL - 1, oob_is_err=False),
                    P._dma_sem(R_pb[sub]), k=16, reads=[R_giu], writes=[R_pb[sub]])
                P.dma("sync", yb[:, sub, :], x_in[128 * ti:128 * (ti + 1), :],
                      None, writes=[R_yb[sub]])
                P.run("vector", lambda E, sub=sub: E.tensor_tensor(
                    yb[:, sub, :], yb[:, sub, :], pb[:, sub, :], op=ALU.add),
                    sv, reads=[R_pb[sub]], writes=[R_yb[sub]])
                us = sub % 2
                P.dma("sync", ub[:, us, :], u_in[128 * ti:128 * (ti + 1), :],
                      None, writes=[R_ub[us]])
                P.dma("sync", ob_[:, us, :], o_in[128 * ti:128 * (ti + 1), :],
                      None, writes=[R_ob[us]])
                P.run("vector", lambda E, us=us: E.tensor_tensor(
                    uo[:, us, :], ub[:, us, :], ob_[:, us, :], op=ALU.mult),
                    sv, reads=[R_ub[us], R_ob[us]], writes=[R_uo[us]])
                for c in range(8):
                    pb2 = (ti * 8 + c) % 2
                    P.run("tensor", lambda E, us=us, c=c, pb2=pb2: E.transpose(
                        pst[:, pb2, 0:128], uo[:, us, 128 * c:128 * (c + 1)],
                        ident[:, :]),
                        st, reads=[R_uo[us], R_ident], writes=[R_pst[pb2]])
                    P.run("vector", lambda E, sub=sub, c=c, pb2=pb2: E.tensor_copy(
                        uoT[:, c, 128 * sub:128 * (sub + 1)], pst[:, pb2, 0:128]),
                        sv, reads=[R_pst[pb2]], writes=[R_uoT])
            for oc in range(8):
                wb = oc % 2
                P.dma("gpsimd", wt[:, wb, :, :],
                      wot[128 * oc:128 * (oc + 1), :].rearrange("p (c j) -> p c j", c=8),
                      None, writes=[R_wt[wb]])
                bank = oc % 2
                P.rd("tensor", [R_wt[wb], R_uoT])
                P.wr("tensor", [R_ps[bank]])
                for c in range(8):
                    f = (lambda E, c=c, wb=wb, bank=bank:
                         E.matmul(ps[:, bank, :], wt[:, wb, c, :], uoT[:, c, :],
                                  start=(c == 0), stop=(c == 7)))
                    if c == 7:
                        st.n += 1
                        P.ops["tensor"].append(lambda E, f=f: f(E).then_inc(_st_, 1))
                        P.done_write([R_ps[bank]], st, st.n)
                        P.done_read([R_wt[wb], R_uoT], st, st.n)
                    else:
                        P.ops["tensor"].append(lambda E, f=f: f(E))
                sl = oc % 2
                P.run("scalar", lambda E, bank=bank, sl=sl: E.activation(
                    oev[:, sl, :], ps[:, bank, :], AF.Identity),
                    sa, reads=[R_ps[bank]], writes=[R_oev[sl]])
                for sub in range(4):
                    pb2 = (oc * 4 + sub) % 2
                    P.run("tensor", lambda E, sl=sl, sub=sub, pb2=pb2: E.transpose(
                        pst[:, pb2, 0:128], oev[:, sl, 128 * sub:128 * (sub + 1)],
                        ident[:, :]),
                        st, reads=[R_oev[sl], R_ident], writes=[R_pst[pb2]])
                    P.run("vector", lambda E, sub=sub, oc=oc, pb2=pb2: E.tensor_tensor(
                        yb[:, sub, 128 * oc:128 * (oc + 1)],
                        yb[:, sub, 128 * oc:128 * (oc + 1)],
                        pst[:, pb2, 0:128], op=ALU.add),
                        sv, reads=[R_pst[pb2]], writes=[R_yb[sub]])
            for sub in range(4):
                ti = 4 * ttok + sub
                P.dma("sync", y_out[128 * ti:128 * (ti + 1), :], yb[:, sub, :],
                      so, reads=[R_yb[sub]])
        P.finish("sync")
        P.emit(blk)
    return nc


def host_prep_L5(inputs, l1_results, l2b_results, l3_results, l4_results, lc_results):
    x = np.asarray(inputs["L0_attn_x"])
    updf = np.zeros((B * KSEL, D), np.float32)
    for p in range(NC):
        upd = np.asarray(l4_results[p]["updo"])
        bases = [(0, 256 * p), (0, 256 * p + 128),
                 (1, 256 * (7 - p)), (1, 256 * (7 - p) + 128)]
        for tl, (b, base) in enumerate(bases):
            updf[b * KSEL + base:b * KSEL + base + 128] = upd[128 * tl:128 * (tl + 1)]
    rkm = np.asarray(l3_results[0]["rkm"])
    rank_flat = rkm[:, 0:128].reshape(-1)
    selm_flat = rkm[:, 128:256].reshape(-1)
    wotl = _tile_w(inputs["L0_attn_W_o"], 8, 8)
    ident = np.eye(128, dtype=np.float32)
    in_maps = []
    for p in range(NC):
        posi = own_positions(p)
        x_own = _f32(np.concatenate([np.asarray(x[b])[posi] for b in range(B)], 0))
        rks = np.zeros((TOKC, 2), np.float32)
        for b in range(B):
            rks[b * TOKB:(b + 1) * TOKB, 0] = rank_flat[b * N + posi]
            rks[b * TOKB:(b + 1) * TOKB, 1] = selm_flat[b * N + posi]
        uT = np.asarray(l1_results[p]["uT"])
        u_own = np.ascontiguousarray(uT.T)
        o_own = np.zeros((TOKC, D), BF16NP)
        for q in range(NC):
            oo = np.asarray(lc_results[q]["oo"])
            for b in range(B):
                o_own[b * TOKB:(b + 1) * TOKB, 128 * q:128 * (q + 1)] = oo[b][posi]
        in_maps.append(dict(x5=x_own, proc5=np.asarray(l2b_results[p]["proc"]),
                            updf=updf, rks=rks, u5=u_own, o5=o_own, wot=wotl,
                            identc=ident))
    return in_maps


def run_L5(inputs, l1_results, l2b_results, l3_results, l4_results, lc_results):
    nc = _get_prog("L5", build_L5)
    in_maps = host_prep_L5(inputs, l1_results, l2b_results, l3_results,
                           l4_results, lc_results)
    res = run_bass_kernel_spmd(nc, in_maps, core_ids=list(range(NC)))
    return res.results

def kernel(**inputs):
    l1 = run_L1(inputs)
    l2a = run_L2a(l1)
    l2b = run_L2b(inputs, l1, l2a)
    lc = run_conv(inputs, l1)
    l3 = run_L3(inputs, l2b)
    l4 = run_L4(inputs, l3)
    l5 = run_L5(inputs, l1, l2b, l3, l4, lc)
    out = np.zeros((B, N, D), np.float32)
    for p in range(NC):
        y = np.asarray(l5[p]["y"])
        posi = own_positions(p)
        for b in range(B):
            out[b, posi] = y[b * TOKB:(b + 1) * TOKB]
    return out


# version 03 (final)
# speedup vs baseline: 1.1864x; 1.1864x over previous
import numpy as np
import ml_dtypes
import concourse.bass as bass
import concourse.mybir as mybir
from concourse.bass_utils import run_bass_kernel_spmd

F32 = mybir.dt.float32
BF16 = mybir.dt.bfloat16
I32 = mybir.dt.int32
U32 = mybir.dt.uint32
AF = mybir.ActivationFunctionType
ALU = mybir.AluOpType
BF16NP = ml_dtypes.bfloat16

B, N, D = 2, 8192, 1024
NC = 8
GR = 512
NG = N // GR
H, DH, HALF = 16, 64, 32
FFN = 4096
KSEL = N // 4
TOKC = 2048
TOKB = 1024


def own_granules(p):
    return [p, NG - 1 - p]


def own_positions(p):
    g0, g1 = own_granules(p)
    return np.concatenate([np.arange(g0 * GR, (g0 + 1) * GR),
                           np.arange(g1 * GR, (g1 + 1) * GR)])


def perm_qk():
    x1 = np.concatenate([np.arange(h * 64, h * 64 + 32) for h in range(H)])
    return np.concatenate([x1, x1 + 32])


PERM_QK = perm_qk()


def rope_tables(positions):
    pos = np.asarray(positions, dtype=np.float64)
    inv = 1.0 / 10000.0 ** (np.arange(HALF) / HALF)
    j = np.arange(128) % HALF
    ang = pos[None, :] * inv[j][:, None]
    return np.cos(ang).astype(np.float32), np.sin(ang).astype(np.float32)


class Sem:
    __slots__ = ("h", "n")

    def __init__(self, handle):
        self.h = handle
        self.n = 0


class Res:
    __slots__ = ("w", "r")

    def __init__(self):
        self.w = None
        self.r = []


class Prog:
    ENGINES = ("sync", "tensor", "vector", "scalar", "gpsimd")

    def __init__(self, nc, sem_pool=None):
        self.nc = nc
        self.ops = {e: [] for e in self.ENGINES}
        self._waited = {e: {} for e in self.ENGINES}
        self.sem_pool = list(sem_pool or [])
        self._res_sem = {}

    def _dma_sem(self, res):
        key = id(res)
        if key not in self._res_sem:
            if not self.sem_pool:
                raise RuntimeError("sem pool exhausted")
            self._res_sem[key] = self.sem_pool.pop()
        return self._res_sem[key]

    def emit(self, blk):
        for e in self.ENGINES:
            dec = getattr(blk, e)
            ops = self.ops[e]

            def body(E, ops=ops):
                for fn in ops:
                    fn(E)
            dec(body)

    def wait(self, eng, sem, val):
        if val is None or val <= 0:
            return
        seen = self._waited[eng].get(id(sem), 0)
        if val <= seen:
            return
        self._waited[eng][id(sem)] = val
        self.ops[eng].append(lambda E, s=sem.h, v=val: E.wait_ge(s, v))

    def _wait_tok(self, eng, tok):
        if tok is not None:
            self.wait(eng, tok[0], tok[1])

    def rd(self, eng, res_list):
        for r in res_list:
            self._wait_tok(eng, r.w)

    def wr(self, eng, res_list):
        for r in res_list:
            for tok in r.r:
                self._wait_tok(eng, tok)
            self._wait_tok(eng, r.w)

    def done_write(self, res_list, sem, val):
        for r in res_list:
            r.w = (sem, val)
            r.r = []

    def done_read(self, res_list, sem, val):
        for r in res_list:
            r.r.append((sem, val))

    def run(self, eng, f, sem, k=1, reads=(), writes=()):
        self.rd(eng, reads)
        self.wr(eng, writes)
        sem.n += k
        val = sem.n

        def g(E, f=f, s=sem.h, k=k):
            f(E).then_inc(s, k)
        self.ops[eng].append(g)
        self.done_write(writes, sem, val)
        self.done_read(reads, sem, val)
        return val

    def run0(self, eng, f, reads=(), writes=(), after=None):
        self.rd(eng, reads)
        self.wr(eng, writes)
        if after is not None:
            self._wait_tok(eng, after)
        self.ops[eng].append(lambda E, f=f: f(E))

    def dma(self, eng, out, in_, sem, reads=(), writes=(), **kw):
        is_store = not writes
        if writes:
            sem = self._dma_sem(writes[0])
        elif reads:
            sem = self._dma_sem(reads[0])
        val = self.run(eng, lambda E, o=out, i=in_, kw=kw: E.dma_start(out=o, i_=i, **kw) if False else E.dma_start(out=o, in_=i, **kw),
                       sem, k=16, reads=reads, writes=writes)
        if is_store:
            if not hasattr(self, "store_toks"):
                self.store_toks = []
            self.store_toks.append((sem, val))
        return val

    def finish(self, eng):
        for sem, val in getattr(self, "store_toks", []):
            self.wait(eng, sem, val)


def build_L1():
    nc = bass.Bass("TRN2", target_bir_lowering=False)
    x_in = nc.dram_tensor("x", [TOKC, D], F32, kind="ExternalInput").ap()
    ln1 = nc.dram_tensor("ln1", [1, D], F32, kind="ExternalInput").ap()
    w_q = nc.dram_tensor("wq", [D, D], F32, kind="ExternalInput").ap()
    w_k = nc.dram_tensor("wk", [D, D], F32, kind="ExternalInput").ap()
    w_v = nc.dram_tensor("wv", [D, D], F32, kind="ExternalInput").ap()
    w_uv = nc.dram_tensor("wuv", [D, 2 * D], F32, kind="ExternalInput").ap()
    cosT = nc.dram_tensor("cosT", [128, TOKC], F32, kind="ExternalInput").ap()
    sinT = nc.dram_tensor("sinT", [128, TOKC], F32, kind="ExternalInput").ap()
    identc = nc.dram_tensor("identc", [128, 128], F32, kind="ExternalInput").ap()
    constv = nc.dram_tensor("constv", [1, 8], F32, kind="ExternalInput").ap()

    xn_out = nc.dram_tensor("xn", [TOKC, D], F32, kind="ExternalOutput").ap()
    qT_out = nc.dram_tensor("qT", [D, TOKC], BF16, kind="ExternalOutput").ap()
    kT_out = nc.dram_tensor("kT", [D, TOKC], BF16, kind="ExternalOutput").ap()
    vT_out = nc.dram_tensor("vT", [D, TOKC], BF16, kind="ExternalOutput").ap()
    uT_out = nc.dram_tensor("uT", [D, TOKC], BF16, kind="ExternalOutput").ap()
    cT_out = nc.dram_tensor("cT", [D, TOKC], BF16, kind="ExternalOutput").ap()

    NT = TOKC // 128

    from contextlib import ExitStack
    with ExitStack() as ctx:
        E_ = ctx.enter_context
        xt = E_(nc.sbuf_tensor("xt", [128, 4, D], F32))
        xnt = E_(nc.sbuf_tensor("xnt", [128, 4, D], F32))
        hb = E_(nc.sbuf_tensor("hb", [128, NT, D], BF16))
        xnb = E_(nc.sbuf_tensor("xnb", [128, NT, D], BF16))
        hT = E_(nc.sbuf_tensor("hT", [128, 8, TOKC], BF16))
        xnT = E_(nc.sbuf_tensor("xnT", [128, 8, TOKC], BF16))
        wt = E_(nc.sbuf_tensor("wt", [128, 8, 2, 128], BF16))
        ident = E_(nc.sbuf_tensor("ident", [128, 128], BF16))
        lnb = E_(nc.sbuf_tensor("lnb", [128, D], F32))
        cosb = E_(nc.sbuf_tensor("cosb", [128, TOKC], F32))
        sinb = E_(nc.sbuf_tensor("sinb", [128, TOKC], F32))
        st6 = E_(nc.sbuf_tensor("st6", [128, NT, 12], F32))
        stat = E_(nc.sbuf_tensor("stat", [128, NT, 4], F32))
        scr = E_(nc.sbuf_tensor("scr", [128, D], BF16))
        cvb = E_(nc.sbuf_tensor("cvb", [128, 8], F32))
        ob = E_(nc.sbuf_tensor("ob", [128, 8, 512], BF16))
        ps = E_(nc.psum_tensor("ps", [128, 2, 512], F32))
        pst = E_(nc.psum_tensor("pst", [128, 2, 1024], BF16))
        _sd = E_(nc.semaphore("sem_d"))
        _st_ = E_(nc.semaphore("sem_t"))
        _sv = E_(nc.semaphore("sem_v"))
        _sa = E_(nc.semaphore("sem_a"))
        _sg = E_(nc.semaphore("sem_g"))
        _so = E_(nc.semaphore("sem_o"))
        sem_pool = [Sem(E_(nc.semaphore("sp%d" % _i))) for _i in range(48)]
        blk = E_(nc.Block())
        P = Prog(nc, sem_pool)
        sd = Sem(_sd)
        st = Sem(_st_)
        sv = Sem(_sv)
        sa = Sem(_sa)
        sg = Sem(_sg)
        so = Sem(_so)

        R_x = [Res() for _ in range(4)]
        R_xn = [Res() for _ in range(4)]
        R_xnb = [Res() for _ in range(NT)]
        R_hb = [Res() for _ in range(NT)]
        R_hT = Res()
        R_xnT = Res()
        R_wt = [[Res() for _ in range(2)] for _ in range(8)]
        R_ps = [Res() for _ in range(2)]
        R_pst = [Res() for _ in range(2)]
        R_ob = [Res() for _ in range(8)]
        R_st6 = [Res() for _ in range(NT)]
        R_stat = [Res() for _ in range(NT)]
        R_ident = Res()
        R_lnb = Res()
        R_cs = Res()
        R_scr = Res()

        P.dma("gpsimd", ident[:, :], identc[:, :], sg, writes=[R_ident])
        P.dma("sync", lnb[:, :], ln1.to_broadcast([128, D]), sd, writes=[R_lnb])
        R_cv = Res()
        P.dma("sync", cvb[:, :], constv.to_broadcast([128, 8]), sd, writes=[R_cv])
        P.dma("sync", cosb[:, :], cosT[:, :], sd, writes=[R_cs])
        P.dma("sync", sinb[:, :], sinT[:, :], sd, writes=[R_cs])
        for i in range(NT):
            s = i % 4
            P.dma("sync", xt[:, s, :], x_in[128 * i:128 * (i + 1), :], sd,
                  writes=[R_x[s]])
            P.run("vector", lambda E, i=i, s=s: E.bn_stats(st6[:, i, 0:6],
                                                           xt[:, s, 0:512]),
                  sv, reads=[R_x[s]], writes=[R_st6[i]])
            P.run("vector", lambda E, i=i, s=s: E.bn_stats(st6[:, i, 6:12],
                                                           xt[:, s, 512:1024]),
                  sv, reads=[R_x[s]], writes=[R_st6[i]])
            P.run("vector", lambda E, i=i: E.bn_aggr(stat[:, i, 0:2], st6[:, i, :]),
                  sv, reads=[R_st6[i]], writes=[R_stat[i]])
            P.run("scalar", lambda E, i=i: E.activation(
                stat[:, i, 2:3], stat[:, i, 1:2], AF.Ln, bias=cvb[:, 0:1]),
                sa, reads=[R_stat[i], R_cv], writes=[R_stat[i]])
            P.run("scalar", lambda E, i=i: E.activation(
                stat[:, i, 2:3], stat[:, i, 2:3], AF.Exp, scale=-0.5),
                sa, reads=[R_stat[i]], writes=[R_stat[i]])
            P.run("vector", lambda E, i=i, s=s: E.tensor_scalar(
                xnt[:, s, :], xt[:, s, :], stat[:, i, 0:1], stat[:, i, 2:3],
                op0=ALU.subtract, op1=ALU.mult),
                sv, reads=[R_x[s], R_stat[i]], writes=[R_xn[s]])
            P.run("vector", lambda E, i=i, s=s: E.tensor_copy(xnb[:, i, :], xnt[:, s, :]),
                  sv, reads=[R_xn[s]], writes=[R_xnb[i]])
            P.run("scalar", lambda E, i=i, s=s: E.activation(
                scr[:, :], xnt[:, s, :], AF.Square, accum_out=stat[:, i, 3:4]),
                sa, reads=[R_xn[s]], writes=[R_stat[i], R_scr])
            P.run("scalar", lambda E, i=i: E.activation(
                stat[:, i, 3:4], stat[:, i, 3:4], AF.Ln, bias=cvb[:, 1:2],
                scale=float(1.0 / D)),
                sa, reads=[R_stat[i], R_cv], writes=[R_stat[i]])
            P.run("scalar", lambda E, i=i: E.activation(
                stat[:, i, 3:4], stat[:, i, 3:4], AF.Exp, scale=-0.5),
                sa, reads=[R_stat[i]], writes=[R_stat[i]])
            P.run("vector", lambda E, i=i, s=s: E.scalar_tensor_tensor(
                hb[:, i, :], xnt[:, s, :], stat[:, i, 3:4], lnb[:, :],
                op0=ALU.mult, op1=ALU.mult),
                sv, reads=[R_xn[s], R_stat[i], R_lnb], writes=[R_hb[i]])
            P.dma("gpsimd", xn_out[128 * i:128 * (i + 1), :], xnt[:, s, :], so,
                  reads=[R_xn[s]])

        def transpose_all(src_tiles, src_res, dstT, dst_res):
            for i in range(NT):
                for c in range(8):
                    pb = (i * 8 + c) % 2
                    P.run("tensor", lambda E, i=i, c=c, pb=pb, s=src_tiles: E.transpose(
                        pst[:, pb, 0:128], s[:, i, 128 * c:128 * (c + 1)], ident[:, :]),
                        st, reads=[src_res[i], R_ident], writes=[R_pst[pb]])
                    P.run("vector", lambda E, i=i, c=c, pb=pb, d=dstT: E.tensor_copy(
                        d[:, c, 128 * i:128 * (i + 1)], pst[:, pb, 0:128]),
                        sv, reads=[R_pst[pb]], writes=[dst_res])

        transpose_all(hb, R_hb, hT, R_hT)
        transpose_all(xnb, R_xnb, xnT, R_xnT)

        def load_wchunks(w_dram, oc, buf):
            for c in range(8):
                P.dma("gpsimd", wt[:, c, buf, :],
                      w_dram[128 * c:128 * (c + 1), 128 * oc:128 * (oc + 1)],
                      sg, writes=[R_wt[c][buf]])
            return [R_wt[c][buf] for c in range(8)]

        def mm_chain(rhsT, rhs_res, wres, pb, ttok, extra_reads=()):
            for c in range(8):
                P.run0("tensor", lambda E, c=c, pb=pb, ttok=ttok, r=rhsT: E.matmul(
                    ps[:, pb, :], wt[:, c, wres[c][1] if False else 0, :],
                    r[:, c, 512 * ttok:512 * (ttok + 1)],
                    start=(c == 0), stop=(c == 7)))
            return None

        def project_plain(w_dram, rhsT, rhs_res, out_dram, silu):
            for oc in range(8):
                buf = oc % 2
                wres = load_wchunks(w_dram, oc, buf)
                for ttok in range(4):
                    pb = ttok % 2
                    P.rd("tensor", wres + [rhs_res])
                    P.wr("tensor", [R_ps[pb]])
                    for c in range(8):
                        last = (c == 7)
                        f = (lambda E, c=c, pb=pb, ttok=ttok, buf=buf, r=rhsT:
                             E.matmul(ps[:, pb, :], wt[:, c, buf, :],
                                      r[:, c, 512 * ttok:512 * (ttok + 1)],
                                      start=(c == 0), stop=(c == 7)))
                        if last:
                            st.n += 1
                            val = st.n
                            P.ops["tensor"].append(
                                lambda E, f=f, v=val: f(E).then_inc(_st_, 1))
                            P.done_write([R_ps[pb]], st, val)
                            P.done_read(wres + [rhs_res], st, val)
                        else:
                            P.ops["tensor"].append(lambda E, f=f: f(E))
                    slot = (oc * 4 + ttok) % 8
                    P.run("scalar", lambda E, pb=pb, slot=slot: E.activation(
                        ob[:, slot, :], ps[:, pb, :],
                        AF.Silu if silu else AF.Identity),
                        sa, reads=[R_ps[pb]], writes=[R_ob[slot]])
                    P.dma("gpsimd",
                          out_dram[128 * oc:128 * (oc + 1), 512 * ttok:512 * (ttok + 1)],
                          ob[:, slot, :], so, reads=[R_ob[slot]])

        def project_rope(w_dram, rhsT, rhs_res, out_dram):
            for ocp in range(4):
                wres0 = load_wchunks(w_dram, ocp, 0)
                wres1 = load_wchunks(w_dram, ocp + 4, 1)
                for ttok in range(4):
                    for half, wres in ((0, wres0), (1, wres1)):
                        P.rd("tensor", wres + [rhs_res])
                        P.wr("tensor", [R_ps[half]])
                        for c in range(8):
                            f = (lambda E, c=c, half=half, ttok=ttok, r=rhsT:
                                 E.matmul(ps[:, half, :], wt[:, c, half, :],
                                          r[:, c, 512 * ttok:512 * (ttok + 1)],
                                          start=(c == 0), stop=(c == 7)))
                            if c == 7:
                                st.n += 1
                                val = st.n
                                P.ops["tensor"].append(
                                    lambda E, f=f: f(E).then_inc(_st_, 1))
                                P.done_write([R_ps[half]], st, val)
                                P.done_read(wres + [rhs_res], st, val)
                            else:
                                P.ops["tensor"].append(lambda E, f=f: f(E))
                    b0, b1, b2, b3 = (R_ob[4], R_ob[5], R_ob[6], R_ob[7])
                    cs_ap = cosb[:, 512 * ttok:512 * (ttok + 1)]
                    sn_ap = sinb[:, 512 * ttok:512 * (ttok + 1)]
                    P.run("vector", lambda E, cs=cs_ap: E.tensor_tensor(
                        ob[:, 4, :], ps[:, 0, :], cs, op=ALU.mult),
                        sv, reads=[R_ps[0], R_cs], writes=[b0])
                    P.run("vector", lambda E, sn=sn_ap: E.tensor_tensor(
                        ob[:, 5, :], ps[:, 1, :], sn, op=ALU.mult),
                        sv, reads=[R_ps[1], R_cs], writes=[b1])
                    P.run("vector", lambda E, cs=cs_ap: E.tensor_tensor(
                        ob[:, 6, :], ps[:, 1, :], cs, op=ALU.mult),
                        sv, reads=[R_ps[1], R_cs], writes=[b2])
                    P.run("vector", lambda E, sn=sn_ap: E.tensor_tensor(
                        ob[:, 7, :], ps[:, 0, :], sn, op=ALU.mult),
                        sv, reads=[R_ps[0], R_cs], writes=[b3])
                    P.run("vector", lambda E: E.tensor_tensor(
                        ob[:, 4, :], ob[:, 4, :], ob[:, 5, :], op=ALU.subtract),
                        sv, reads=[b1], writes=[b0])
                    P.run("vector", lambda E: E.tensor_tensor(
                        ob[:, 6, :], ob[:, 6, :], ob[:, 7, :], op=ALU.add),
                        sv, reads=[b3], writes=[b2])
                    P.dma("gpsimd",
                          out_dram[128 * ocp:128 * (ocp + 1), 512 * ttok:512 * (ttok + 1)],
                          ob[:, 4, :], so, reads=[b0])
                    P.dma("gpsimd",
                          out_dram[128 * (ocp + 4):128 * (ocp + 5), 512 * ttok:512 * (ttok + 1)],
                          ob[:, 6, :], so, reads=[b2])

        project_rope(w_q, hT, R_hT, qT_out)
        project_rope(w_k, hT, R_hT, kT_out)
        project_plain(w_v, hT, R_hT, vT_out, silu=False)
        project_plain(w_uv[:, 0:D], xnT, R_xnT, uT_out, silu=True)
        project_plain(w_uv[:, D:2 * D], xnT, R_xnT, cT_out, silu=True)

        P.finish("gpsimd")
        P.emit(blk)
    return nc


_progs = {}


def _get_prog(name, builder):
    if name not in _progs:
        _progs[name] = builder()
    return _progs[name]


def _f32(a):
    return np.ascontiguousarray(a, dtype=np.float32)


def run_L1(inputs):
    nc = _get_prog("L1", build_L1)
    x = np.asarray(inputs["L0_attn_x"])
    wq = _f32(np.asarray(inputs["L0_ffn_wq1"])[:, PERM_QK])
    wk = _f32(np.asarray(inputs["L0_ffn_wk1"])[:, PERM_QK])
    wv = _f32(inputs["L0_ffn_wv1"])
    wuv = _f32(inputs["L0_attn_W_uv"])
    ln1 = _f32(inputs["L0_ffn_ln1_1"]).reshape(1, D)
    ident = np.eye(128, dtype=np.float32)
    in_maps = []
    for p in range(NC):
        posi = own_positions(p)
        x_own = np.concatenate([np.asarray(x[b])[posi] for b in range(B)], axis=0)
        cosT, sinT = rope_tables(np.concatenate([posi, posi]))
        consts = np.array([[1e-5, 1e-6, 0, 0, 0, 0, 0, 0]], np.float32)
        in_maps.append(dict(x=_f32(x_own), ln1=ln1, wq=wq, wk=wk, wv=wv, wuv=wuv,
                            cosT=cosT, sinT=sinT, identc=ident, constv=consts))
    res = run_bass_kernel_spmd(nc, in_maps, core_ids=list(range(NC)))
    return res.results




NITER = 32
SPAN = 68
EXPB = 24.0


def l2a_iter_meta(p):
    meta = []
    for i in range(NITER):
        a, u = i // 4, i % 4
        m = ((p + a) % 8) + 8 * u
        meta.append((a, m // 16, m % 16))
    return meta


def build_L2a():
    nc = bass.Bass("TRN2", target_bir_lowering=False)
    qseq = nc.dram_tensor("qseq", [NITER, 2, 64, GR], BF16, kind="ExternalInput").ap()
    kseq = nc.dram_tensor("kseq", [NITER, 64, SPAN * 128], BF16, kind="ExternalInput").ap()
    vswz = nc.dram_tensor("vswz", [NITER, 128, SPAN * 65], BF16, kind="ExternalInput").ap()
    maskc = nc.dram_tensor("maskc", [4 * 128, GR], F32, kind="ExternalInput").ap()
    constv2 = nc.dram_tensor("constv2", [1, 8], F32, kind="ExternalInput").ap()
    o_out = nc.dram_tensor("oat", [NITER, 2, 64, GR], BF16, kind="ExternalOutput").ap()

    from contextlib import ExitStack
    with ExitStack() as ctx:
        E_ = ctx.enter_context
        kb = E_(nc.sbuf_tensor("kb", [64, 2, SPAN * 128], BF16))
        vb = E_(nc.sbuf_tensor("vb", [128, 2, SPAN * 65], BF16))
        qb = E_(nc.sbuf_tensor("qb", [64, 2, 2, GR], BF16))
        mask = E_(nc.sbuf_tensor("mask", [128, 4, GR], BF16))
        pt = E_(nc.sbuf_tensor("pt", [128, 2, GR], BF16))
        rb = E_(nc.sbuf_tensor("rb", [64, GR], F32))
        rd = E_(nc.sbuf_tensor("rd", [1, GR], F32))
        ones1 = E_(nc.sbuf_tensor("ones1", [1, 64], F32))
        cvb2 = E_(nc.sbuf_tensor("cvb2", [128, 8], F32))
        ob = E_(nc.sbuf_tensor("ob2", [64, 2, GR], BF16))
        psS = E_(nc.psum_tensor("psS", [128, 2, 512], F32))
        psO = E_(nc.psum_tensor("psO", [128, 2, 512], F32))
        psB = E_(nc.psum_tensor("psB", [128, 512], F32))
        _sd = E_(nc.semaphore("sem_d"))
        _st_ = E_(nc.semaphore("sem_t"))
        _sv = E_(nc.semaphore("sem_v"))
        _sa = E_(nc.semaphore("sem_a"))
        _so = E_(nc.semaphore("sem_o"))
        _sg = E_(nc.semaphore("sem_g"))
        sem_pool = [Sem(E_(nc.semaphore("sp%d" % _i))) for _i in range(24)]
        blk = E_(nc.Block())

        P = Prog(nc, sem_pool)
        sd, st, sv, sa, so = Sem(_sd), Sem(_st_), Sem(_sv), Sem(_sa), Sem(_so)
        sg = Sem(_sg)

        R_kb = [Res() for _ in range(2)]
        R_vb = [Res() for _ in range(2)]
        R_qb = [Res() for _ in range(2)]
        R_mask = Res()
        R_pt = [Res() for _ in range(2)]
        R_psS = [Res() for _ in range(2)]
        R_psO = [Res() for _ in range(2)]
        R_psB = Res()
        R_rb = Res()
        R_rd = Res()
        R_ones = Res()
        R_ob = [Res() for _ in range(2)]

        P.dma("gpsimd", mask[:, :, :], maskc.rearrange("(a p) q -> p a q", p=128),
              sg, writes=[R_mask])
        P.run("vector", lambda E: E.memset(ones1[:, :], 1.0), sv, writes=[R_ones])
        R_cv2 = Res()
        P.dma("sync", cvb2[:, :], constv2.to_broadcast([128, 8]), sd, writes=[R_cv2])

        for i in range(NITER):
            a = i // 4
            nA, nB = 4 * a + 4, 64 - 4 * a
            ib = i % 2
            P.dma("sync", kb[:, ib, :], kseq[i, :, :], sd, writes=[R_kb[ib]])
            P.dma("sync", vb[:, ib, :], vswz[i, :, :], sd, writes=[R_vb[ib]])
            P.dma("sync", qb[:, ib, 0, :], qseq[i, 0, :, :], sd, writes=[R_qb[ib]])
            P.dma("sync", qb[:, ib, 1, :], qseq[i, 1, :, :], sd, writes=[R_qb[ib]])
            for gs, (base, nchunk) in enumerate(((0, nA), (nA, nB))):
                def issue_S(c, ib=ib, gs=gs, base=base, nchunk=nchunk):
                    sb = c % 2
                    col = (base + c) * 128
                    P.run("tensor", lambda E, ib=ib, gs=gs, sb=sb, col=col: E.matmul(
                        psS[:, sb, :], kb[:, ib, col:col + 128], qb[:, ib, gs, :],
                        start=True, stop=True),
                        st, reads=[R_kb[ib], R_qb[ib]], writes=[R_psS[sb]])
                    P.run("scalar", lambda E, sb=sb: E.activation(
                        pt[:, sb, :], psS[:, sb, :], AF.Exp, scale=0.125,
                        bias=cvb2[:, 0:1]),
                        sa, reads=[R_psS[sb], R_cv2], writes=[R_pt[sb]])
                    dd = c - (nchunk - 4)
                    if dd >= 0:
                        P.run("vector", lambda E, sb=sb, dd=dd: E.tensor_tensor(
                            pt[:, sb, :], pt[:, sb, :], mask[:, dd, :], op=ALU.mult),
                            sv, reads=[R_mask, R_pt[sb]], writes=[R_pt[sb]])

                def issue_PV(c, ib=ib, gs=gs, base=base, nchunk=nchunk):
                    sb = c % 2
                    vcol = (base + c) * 65
                    P.run("tensor", lambda E, ib=ib, gs=gs, sb=sb, vcol=vcol, c=c,
                          nchunk=nchunk: E.matmul(
                        psO[0:65, gs, :], vb[:, ib, vcol:vcol + 65], pt[:, sb, :],
                        start=(c == 0), stop=(c == nchunk - 1)),
                        st, reads=[R_vb[ib], R_pt[sb]], writes=[R_psO[gs]])

                issue_S(0)
                for c in range(1, nchunk):
                    issue_S(c)
                    issue_PV(c - 1)
                issue_PV(nchunk - 1)
                P.run("vector", lambda E, gs=gs: E.reciprocal(
                    rd[:, :], psO[64:65, gs, :]),
                    sv, reads=[R_psO[gs]], writes=[R_rd])
                P.run("tensor", lambda E: E.matmul(
                    psB[0:64, :], ones1[:, :], rd[:, :], start=True, stop=True),
                    st, reads=[R_rd, R_ones], writes=[R_psB])
                P.run("vector", lambda E: E.tensor_copy(rb[:, :], psB[0:64, :]),
                      sv, reads=[R_psB], writes=[R_rb])
                P.run("vector", lambda E, gs=gs: E.tensor_tensor(
                    ob[:, gs, :], psO[0:64, gs, :], rb[:, :], op=ALU.mult),
                    sv, reads=[R_psO[gs], R_rb], writes=[R_ob[gs]])
                P.dma("gpsimd", o_out[i, gs, :, :], ob[:, gs, :], so,
                      reads=[R_ob[gs]])
        P.finish("gpsimd")
        P.emit(blk)
    return nc


def host_prep_L2a(l1_results):
    kT = [np.zeros((D, N), BF16NP) for _ in range(B)]
    vfull = [np.zeros((N, D), BF16NP) for _ in range(B)]
    qT = []
    for p in range(NC):
        r = l1_results[p]
        posi = own_positions(p)
        for b in range(B):
            kT[b][:, posi] = np.asarray(r["kT"])[:, b * TOKB:(b + 1) * TOKB]
            vfull[b][posi, :] = np.asarray(r["vT"])[:, b * TOKB:(b + 1) * TOKB].T
        qT.append(np.asarray(r["qT"]))
    in_maps = []
    for p in range(NC):
        meta = l2a_iter_meta(p)
        qseq = np.zeros((NITER, 2, 64, GR), BF16NP)
        kseq = np.zeros((NITER, 64, SPAN * 128), BF16NP)
        vswz = np.ones((NITER, 128, SPAN * 65), BF16NP)
        for i, (a, b, h) in enumerate(meta):
            rows = np.concatenate([np.arange(32 * h, 32 * h + 32),
                                   np.arange(512 + 32 * h, 512 + 32 * h + 32)])
            for gs, g in enumerate((a, NG - 1 - a)):
                owner = g if g < 8 else NG - 1 - g
                slot = 0 if g == owner else 1
                cols = b * TOKB + slot * GR
                qseq[i, gs] = qT[owner][rows, cols:cols + GR]
            nA = 4 * a + 4
            nB = 64 - 4 * a
            kseq[i, :, 0:nA * 128] = kT[b][rows, 0:nA * 128]
            kseq[i, :, nA * 128:SPAN * 128] = kT[b][rows, 0:nB * 128]
            vA = vfull[b][0:nA * 128, 64 * h:64 * h + 64].reshape(nA, 128, 64)
            vB = vfull[b][0:nB * 128, 64 * h:64 * h + 64].reshape(nB, 128, 64)
            vs = vswz[i].reshape(128, SPAN, 65)
            vs[:, 0:nA, 0:64] = vA.transpose(1, 0, 2)
            vs[:, nA:SPAN, 0:64] = vB.transpose(1, 0, 2)
        in_maps.append(dict(qseq=qseq, kseq=kseq, vswz=vswz, maskc=l2a_masks(),
                            constv2=np.array([[-EXPB, 0, 0, 0, 0, 0, 0, 0]], np.float32)))
    return in_maps


def l2a_masks():
    kk = np.arange(128)[:, None]
    qq = np.arange(GR)[None, :]
    m = np.zeros((4, 128, GR), np.float32)
    for d_ in range(4):
        m[d_] = (128 * d_ + kk <= qq).astype(np.float32)
    return m.reshape(4 * 128, GR)


def run_L2a(l1_results):
    nc = _get_prog("L2a", build_L2a)
    in_maps = host_prep_L2a(l1_results)
    res = run_bass_kernel_spmd(nc, in_maps, core_ids=list(range(NC)))
    return res.results


def build_L2b():
    nc = bass.Bass("TRN2", target_bir_lowering=False)
    OT_in = nc.dram_tensor("OT", [D, TOKC], BF16, kind="ExternalInput").ap()
    xn_in = nc.dram_tensor("xn", [TOKC, D], F32, kind="ExternalInput").ap()
    wo1t = nc.dram_tensor("wo1t", [8 * 128, 8 * 128], F32, kind="ExternalInput").ap()
    wgt = nc.dram_tensor("wgt", [32 * 128, 8 * 128], F32, kind="ExternalInput").ap()
    wut = nc.dram_tensor("wut", [32 * 128, 8 * 128], F32, kind="ExternalInput").ap()
    wdt = nc.dram_tensor("wdt", [8 * 128, 32 * 128], F32, kind="ExternalInput").ap()
    prwt = nc.dram_tensor("prwt", [16 * 128, 8 * 128], F32, kind="ExternalInput").ap()
    ln2 = nc.dram_tensor("ln2", [1, D], F32, kind="ExternalInput").ap()
    pnw = nc.dram_tensor("pnw", [1, D], F32, kind="ExternalInput").ap()
    identc = nc.dram_tensor("identc", [128, 128], F32, kind="ExternalInput").ap()
    constv = nc.dram_tensor("constv", [1, 8], F32, kind="ExternalInput").ap()

    proc_out = nc.dram_tensor("proc", [TOKC, D], F32, kind="ExternalOutput").ap()
    dst_out = nc.dram_tensor("dstch", [TOKC, 2], F32, kind="ExternalOutput").ap()

    wob = nc.dram_tensor("wob", [8 * 128, 8 * 128], BF16).ap()
    wgb = nc.dram_tensor("wgb", [32 * 128, 8 * 128], BF16).ap()
    wub = nc.dram_tensor("wub", [32 * 128, 8 * 128], BF16).ap()
    wdb = nc.dram_tensor("wdb", [8 * 128, 32 * 128], BF16).ap()
    prb = nc.dram_tensor("prb", [16 * 128, 8 * 128], BF16).ap()
    mlv = nc.dram_tensor("mlv", [TOKC, 2 * D], BF16).ap()

    from contextlib import ExitStack
    with ExitStack() as ctx:
        E_ = ctx.enter_context
        big1 = E_(nc.sbuf_tensor("big1", [128, 8, TOKC], BF16))
        big2 = E_(nc.sbuf_tensor("big2", [128, 8, TOKC], BF16))
        gub = E_(nc.sbuf_tensor("gub", [128, 32, 512], BF16))
        mlvb = E_(nc.sbuf_tensor("mlvb", [128, 4, 2048], BF16))
        x1b = E_(nc.sbuf_tensor("x1b", [128, 4, D], F32))
        xnb4 = E_(nc.sbuf_tensor("xnb4", [128, 4, D], F32))
        wt = E_(nc.sbuf_tensor("wt2", [128, 2, 8, 128], BF16))
        wtd = E_(nc.sbuf_tensor("wtd", [128, 2, 32, 128], BF16))
        ob = E_(nc.sbuf_tensor("ob3", [128, 6, 512], BF16))
        hb2 = E_(nc.sbuf_tensor("hb2", [128, 2, D], BF16))
        resb = E_(nc.sbuf_tensor("resb", [128, D], F32))
        eb = E_(nc.sbuf_tensor("eb", [128, D], F32))
        sqb = E_(nc.sbuf_tensor("sqb", [128, D], F32))
        lnb2 = E_(nc.sbuf_tensor("lnb2", [128, D], F32))
        pnb = E_(nc.sbuf_tensor("pnb", [128, D], F32))
        cvb = E_(nc.sbuf_tensor("cvb3", [128, 8], F32))
        ident = E_(nc.sbuf_tensor("ident2", [128, 128], BF16))
        stat = E_(nc.sbuf_tensor("stat2", [128, 16, 4], F32))
        dacc = E_(nc.sbuf_tensor("dacc", [128, 16, 2], F32))
        scr = E_(nc.sbuf_tensor("scr2", [128, D], BF16))
        ps = E_(nc.psum_tensor("ps2", [128, 4, 512], F32))
        pst = E_(nc.psum_tensor("pst2", [128, 2, 1024], BF16))
        _sd = E_(nc.semaphore("sem_d"))
        _st_ = E_(nc.semaphore("sem_t"))
        _sv = E_(nc.semaphore("sem_v"))
        _sa = E_(nc.semaphore("sem_a"))
        _sg = E_(nc.semaphore("sem_g"))
        _so = E_(nc.semaphore("sem_o"))
        sem_pool = [Sem(E_(nc.semaphore("sp%d" % _i))) for _i in range(24)]
        blk = E_(nc.Block())

        P = Prog(nc, sem_pool)
        sd, st, sv, sa, sg, so = (Sem(_sd), Sem(_st_), Sem(_sv), Sem(_sa),
                                  Sem(_sg), Sem(_so))

        R_big1 = Res()
        R_big2 = Res()
        R_gub = [Res() for _ in range(32)]
        R_mlvb = [Res() for _ in range(4)]
        R_x1 = [Res() for _ in range(4)]
        R_xn4 = [Res() for _ in range(4)]
        R_wt = [Res() for _ in range(2)]
        R_wtd = [Res() for _ in range(2)]
        R_ob = [Res() for _ in range(6)]
        R_hb2 = [Res() for _ in range(2)]
        R_res = Res()
        R_eb = Res()
        R_sqb = Res()
        R_ln2 = Res()
        R_pnb = Res()
        R_cv = Res()
        R_ident = Res()
        R_stat = [Res() for _ in range(16)]
        R_dacc = Res()
        R_scr = Res()
        R_ps = [Res() for _ in range(4)]
        R_pst = [Res() for _ in range(2)]
        R_cast = Res()

        P.dma("gpsimd", ident[:, :], identc[:, :], sg, writes=[R_ident])
        for srcap, dstap in ((wo1t, wob), (wgt, wgb), (wut, wub), (wdt, wdb),
                             (prwt, prb)):
            P.dma("gpsimd", dstap[:, :], srcap[:, :], sg, writes=[R_cast])
        P.dma("sync", lnb2[:, :], ln2.to_broadcast([128, D]), sd, writes=[R_ln2])
        P.dma("sync", pnb[:, :], pnw.to_broadcast([128, D]), sd, writes=[R_pnb])
        P.dma("sync", cvb[:, :], constv.to_broadcast([128, 8]), sd, writes=[R_cv])
        for c in range(8):
            P.dma("sync", big1[:, c, :], OT_in[128 * c:128 * (c + 1), :], sd,
                  writes=[R_big1])

        def transpose128(src_ap, dst_ap, reads, writes, pb):
            P.run("tensor", lambda E, s=src_ap, pb=pb: E.transpose(
                pst[:, pb, 0:128], s, ident[:, :]),
                st, reads=list(reads) + [R_ident], writes=[R_pst[pb]])
            P.run("vector", lambda E, d=dst_ap, pb=pb: E.tensor_copy(d, pst[:, pb, 0:128]),
                  sv, reads=[R_pst[pb]], writes=list(writes))

        tpb = 0
        for ti in range(16):
            sl = ti % 4
            P.dma("sync", xnb4[:, sl, :], xn_in[128 * ti:128 * (ti + 1), :], sd,
                  writes=[R_xn4[sl]])
            P.run("scalar", lambda E, sl=sl, ti=ti: E.activation(
                scr[:, :], xnb4[:, sl, :], AF.Square, accum_out=stat[:, ti, 0:1]),
                sa, reads=[R_xn4[sl]], writes=[R_stat[ti], R_scr])
            P.run("scalar", lambda E, ti=ti: E.activation(
                stat[:, ti, 0:1], stat[:, ti, 0:1], AF.Ln, bias=cvb[:, 1:2],
                scale=float(1.0 / D)),
                sa, reads=[R_stat[ti], R_cv], writes=[R_stat[ti]])
            P.run("scalar", lambda E, ti=ti: E.activation(
                stat[:, ti, 0:1], stat[:, ti, 0:1], AF.Exp, scale=-0.5),
                sa, reads=[R_stat[ti]], writes=[R_stat[ti]])
            hsl = ti % 2
            P.run("vector", lambda E, sl=sl, ti=ti, hsl=hsl: E.scalar_tensor_tensor(
                hb2[:, hsl, :], xnb4[:, sl, :], stat[:, ti, 0:1], pnb[:, :],
                op0=ALU.mult, op1=ALU.mult),
                sv, reads=[R_xn4[sl], R_stat[ti], R_pnb], writes=[R_hb2[hsl]])
            for c in range(8):
                transpose128(hb2[:, hsl, 128 * c:128 * (c + 1)],
                             big2[:, c, 128 * ti:128 * (ti + 1)],
                             [R_hb2[hsl]], [R_big2], tpb % 2)
                tpb += 1
        for ttok in range(4):
            for oc in range(16):
                wb = oc % 2
                P.dma("sync", wt[:, wb, :, :],
                      prb[128 * oc:128 * (oc + 1), :].rearrange("p (c j) -> p c j", c=8),
                      sd, writes=[R_wt[wb]])
                pb = oc % 2
                P.rd("tensor", [R_wt[wb], R_big2])
                P.wr("tensor", [R_ps[pb]])
                for c in range(8):
                    f = (lambda E, c=c, oc=oc, ttok=ttok, wb=wb, pb=pb:
                         E.matmul(ps[:, pb, :], wt[:, wb, c, :],
                                  big2[:, c, 512 * ttok:512 * (ttok + 1)],
                                  start=(c == 0), stop=(c == 7)))
                    if c == 7:
                        st.n += 1
                        P.ops["tensor"].append(lambda E, f=f: f(E).then_inc(_st_, 1))
                        P.done_write([R_ps[pb]], st, st.n)
                        P.done_read([R_wt[wb], R_big2], st, st.n)
                    else:
                        P.ops["tensor"].append(lambda E, f=f: f(E))
                slot = oc % 4
                P.run("scalar", lambda E, pb=pb, slot=slot: E.activation(
                    ob[:, slot, :], ps[:, pb, :], AF.Identity),
                    sa, reads=[R_ps[pb]], writes=[R_ob[slot]])
                for sub in range(4):
                    transpose128(ob[:, slot, 128 * sub:128 * (sub + 1)],
                                 mlvb[:, sub, 128 * oc:128 * (oc + 1)],
                                 [R_ob[slot]], [R_mlvb[sub]], tpb % 2)
                    tpb += 1
            for sub in range(4):
                ti = 4 * ttok + sub
                P.dma("gpsimd", mlv[128 * ti:128 * (ti + 1), :], mlvb[:, sub, :],
                      sg, reads=[R_mlvb[sub]])

        for ttok in range(4):
            for sub in range(4):
                ti = 4 * ttok + sub
                sl = sub
                P.dma("sync", xnb4[:, sl, :], xn_in[128 * ti:128 * (ti + 1), :],
                      sd, writes=[R_xn4[sl]])
            for oc in range(8):
                wb = oc % 2
                P.dma("sync", wt[:, wb, :, :],
                      wob[128 * oc:128 * (oc + 1), :].rearrange("p (c j) -> p c j", c=8),
                      sd, writes=[R_wt[wb]])
                pb = oc % 2
                P.rd("tensor", [R_wt[wb], R_big1])
                P.wr("tensor", [R_ps[pb]])
                for c in range(8):
                    f = (lambda E, c=c, wb=wb, pb=pb, ttok=ttok:
                         E.matmul(ps[:, pb, :], wt[:, wb, c, :],
                                  big1[:, c, 512 * ttok:512 * (ttok + 1)],
                                  start=(c == 0), stop=(c == 7)))
                    if c == 7:
                        st.n += 1
                        P.ops["tensor"].append(lambda E, f=f: f(E).then_inc(_st_, 1))
                        P.done_write([R_ps[pb]], st, st.n)
                        P.done_read([R_wt[wb], R_big1], st, st.n)
                    else:
                        P.ops["tensor"].append(lambda E, f=f: f(E))
                slot = 4 + (oc % 2)
                P.run("scalar", lambda E, pb=pb, slot=slot: E.activation(
                    ob[:, slot, :], ps[:, pb, :], AF.Identity),
                    sa, reads=[R_ps[pb]], writes=[R_ob[slot]])
                for sub in range(4):
                    pb2 = tpb % 2
                    P.run("tensor", lambda E, slot=slot, sub=sub, pb2=pb2: E.transpose(
                        pst[:, pb2, 0:128], ob[:, slot, 128 * sub:128 * (sub + 1)],
                        ident[:, :]),
                        st, reads=[R_ob[slot], R_ident], writes=[R_pst[pb2]])
                    P.run("vector", lambda E, sub=sub, oc=oc, pb2=pb2: E.tensor_tensor(
                        x1b[:, sub, 128 * oc:128 * (oc + 1)],
                        xnb4[:, sub, 128 * oc:128 * (oc + 1)],
                        pst[:, pb2, 0:128], op=ALU.add),
                        sv, reads=[R_pst[pb2], R_xn4[sub]], writes=[R_x1[sub]])
                    tpb += 1
            for sub in range(4):
                ti = 4 * ttok + sub
                P.run("scalar", lambda E, sub=sub, ti=ti: E.activation(
                    scr[:, :], x1b[:, sub, :], AF.Square, accum_out=stat[:, ti, 1:2]),
                    sa, reads=[R_x1[sub]], writes=[R_stat[ti], R_scr])
                P.run("scalar", lambda E, ti=ti: E.activation(
                    stat[:, ti, 1:2], stat[:, ti, 1:2], AF.Ln, bias=cvb[:, 1:2],
                    scale=float(1.0 / D)),
                    sa, reads=[R_stat[ti], R_cv], writes=[R_stat[ti]])
                P.run("scalar", lambda E, ti=ti: E.activation(
                    stat[:, ti, 1:2], stat[:, ti, 1:2], AF.Exp, scale=-0.5),
                    sa, reads=[R_stat[ti]], writes=[R_stat[ti]])
                hsl = sub % 2
                P.run("vector", lambda E, sub=sub, ti=ti, hsl=hsl: E.scalar_tensor_tensor(
                    hb2[:, hsl, :], x1b[:, sub, :], stat[:, ti, 1:2], lnb2[:, :],
                    op0=ALU.mult, op1=ALU.mult),
                    sv, reads=[R_x1[sub], R_stat[ti], R_ln2], writes=[R_hb2[hsl]])
                for c in range(8):
                    transpose128(hb2[:, hsl, 128 * c:128 * (c + 1)],
                                 big2[:, c, 128 * (4 * ttok + sub):128 * (4 * ttok + sub + 1)],
                                 [R_hb2[hsl]], [R_big2], tpb % 2)
                    tpb += 1
            for nch in range(32):
                wb = nch % 2
                P.dma("sync", wt[:, wb, :, :],
                      wgb[128 * nch:128 * (nch + 1), :].rearrange("p (c j) -> p c j", c=8),
                      sd, writes=[R_wt[wb]])
                P.rd("tensor", [R_wt[wb], R_big2])
                P.wr("tensor", [R_ps[2]])
                for c in range(8):
                    f = (lambda E, c=c, wb=wb, ttok=ttok:
                         E.matmul(ps[:, 2, :], wt[:, wb, c, :],
                                  big2[:, c, 512 * ttok:512 * (ttok + 1)],
                                  start=(c == 0), stop=(c == 7)))
                    if c == 7:
                        st.n += 1
                        P.ops["tensor"].append(lambda E, f=f: f(E).then_inc(_st_, 1))
                        P.done_write([R_ps[2]], st, st.n)
                        P.done_read([R_wt[wb], R_big2], st, st.n)
                    else:
                        P.ops["tensor"].append(lambda E, f=f: f(E))
                wb2 = nch % 2
                P.dma("gpsimd", wtd[:, wb2, 0:8, :],
                      wub[128 * nch:128 * (nch + 1), :].rearrange("p (c j) -> p c j", c=8),
                      sg, writes=[R_wtd[wb2]])
                P.rd("tensor", [R_wtd[wb2], R_big2])
                P.wr("tensor", [R_ps[3]])
                for c in range(8):
                    f = (lambda E, c=c, wb2=wb2, ttok=ttok:
                         E.matmul(ps[:, 3, :], wtd[:, wb2, c, :],
                                  big2[:, c, 512 * ttok:512 * (ttok + 1)],
                                  start=(c == 0), stop=(c == 7)))
                    if c == 7:
                        st.n += 1
                        P.ops["tensor"].append(lambda E, f=f: f(E).then_inc(_st_, 1))
                        P.done_write([R_ps[3]], st, st.n)
                        P.done_read([R_wtd[wb2], R_big2], st, st.n)
                    else:
                        P.ops["tensor"].append(lambda E, f=f: f(E))
                P.run("scalar", lambda E: E.activation(
                    ob[:, 0, :], ps[:, 2, :], AF.Silu),
                    sa, reads=[R_ps[2]], writes=[R_ob[0]])
                P.run("vector", lambda E, nch=nch: E.tensor_tensor(
                    gub[:, nch, :], ob[:, 0, :], ps[:, 3, :], op=ALU.mult),
                    sv, reads=[R_ob[0], R_ps[3]], writes=[R_gub[nch]])
            for oc in range(8):
                wb = oc % 2
                P.dma("sync", wtd[:, wb, :, :],
                      wdb[128 * oc:128 * (oc + 1), :].rearrange("p (n j) -> p n j", n=32),
                      sd, writes=[R_wtd[wb]])
                pb = oc % 2
                P.rd("tensor", [R_wtd[wb]] + R_gub)
                P.wr("tensor", [R_ps[pb]])
                for n in range(32):
                    f = (lambda E, n=n, wb=wb, pb=pb:
                         E.matmul(ps[:, pb, :], wtd[:, wb, n, :], gub[:, n, :],
                                  start=(n == 0), stop=(n == 31)))
                    if n == 31:
                        st.n += 1
                        P.ops["tensor"].append(lambda E, f=f: f(E).then_inc(_st_, 1))
                        P.done_write([R_ps[pb]], st, st.n)
                        P.done_read([R_wtd[wb]] + R_gub, st, st.n)
                    else:
                        P.ops["tensor"].append(lambda E, f=f: f(E))
                slot = 4 + (oc % 2)
                P.run("scalar", lambda E, pb=pb, slot=slot: E.activation(
                    ob[:, slot, :], ps[:, pb, :], AF.Identity),
                    sa, reads=[R_ps[pb]], writes=[R_ob[slot]])
                for sub in range(4):
                    pb2 = tpb % 2
                    P.run("tensor", lambda E, slot=slot, sub=sub, pb2=pb2: E.transpose(
                        pst[:, pb2, 0:128], ob[:, slot, 128 * sub:128 * (sub + 1)],
                        ident[:, :]),
                        st, reads=[R_ob[slot], R_ident], writes=[R_pst[pb2]])
                    P.run("vector", lambda E, sub=sub, oc=oc, pb2=pb2: E.tensor_tensor(
                        x1b[:, sub, 128 * oc:128 * (oc + 1)],
                        x1b[:, sub, 128 * oc:128 * (oc + 1)],
                        pst[:, pb2, 0:128], op=ALU.add),
                        sv, reads=[R_pst[pb2]], writes=[R_x1[sub]])
                    tpb += 1
            for sub in range(4):
                ti = 4 * ttok + sub
                P.dma("gpsimd", proc_out[128 * ti:128 * (ti + 1), :], x1b[:, sub, :],
                      so, reads=[R_x1[sub]])
                P.dma("sync", mlvb[:, 0, :], mlv[128 * ti:128 * (ti + 1), :], sd,
                      writes=[R_mlvb[0]])
                P.run("vector", lambda E, sub=sub: E.tensor_tensor(
                    resb[:, :], x1b[:, sub, :], xnb4[:, sub, :], op=ALU.subtract),
                    sv, reads=[R_x1[sub], R_xn4[sub]], writes=[R_res])
                P.run("scalar", lambda E, ti=ti: E.activation(
                    scr[:, :], resb[:, :], AF.Square, accum_out=dacc[:, ti, 0:1]),
                    sa, reads=[R_res], writes=[R_dacc, R_scr])
                P.run("scalar", lambda E: E.activation(
                    eb[:, :], mlvb[:, 0, 1024:2048], AF.Exp, scale=-1.0),
                    sa, reads=[R_mlvb[0]], writes=[R_eb])
                P.run("vector", lambda E: E.tensor_tensor(
                    sqb[:, :], resb[:, :], mlvb[:, 0, 0:1024], op=ALU.subtract),
                    sv, reads=[R_res, R_mlvb[0]], writes=[R_sqb])
                P.run("scalar", lambda E: E.activation(
                    sqb[:, :], sqb[:, :], AF.Square),
                    sa, reads=[R_sqb], writes=[R_sqb])
                P.run("vector", lambda E: E.tensor_tensor(
                    sqb[:, :], sqb[:, :], eb[:, :], op=ALU.mult),
                    sv, reads=[R_sqb, R_eb], writes=[R_sqb])
                P.run("vector", lambda E: E.tensor_tensor(
                    sqb[:, :], sqb[:, :], eb[:, :], op=ALU.add),
                    sv, reads=[R_sqb, R_eb], writes=[R_sqb])
                P.run("vector", lambda E, ti=ti: E.scalar_tensor_tensor(
                    scr[:, :], sqb[:, :], -1.0, mlvb[:, 0, 1024:2048],
                    op0=ALU.add, op1=ALU.add, accum_out=dacc[:, ti, 1:2]),
                    sv, reads=[R_sqb, R_mlvb[0]], writes=[R_scr, R_dacc])
        P.run("vector", lambda E: E.tensor_scalar(
            dacc[:, :, 0:1].rearrange("p t c -> p (t c)"),
            dacc[:, :, 0:1].rearrange("p t c -> p (t c)"),
            float(1.0 / D), None, op0=ALU.mult),
            sv, reads=[R_dacc], writes=[R_dacc])
        P.run("vector", lambda E: E.tensor_scalar(
            dacc[:, :, 1:2].rearrange("p t c -> p (t c)"),
            dacc[:, :, 1:2].rearrange("p t c -> p (t c)"),
            float(0.5 / D), None, op0=ALU.mult),
            sv, reads=[R_dacc], writes=[R_dacc])
        P.dma("gpsimd", dst_out.rearrange("(t p) c -> p t c", p=128),
              dacc[:, :, :], so, reads=[R_dacc])
        P.finish("gpsimd")
        P.emit(blk)
    return nc


def _tile_w(w, KCH, OCH):
    w = np.ascontiguousarray(w, dtype=np.float32)
    K, O = w.shape
    assert K == KCH * 128 and O == OCH * 128
    t = w.reshape(KCH, 128, OCH, 128).transpose(2, 1, 0, 3)
    return np.ascontiguousarray(t.reshape(OCH * 128, KCH * 128))


def host_prep_L2b(inputs, l1_results, l2a_results):
    OTs = [np.zeros((D, TOKC), BF16NP) for _ in range(NC)]
    for p in range(NC):
        meta = l2a_iter_meta(p)
        oat = np.asarray(l2a_results[p]["oat"])
        for i, (a, b, h) in enumerate(meta):
            for gs, g in enumerate((a, NG - 1 - a)):
                owner = g if g < 8 else NG - 1 - g
                slot = 0 if g == owner else 1
                OTs[owner][64 * h:64 * h + 64,
                           b * TOKB + slot * GR:b * TOKB + (slot + 1) * GR] = oat[i, gs]
    wo1t = _tile_w(inputs["L0_ffn_wo1"], 8, 8)
    wgt = _tile_w(inputs["L0_ffn_wg1"], 8, 32)
    wut = _tile_w(inputs["L0_ffn_wu1"], 8, 32)
    wdt = _tile_w(inputs["L0_ffn_wd1"], 32, 8)
    prwt = _tile_w(inputs["L0_ffn_prior_w"], 8, 16)
    ln2 = _f32(inputs["L0_ffn_ln2_1"]).reshape(1, D)
    pnw = _f32(inputs["L0_ffn_prior_norm_w"]).reshape(1, D)
    ident = np.eye(128, dtype=np.float32)
    consts = np.array([[1e-5, 1e-6, 0, 0, 0, 0, 0, 0]], np.float32)
    in_maps = []
    for p in range(NC):
        in_maps.append(dict(OT=OTs[p], xn=np.asarray(l1_results[p]["xn"]),
                            wo1t=wo1t, wgt=wgt, wut=wut, wdt=wdt, prwt=prwt,
                            ln2=ln2, pnw=pnw, identc=ident, constv=consts))
    return in_maps


def run_L2b(inputs, l1_results, l2a_results):
    nc = _get_prog("L2b", build_L2b)
    in_maps = host_prep_L2b(inputs, l1_results, l2a_results)
    res = run_bass_kernel_spmd(nc, in_maps, core_ids=list(range(NC)))
    return res.results


NBIS = 48


def build_L3():
    nc = bass.Bass("TRN2", target_bir_lowering=False)
    dsti = nc.dram_tensor("dst", [128, 128], F32, kind="ExternalInput").ap()
    dchi = nc.dram_tensor("dch", [128, 128], F32, kind="ExternalInput").ap()
    procfull = nc.dram_tensor("procfull", [B * N, D], F32, kind="ExternalInput").ap()
    cstabi = nc.dram_tensor("cstab", [N, 64], F32, kind="ExternalInput").ap()
    w_q2 = nc.dram_tensor("wq2", [D, D], F32, kind="ExternalInput").ap()
    w_k2 = nc.dram_tensor("wk2", [D, D], F32, kind="ExternalInput").ap()
    w_v2 = nc.dram_tensor("wv2", [D, D], F32, kind="ExternalInput").ap()
    ln12 = nc.dram_tensor("ln12", [1, D], F32, kind="ExternalInput").ap()
    identc = nc.dram_tensor("identc", [128, 128], F32, kind="ExternalInput").ap()
    constv = nc.dram_tensor("constv", [1, 8], F32, kind="ExternalInput").ap()
    bselc = nc.dram_tensor("bselc", [128, 2], F32, kind="ExternalInput").ap()
    bselTc = nc.dram_tensor("bselTc", [2, 128], F32, kind="ExternalInput").ap()
    lprefc = nc.dram_tensor("lprefc", [128, 128], F32, kind="ExternalInput").ap()
    iotaFc = nc.dram_tensor("iotaFc", [128, 128], F32, kind="ExternalInput").ap()
    fcolc = nc.dram_tensor("fcolc", [128, 1], F32, kind="ExternalInput").ap()
    ccolc = nc.dram_tensor("ccolc", [128, 64], F32, kind="ExternalInput").ap()
    repcc = nc.dram_tensor("repcc", [32, 128], F32, kind="ExternalInput").ap()
    rkbasec = nc.dram_tensor("rkbase", [1, 8], F32, kind="ExternalInput").ap()

    q2T_out = nc.dram_tensor("q2T", [D, 512], BF16, kind="ExternalOutput").ap()
    k2T_out = nc.dram_tensor("k2T", [D, 512], BF16, kind="ExternalOutput").ap()
    v2T_out = nc.dram_tensor("v2T", [D, 512], BF16, kind="ExternalOutput").ap()
    sel_out = nc.dram_tensor("selo", [512, D], F32, kind="ExternalOutput").ap()
    sc_out = nc.dram_tensor("sco", [128, 8], F32, kind="ExternalOutput").ap()
    rkm_out = nc.dram_tensor("rkm", [128, 256], F32, kind="ExternalOutput").ap()

    from contextlib import ExitStack
    with ExitStack() as ctx:
        E_ = ctx.enter_context
        g2 = E_(nc.sbuf_tensor("g2", [128, 128], F32))
        gT = E_(nc.sbuf_tensor("gT", [128, 128], BF16))
        rkT = E_(nc.sbuf_tensor("rkT", [128, 128], F32))
        rkTs = E_(nc.sbuf_tensor("rkTs", [128, 128], F32))
        smT = E_(nc.sbuf_tensor("smT", [128, 128], F32))
        work = E_(nc.sbuf_tensor("work", [128, 12, 128], F32))
        zer = E_(nc.sbuf_tensor("zer", [128, 128], F32))
        sc1 = E_(nc.sbuf_tensor("sc1", [128, 16], F32))
        onr = E_(nc.sbuf_tensor("onr", [1, 128], F32))
        bsel = E_(nc.sbuf_tensor("bsel", [128, 2], F32))
        bselT = E_(nc.sbuf_tensor("bselT", [2, 128], F32))
        lpref = E_(nc.sbuf_tensor("lpref", [128, 128], F32))
        iotaF = E_(nc.sbuf_tensor("iotaF", [128, 128], F32))
        fcol = E_(nc.sbuf_tensor("fcol", [128, 1], BF16))
        ccols = E_(nc.sbuf_tensor("ccols", [128, 64], BF16))
        repcb = E_(nc.sbuf_tensor("repcb", [32, 128], BF16))
        ohb = E_(nc.sbuf_tensor("ohb", [128, 2, 128], BF16))
        idxf = E_(nc.sbuf_tensor("idxf", [128, 8], F32))
        idxu = E_(nc.sbuf_tensor("idxu", [128, 2, 4], U32))
        selb = E_(nc.sbuf_tensor("selb", [128, 4, D], F32))
        csg = E_(nc.sbuf_tensor("csg", [128, 4, 64], F32))
        csrC = E_(nc.sbuf_tensor("csrC", [32, 512], BF16))
        csrS = E_(nc.sbuf_tensor("csrS", [32, 512], BF16))
        cs2 = E_(nc.sbuf_tensor("cs2", [128, 2, 512], F32))
        hb3 = E_(nc.sbuf_tensor("hb3", [128, 4, D], BF16))
        h2T = E_(nc.sbuf_tensor("h2T3", [128, 8, 512], BF16))
        wt = E_(nc.sbuf_tensor("wt3", [128, 8, 2, 128], BF16))
        ob = E_(nc.sbuf_tensor("ob4", [128, 8, 512], BF16))
        lnb = E_(nc.sbuf_tensor("lnb3", [128, D], F32))
        cvb = E_(nc.sbuf_tensor("cvb4", [128, 8], F32))
        rkb = E_(nc.sbuf_tensor("rkb", [128, 8], F32))
        ident = E_(nc.sbuf_tensor("ident3", [128, 128], BF16))
        identF = E_(nc.sbuf_tensor("identF", [128, 128], F32))
        stat = E_(nc.sbuf_tensor("stat3", [128, 4, 4], F32))
        scr = E_(nc.sbuf_tensor("scr3", [128, D], BF16))
        ps = E_(nc.psum_tensor("ps3", [128, 4, 512], F32))
        psF = E_(nc.psum_tensor("psF3", [128, 128], F32))
        pst = E_(nc.psum_tensor("pst3", [128, 2, 1024], BF16))
        _sd = E_(nc.semaphore("sem_d"))
        _st_ = E_(nc.semaphore("sem_t"))
        _sv = E_(nc.semaphore("sem_v"))
        _sa = E_(nc.semaphore("sem_a"))
        _sg = E_(nc.semaphore("sem_g"))
        _so = E_(nc.semaphore("sem_o"))
        sem_pool = [Sem(E_(nc.semaphore("sp%d" % _i))) for _i in range(24)]
        blk = E_(nc.Block())

        P = Prog(nc, sem_pool)
        sd, st, sv, sa, sg, so = (Sem(_sd), Sem(_st_), Sem(_sv), Sem(_sa),
                                  Sem(_sg), Sem(_so))

        Rs = lambda n: [Res() for _ in range(n)]
        R_g2, R_gT, R_rkT, R_rkTs, R_smT = Res(), Res(), Res(), Res(), Res()
        R_work = Rs(12)
        R_sc1, R_const, R_zer = Res(), Res(), Res()
        R_ohb = Rs(2)
        R_idxf, R_idxu = Res(), Res()
        R_selb = Rs(4)
        R_csg, R_csr, R_cs2 = Res(), Res(), Res()
        R_hb3 = Rs(4)
        R_h2T = Res()
        R_wt = Rs(2)
        R_ob = Rs(8)
        R_stat = Rs(4)
        R_scr = Res()
        R_ps = Rs(4)
        R_psF = Res()
        R_pst = Rs(2)
        R_ident, R_identF = Res(), Res()

        P.dma("gpsimd", ident[:, :], identc[:, :], sg, writes=[R_ident])
        P.dma("gpsimd", fcol[:, :], fcolc[:, :], sg, writes=[R_const])
        P.dma("gpsimd", ccols[:, :], ccolc[:, :], sg, writes=[R_const])
        P.dma("gpsimd", repcb[:, :], repcc[:, :], sg, writes=[R_const])
        P.dma("sync", identF[:, :], identc[:, :], sd, writes=[R_identF])
        P.dma("sync", bsel[:, :], bselc[:, :], sd, writes=[R_const])
        P.dma("sync", bselT[:, :], bselTc[:, :], sd, writes=[R_const])
        P.dma("sync", lpref[:, :], lprefc[:, :], sd, writes=[R_const])
        P.dma("sync", iotaF[:, :], iotaFc[:, :], sd, writes=[R_const])
        P.dma("sync", cvb[:, :], constv.to_broadcast([128, 8]), sd, writes=[R_const])
        P.dma("sync", rkb[:, :], rkbasec.to_broadcast([128, 8]), sd, writes=[R_const])
        P.dma("sync", lnb[:, :], ln12.to_broadcast([128, D]), sd, writes=[R_const])
        P.dma("sync", work[:, 0, :], dsti[:, :], sd, writes=[R_work[0]])
        P.dma("sync", work[:, 1, :], dchi[:, :], sd, writes=[R_work[1]])
        P.run("vector", lambda E: E.memset(zer[:, :], 0.0), sv, writes=[R_zer])
        R_onr = Res()
        P.run("vector", lambda E: E.memset(onr[:, :], 1.0), sv, writes=[R_onr])

        P.run("vector", lambda E: E.tensor_scalar(
            work[:, 2, :], work[:, 0, :], 0.0, None, op0=ALU.add, op1=ALU.add,
            accum_out=sc1[:, 0:1]),
            sv, reads=[R_work[0]], writes=[R_work[2], R_sc1])
        P.run("vector", lambda E: E.tensor_scalar(
            work[:, 2, :], work[:, 1, :], 0.0, None, op0=ALU.add, op1=ALU.add,
            accum_out=sc1[:, 1:2]),
            sv, reads=[R_work[1]], writes=[R_work[2], R_sc1])
        P.run("vector", lambda E: E.tensor_tensor(
            sc1[:, 2:3], sc1[:, 0:1], sc1[:, 1:2], op=ALU.add),
            sv, reads=[R_sc1], writes=[R_sc1])
        P.run("vector", lambda E: E.tensor_tensor(
            sc1[:, 8:9], bsel[:, 0:1], bsel[:, 1:2], op=ALU.add),
            sv, reads=[R_const], writes=[R_sc1])
        P.run("tensor", lambda E: E.matmul(
            psF[0:1, 0:1], sc1[:, 2:3], sc1[:, 8:9], start=True, stop=True),
            st, reads=[R_sc1], writes=[R_psF])
        P.run("vector", lambda E: E.tensor_scalar(
            sc1[0:1, 3:4], psF[0:1, 0:1], float(-1.0 / (B * N)), None, op0=ALU.mult),
            sv, reads=[R_psF], writes=[R_sc1])
        P.run("tensor", lambda E: E.matmul(
            psF[:, 0:1], onr[:, :], sc1[0:1, 3:4], start=True, stop=True),
            st, reads=[R_sc1, R_onr], writes=[R_psF])
        P.run("vector", lambda E: E.tensor_copy(sc1[:, 4:5], psF[:, 0:1]),
              sv, reads=[R_psF], writes=[R_sc1])
        P.run("vector", lambda E: E.tensor_tensor(
            g2[:, :], work[:, 0, :], work[:, 1, :], op=ALU.add),
            sv, reads=[R_work[0], R_work[1]], writes=[R_g2])
        P.run("scalar", lambda E: E.activation(
            g2[:, :], g2[:, :], AF.Sigmoid, bias=sc1[:, 4:5]),
            sa, reads=[R_g2, R_sc1], writes=[R_g2])

        P.run("vector", lambda E: E.memset(work[:, 2, 0:1], -1.0), sv,
              writes=[R_work[2]])
        P.run("vector", lambda E: E.memset(work[:, 3, 0:1], 2.0), sv,
              writes=[R_work[3]])

        def count_gt(thr_col, Rthr):
            P.run("vector", lambda E, t=thr_col: E.tensor_scalar(
                work[:, 5, :], g2[:, :], t, None, op0=ALU.is_gt,
                op1=ALU.add, accum_out=sc1[:, 5:6]),
                sv, reads=[R_g2, Rthr], writes=[R_work[5], R_sc1])
            P.run("tensor", lambda E: E.matmul(
                psF[0:2, 0:1], bsel[:, :], sc1[:, 5:6], start=True, stop=True),
                st, reads=[R_sc1, R_const], writes=[R_psF])
            P.run("vector", lambda E: E.tensor_copy(sc1[0:2, 6:7], psF[0:2, 0:1]),
                  sv, reads=[R_psF], writes=[R_sc1])
            P.run("tensor", lambda E: E.matmul(
                psF[:, 0:1], bselT[:, :], sc1[0:2, 6:7], start=True, stop=True),
                st, reads=[R_sc1, R_const], writes=[R_psF])
            P.run("vector", lambda E: E.tensor_copy(sc1[:, 7:8], psF[:, 0:1]),
                  sv, reads=[R_psF], writes=[R_sc1])

        for it in range(NBIS):
            P.run("vector", lambda E: E.tensor_tensor(
                work[:, 4, 0:1], work[:, 2, 0:1], work[:, 3, 0:1], op=ALU.add),
                sv, reads=[R_work[2], R_work[3]], writes=[R_work[4]])
            P.run("vector", lambda E: E.tensor_scalar(
                work[:, 4, 0:1], work[:, 4, 0:1], 0.5, None, op0=ALU.mult),
                sv, reads=[R_work[4]], writes=[R_work[4]])
            count_gt(work[:, 4, 0:1], R_work[4])
            P.run("vector", lambda E: E.tensor_scalar(
                work[:, 6, 0:1], sc1[:, 7:8], float(KSEL), None, op0=ALU.is_ge),
                sv, reads=[R_sc1], writes=[R_work[6]])
            P.run("vector", lambda E: E.tensor_tensor(
                work[:, 7, 0:1], work[:, 4, 0:1], work[:, 2, 0:1], op=ALU.subtract),
                sv, reads=[R_work[4], R_work[2]], writes=[R_work[7]])
            P.run("vector", lambda E: E.scalar_tensor_tensor(
                work[:, 2, 0:1], work[:, 7, 0:1], work[:, 6, 0:1], work[:, 2, 0:1],
                op0=ALU.mult, op1=ALU.add),
                sv, reads=[R_work[7], R_work[6], R_work[2]], writes=[R_work[2]])
            P.run("vector", lambda E: E.tensor_tensor(
                work[:, 7, 0:1], work[:, 4, 0:1], work[:, 3, 0:1], op=ALU.subtract),
                sv, reads=[R_work[4], R_work[3]], writes=[R_work[7]])
            P.run("vector", lambda E: E.tensor_scalar(
                work[:, 8, 0:1], work[:, 6, 0:1], float(-1.0), float(1.0),
                op0=ALU.mult, op1=ALU.add),
                sv, reads=[R_work[6]], writes=[R_work[8]])
            P.run("vector", lambda E: E.scalar_tensor_tensor(
                work[:, 3, 0:1], work[:, 7, 0:1], work[:, 8, 0:1], work[:, 3, 0:1],
                op0=ALU.mult, op1=ALU.add),
                sv, reads=[R_work[7], R_work[8], R_work[3]], writes=[R_work[3]])
        count_gt(work[:, 3, 0:1], R_work[3])
        P.run("vector", lambda E: E.tensor_scalar(
            work[:, 6, 0:1], sc1[:, 7:8], float(-1.0), float(KSEL),
            op0=ALU.mult, op1=ALU.add),
            sv, reads=[R_sc1], writes=[R_work[6]])
        P.run("vector", lambda E: E.tensor_scalar(
            work[:, 7, :], g2[:, :], work[:, 3, 0:1], None, op0=ALU.is_equal),
            sv, reads=[R_g2, R_work[3]], writes=[R_work[7]])

        def hier_scan(src, dst, tmp):
            P.run("vector", lambda E, s=src, t=tmp: E.tensor_tensor_scan(
                work[:, t, :], work[:, s, :], zer[:, :],
                initial=0.0, op0=ALU.add, op1=ALU.bypass),
                sv, reads=[R_work[src], R_zer], writes=[R_work[tmp]])
            P.run("vector", lambda E, s=src, d=dst, t=tmp: E.tensor_tensor(
                work[:, d, :], work[:, t, :], work[:, s, :], op=ALU.subtract),
                sv, reads=[R_work[tmp], R_work[src]], writes=[R_work[dst]])
            P.run("tensor", lambda E, t=tmp: E.matmul(
                psF[:, 0:1], lpref[:, :], work[:, t, 127:128],
                start=True, stop=True),
                st, reads=[R_work[tmp], R_const], writes=[R_psF])
            P.run("vector", lambda E: E.tensor_copy(sc1[:, 9:10], psF[:, 0:1]),
                  sv, reads=[R_psF], writes=[R_sc1])
            P.run("vector", lambda E, d=dst: E.tensor_scalar(
                work[:, d, :], work[:, d, :], sc1[:, 9:10], None, op0=ALU.add),
                sv, reads=[R_sc1, R_work[dst]], writes=[R_work[dst]])

        hier_scan(7, 8, 9)
        P.run("vector", lambda E: E.tensor_scalar(
            work[:, 9, :], work[:, 8, :], work[:, 6, 0:1], None, op0=ALU.is_lt),
            sv, reads=[R_work[8], R_work[6]], writes=[R_work[9]])
        P.run("vector", lambda E: E.tensor_tensor(
            work[:, 9, :], work[:, 9, :], work[:, 7, :], op=ALU.mult),
            sv, reads=[R_work[9], R_work[7]], writes=[R_work[9]])
        P.run("vector", lambda E: E.tensor_tensor(
            work[:, 9, :], work[:, 9, :], work[:, 5, :], op=ALU.add),
            sv, reads=[R_work[9], R_work[5]], writes=[R_work[9]])
        hier_scan(9, 10, 4)
        P.dma("gpsimd", rkm_out[:, 0:128], work[:, 10, :], so, reads=[R_work[10]])
        P.dma("gpsimd", rkm_out[:, 128:256], work[:, 9, :], so, reads=[R_work[9]])

        P.run("vector", lambda E: E.tensor_copy(scr[:, 0:128], g2[:, :]),
              sv, reads=[R_g2], writes=[R_scr])
        P.run("tensor", lambda E: E.transpose(
            pst[:, 0, 0:128], scr[:, 0:128], ident[:, :]),
            st, reads=[R_scr, R_ident], writes=[R_pst[0]])
        P.run("vector", lambda E: E.tensor_copy(gT[:, :], pst[:, 0, 0:128]),
              sv, reads=[R_pst[0]], writes=[R_gT])
        for src, dst, dres, sres in ((10, rkT, R_rkT, R_work[10]),
                                     (9, smT, R_smT, R_work[9])):
            P.run("tensor", lambda E, s=src: E.transpose(
                psF[:, :], work[:, s, :], identF[:, :]),
                st, reads=[sres, R_identF], writes=[R_psF])
            P.run("vector", lambda E, d=dst: E.tensor_copy(d[:, :], psF[:, :]),
                  sv, reads=[R_psF], writes=[dres])

        for tl in range(4):
            bb = tl // 2
            P.run("vector", lambda E, tl=tl: E.tensor_scalar(
                rkTs[:, :], rkT[:, :], rkb[:, tl:tl + 1], None, op0=ALU.subtract),
                sv, reads=[R_rkT, R_const], writes=[R_rkTs])
            for c in range(64):
                cc = bb * 64 + c
                hb_ = c % 2
                P.run("vector", lambda E, cc=cc, hb_=hb_: E.tensor_scalar(
                    ohb[:, hb_, :], iotaF[:, :], rkTs[:, cc:cc + 1], None,
                    op0=ALU.is_equal),
                    sv, reads=[R_const, R_rkTs], writes=[R_ohb[hb_]])
                P.run("vector", lambda E, cc=cc, hb_=hb_: E.tensor_scalar(
                    ohb[:, hb_, :], ohb[:, hb_, :], smT[:, cc:cc + 1], None,
                    op0=ALU.mult),
                    sv, reads=[R_smT, R_ohb[hb_]], writes=[R_ohb[hb_]])
                P.run("tensor", lambda E, c=c, hb_=hb_: E.matmul(
                    ps[:, 0, 0:1], ohb[:, hb_, :], fcol[:, :],
                    start=(c == 0), stop=(c == 63)),
                    st, reads=[R_ohb[hb_], R_const], writes=[R_ps[0]])
                P.run("tensor", lambda E, c=c, cc=cc, hb_=hb_: E.matmul(
                    ps[:, 1, 0:1], ohb[:, hb_, :], gT[:, cc:cc + 1],
                    start=(c == 0), stop=(c == 63)),
                    st, reads=[R_ohb[hb_], R_gT], writes=[R_ps[1]])
                P.run("tensor", lambda E, c=c, hb_=hb_: E.matmul(
                    ps[:, 2, 0:1], ohb[:, hb_, :], ccols[:, c:c + 1],
                    start=(c == 0), stop=(c == 63)),
                    st, reads=[R_ohb[hb_], R_const], writes=[R_ps[2]])
            P.run("vector", lambda E, tl=tl: E.tensor_scalar(
                idxf[:, tl:tl + 1], ps[:, 2, 0:1], 128.0, None, op0=ALU.mult),
                sv, reads=[R_ps[2]], writes=[R_idxf])
            P.run("vector", lambda E, tl=tl: E.tensor_tensor(
                idxf[:, tl:tl + 1], idxf[:, tl:tl + 1], ps[:, 0, 0:1], op=ALU.add),
                sv, reads=[R_ps[0], R_idxf], writes=[R_idxf])
            P.run("vector", lambda E, tl=tl: E.tensor_copy(
                idxf[:, 4 + tl:5 + tl], ps[:, 1, 0:1]),
                sv, reads=[R_ps[1]], writes=[R_idxf])
        P.dma("gpsimd", sc_out[:, :], idxf[:, :], so, reads=[R_idxf])

        P.run("vector", lambda E: E.tensor_copy(idxu[:, 0, :], idxf[:, 0:4]),
              sv, reads=[R_idxf], writes=[R_idxu])
        P.run("vector", lambda E: E.tensor_copy(work[:, 4, 0:4], idxf[:, 0:4]),
              sv, reads=[R_idxf], writes=[R_work[4]])
        P.run("vector", lambda E: E.tensor_scalar(
            work[:, 4, 2:4], work[:, 4, 2:4], float(N), None, op0=ALU.add),
            sv, reads=[R_work[4]], writes=[R_work[4]])
        P.run("vector", lambda E: E.tensor_copy(idxu[:, 1, :], work[:, 4, 0:4]),
              sv, reads=[R_work[4]], writes=[R_idxu])

        for tl in range(4):
            P.run("gpsimd", lambda E, tl=tl: E.indirect_dma_start(
                out=selb[:, tl, :], out_offset=None, in_=procfull[:, :],
                in_offset=bass.IndirectOffsetOnAxis(ap=idxu[:, 1, tl:tl + 1], axis=0),
                bounds_check=B * N - 1, oob_is_err=True),
                P._dma_sem(R_selb[tl]), k=16, reads=[R_idxu], writes=[R_selb[tl]])
            P.run("gpsimd", lambda E, tl=tl: E.indirect_dma_start(
                out=csg[:, tl, :], out_offset=None, in_=cstabi[:, :],
                in_offset=bass.IndirectOffsetOnAxis(ap=idxu[:, 0, tl:tl + 1], axis=0),
                bounds_check=N - 1, oob_is_err=True),
                P._dma_sem(R_csg), k=16, reads=[R_idxu], writes=[R_csg])
        for tl in range(4):
            P.dma("gpsimd", sel_out[128 * tl:128 * (tl + 1), :], selb[:, tl, :],
                  so, reads=[R_selb[tl]])

        for tl in range(4):
            P.run("tensor", lambda E, tl=tl: E.transpose(
                psF[0:64, 0:128], csg[:, tl, :], identF[:, :]),
                st, reads=[R_csg, R_identF], writes=[R_psF])
            P.run("vector", lambda E, tl=tl: E.tensor_copy(
                csrC[:, 128 * tl:128 * (tl + 1)], psF[0:32, 0:128]),
                sv, reads=[R_psF], writes=[R_csr])
            P.run("vector", lambda E, tl=tl: E.tensor_copy(
                csrS[:, 128 * tl:128 * (tl + 1)], psF[32:64, 0:128]),
                sv, reads=[R_psF], writes=[R_csr])
        P.run("tensor", lambda E: E.matmul(
            ps[:, 3, :], repcb[:, :], csrC[:, :], start=True, stop=True),
            st, reads=[R_csr, R_const], writes=[R_ps[3]])
        P.run("vector", lambda E: E.tensor_copy(cs2[:, 0, :], ps[:, 3, :]),
              sv, reads=[R_ps[3]], writes=[R_cs2])
        P.run("tensor", lambda E: E.matmul(
            ps[:, 3, :], repcb[:, :], csrS[:, :], start=True, stop=True),
            st, reads=[R_csr, R_const], writes=[R_ps[3]])
        P.run("vector", lambda E: E.tensor_copy(cs2[:, 1, :], ps[:, 3, :]),
              sv, reads=[R_ps[3]], writes=[R_cs2])

        for tl in range(4):
            P.run("scalar", lambda E, tl=tl: E.activation(
                scr[:, :], selb[:, tl, :], AF.Square, accum_out=stat[:, tl, 0:1]),
                sa, reads=[R_selb[tl]], writes=[R_stat[tl], R_scr])
            P.run("scalar", lambda E, tl=tl: E.activation(
                stat[:, tl, 0:1], stat[:, tl, 0:1], AF.Ln, bias=cvb[:, 1:2],
                scale=float(1.0 / D)),
                sa, reads=[R_stat[tl], R_const], writes=[R_stat[tl]])
            P.run("scalar", lambda E, tl=tl: E.activation(
                stat[:, tl, 0:1], stat[:, tl, 0:1], AF.Exp, scale=-0.5),
                sa, reads=[R_stat[tl]], writes=[R_stat[tl]])
            P.run("vector", lambda E, tl=tl: E.scalar_tensor_tensor(
                hb3[:, tl, :], selb[:, tl, :], stat[:, tl, 0:1], lnb[:, :],
                op0=ALU.mult, op1=ALU.mult),
                sv, reads=[R_selb[tl], R_stat[tl], R_const], writes=[R_hb3[tl]])
            for c in range(8):
                pb = (tl * 8 + c) % 2
                P.run("tensor", lambda E, tl=tl, c=c, pb=pb: E.transpose(
                    pst[:, pb, 0:128], hb3[:, tl, 128 * c:128 * (c + 1)], ident[:, :]),
                    st, reads=[R_hb3[tl], R_ident], writes=[R_pst[pb]])
                P.run("vector", lambda E, tl=tl, c=c, pb=pb: E.tensor_copy(
                    h2T[:, c, 128 * tl:128 * (tl + 1)], pst[:, pb, 0:128]),
                    sv, reads=[R_pst[pb]], writes=[R_h2T])

        def load_w(w_dram, oc, buf):
            for c in range(8):
                P.dma("gpsimd", wt[:, c, buf, :],
                      w_dram[128 * c:128 * (c + 1), 128 * oc:128 * (oc + 1)],
                      sg, writes=[R_wt[buf]])
            return R_wt[buf]

        def mmchain(wres, buf, bank):
            P.rd("tensor", [wres, R_h2T])
            P.wr("tensor", [R_ps[bank]])
            for c in range(8):
                f = (lambda E, c=c, buf=buf, bank=bank:
                     E.matmul(ps[:, bank, :], wt[:, c, buf, :], h2T[:, c, :],
                              start=(c == 0), stop=(c == 7)))
                if c == 7:
                    st.n += 1
                    P.ops["tensor"].append(lambda E, f=f: f(E).then_inc(_st_, 1))
                    P.done_write([R_ps[bank]], st, st.n)
                    P.done_read([wres, R_h2T], st, st.n)
                else:
                    P.ops["tensor"].append(lambda E, f=f: f(E))

        def rope_proj(w_dram, out_dram):
            for ocp in range(4):
                w0 = load_w(w_dram, ocp, 0)
                w1 = load_w(w_dram, ocp + 4, 1)
                mmchain(w0, 0, 0)
                mmchain(w1, 1, 1)
                b0, b1, b2, b3 = R_ob[4], R_ob[5], R_ob[6], R_ob[7]
                P.run("vector", lambda E: E.tensor_tensor(
                    ob[:, 4, :], ps[:, 0, :], cs2[:, 0, :], op=ALU.mult),
                    sv, reads=[R_ps[0], R_cs2], writes=[b0])
                P.run("vector", lambda E: E.tensor_tensor(
                    ob[:, 5, :], ps[:, 1, :], cs2[:, 1, :], op=ALU.mult),
                    sv, reads=[R_ps[1], R_cs2], writes=[b1])
                P.run("vector", lambda E: E.tensor_tensor(
                    ob[:, 6, :], ps[:, 1, :], cs2[:, 0, :], op=ALU.mult),
                    sv, reads=[R_ps[1], R_cs2], writes=[b2])
                P.run("vector", lambda E: E.tensor_tensor(
                    ob[:, 7, :], ps[:, 0, :], cs2[:, 1, :], op=ALU.mult),
                    sv, reads=[R_ps[0], R_cs2], writes=[b3])
                P.run("vector", lambda E: E.tensor_tensor(
                    ob[:, 4, :], ob[:, 4, :], ob[:, 5, :], op=ALU.subtract),
                    sv, reads=[b1], writes=[b0])
                P.run("vector", lambda E: E.tensor_tensor(
                    ob[:, 6, :], ob[:, 6, :], ob[:, 7, :], op=ALU.add),
                    sv, reads=[b3], writes=[b2])
                P.dma("gpsimd", out_dram[128 * ocp:128 * (ocp + 1), :], ob[:, 4, :],
                      so, reads=[b0])
                P.dma("gpsimd", out_dram[128 * (ocp + 4):128 * (ocp + 5), :],
                      ob[:, 6, :], so, reads=[b2])

        rope_proj(w_q2, q2T_out)
        rope_proj(w_k2, k2T_out)
        for oc in range(8):
            w0 = load_w(w_v2, oc, oc % 2)
            mmchain(w0, oc % 2, 2)
            slot = oc % 4
            P.run("scalar", lambda E, slot=slot: E.activation(
                ob[:, slot, :], ps[:, 2, :], AF.Identity),
                sa, reads=[R_ps[2]], writes=[R_ob[slot]])
            P.dma("gpsimd", v2T_out[128 * oc:128 * (oc + 1), :], ob[:, slot, :],
                  so, reads=[R_ob[slot]])

        P.finish("gpsimd")
        P.emit(blk)
    return nc


def host_prep_L3(inputs, l2b_results):
    dst = np.zeros(B * N, np.float32)
    dch = np.zeros(B * N, np.float32)
    for p in range(NC):
        posi = own_positions(p)
        d = np.asarray(l2b_results[p]["dstch"])
        for b in range(B):
            dst[b * N + posi] = d[b * TOKB:(b + 1) * TOKB, 0]
            dch[b * N + posi] = d[b * TOKB:(b + 1) * TOKB, 1]
    dst = dst.reshape(128, 128)
    dch = dch.reshape(128, 128)
    procfull = np.zeros((B * N, D), np.float32)
    for p in range(NC):
        posi = own_positions(p)
        pr = np.asarray(l2b_results[p]["proc"])
        for b in range(B):
            procfull[b * N + posi] = pr[b * TOKB:(b + 1) * TOKB]
    pos = np.arange(N, dtype=np.float64)
    inv = 1.0 / 10000.0 ** (np.arange(HALF) / HALF)
    ang = pos[:, None] * inv[None, :]
    cstab = np.concatenate([np.cos(ang), np.sin(ang)], 1).astype(np.float32)
    ident = np.eye(128, dtype=np.float32)
    consts = np.array([[1e-5, 1e-6, 0, 0, 0, 0, 0, 0]], np.float32)
    pp = np.arange(128)
    bselc = np.stack([(pp < 64).astype(np.float32),
                      (pp >= 64).astype(np.float32)], 1)
    bselTc = bselc.T.copy()
    lprefc = ((pp[:, None] < pp[None, :]) &
              ((pp[:, None] // 64) == (pp[None, :] // 64))).astype(np.float32)
    iotaFc = np.tile(np.arange(128, dtype=np.float32)[None, :], (128, 1))
    fcolc = np.arange(128, dtype=np.float32)[:, None].copy()
    ccolc = np.tile(np.arange(64, dtype=np.float32)[None, :], (128, 1))
    repcc = np.zeros((32, 128), np.float32)
    for i in range(32):
        repcc[i, np.arange(128) % 32 == i] = 1.0
    wq2 = _f32(np.asarray(inputs["L0_ffn_wq2"])[:, PERM_QK])
    wk2 = _f32(np.asarray(inputs["L0_ffn_wk2"])[:, PERM_QK])
    wv2 = _f32(inputs["L0_ffn_wv2"])
    ln12 = _f32(inputs["L0_ffn_ln1_2"]).reshape(1, D)
    in_maps = []
    for p in range(NC):
        rkbase = np.array([[256 * p, 256 * p + 128,
                            256 * (7 - p), 256 * (7 - p) + 128, 0, 0, 0, 0]],
                          np.float32)
        in_maps.append(dict(dst=dst, dch=dch, procfull=procfull, cstab=cstab,
                            wq2=wq2, wk2=wk2, wv2=wv2, ln12=ln12,
                            identc=ident, constv=consts, bselc=bselc,
                            bselTc=bselTc, lprefc=lprefc, iotaFc=iotaFc,
                            fcolc=fcolc, ccolc=ccolc, repcc=repcc,
                            rkbase=rkbase))
    return in_maps


def run_L3(inputs, l2b_results):
    nc = _get_prog("L3", build_L3)
    in_maps = host_prep_L3(inputs, l2b_results)
    res = run_bass_kernel_spmd(nc, in_maps, core_ids=list(range(NC)))
    return res.results


def build_L4():
    nc = bass.Bass("TRN2", target_bir_lowering=False)
    q2T_in = nc.dram_tensor("q2T", [D, 512], BF16, kind="ExternalInput").ap()
    k2f = nc.dram_tensor("k2f", [D, B * KSEL], BF16, kind="ExternalInput").ap()
    vswz2 = nc.dram_tensor("vswz2", [B * H, 128, 16 * 65], BF16, kind="ExternalInput").ap()
    sel_in = nc.dram_tensor("seli", [512, D], F32, kind="ExternalInput").ap()
    sc_in = nc.dram_tensor("sci", [128, 8], F32, kind="ExternalInput").ap()
    mask2c = nc.dram_tensor("mask2c", [128, 2 * 16 * 256], F32, kind="ExternalInput").ap()
    wo2t = nc.dram_tensor("wo2t", [8 * 128, 8 * 128], F32, kind="ExternalInput").ap()
    wg2t = nc.dram_tensor("wg2t", [32 * 128, 8 * 128], F32, kind="ExternalInput").ap()
    wu2t = nc.dram_tensor("wu2t", [32 * 128, 8 * 128], F32, kind="ExternalInput").ap()
    wd2t = nc.dram_tensor("wd2t", [8 * 128, 32 * 128], F32, kind="ExternalInput").ap()
    ln22 = nc.dram_tensor("ln22", [1, D], F32, kind="ExternalInput").ap()
    identc = nc.dram_tensor("identc", [128, 128], F32, kind="ExternalInput").ap()
    constv = nc.dram_tensor("constv", [1, 8], F32, kind="ExternalInput").ap()

    upd_out = nc.dram_tensor("updo", [512, D], F32, kind="ExternalOutput").ap()

    wob = nc.dram_tensor("wob4", [8 * 128, 8 * 128], BF16).ap()
    wgb = nc.dram_tensor("wgb4", [32 * 128, 8 * 128], BF16).ap()
    wub = nc.dram_tensor("wub4", [32 * 128, 8 * 128], BF16).ap()
    wdb = nc.dram_tensor("wdb4", [8 * 128, 32 * 128], BF16).ap()

    from contextlib import ExitStack
    with ExitStack() as ctx:
        E_ = ctx.enter_context
        kb = E_(nc.sbuf_tensor("kb4", [64, 2, 2048], BF16))
        vb = E_(nc.sbuf_tensor("vb4", [128, 2, 16 * 65], BF16))
        qb = E_(nc.sbuf_tensor("qb4", [64, 2, 256], BF16))
        mk2 = E_(nc.sbuf_tensor("mk2", [128, 8192], BF16))
        pt = E_(nc.sbuf_tensor("pt4", [128, 2, 256], BF16))
        rd = E_(nc.sbuf_tensor("rd4", [1, 256], F32))
        rb = E_(nc.sbuf_tensor("rb4", [64, 256], F32))
        on1 = E_(nc.sbuf_tensor("on14", [1, 64], F32))
        OT2 = E_(nc.sbuf_tensor("OT2", [128, 8, 512], BF16))
        selb = E_(nc.sbuf_tensor("selb4", [128, 4, D], F32))
        x2b = E_(nc.sbuf_tensor("x2b", [128, 4, D], F32))
        gub = E_(nc.sbuf_tensor("gub4", [128, 32, 512], BF16))
        h2T = E_(nc.sbuf_tensor("h2T4", [128, 8, 512], BF16))
        hb4 = E_(nc.sbuf_tensor("hb4", [128, 2, D], BF16))
        wt = E_(nc.sbuf_tensor("wt4", [128, 2, 8, 128], BF16))
        wtd = E_(nc.sbuf_tensor("wtd4", [128, 2, 32, 128], BF16))
        ob = E_(nc.sbuf_tensor("ob5", [128, 6, 512], BF16))
        scb = E_(nc.sbuf_tensor("scb", [128, 8], F32))
        lnb = E_(nc.sbuf_tensor("lnb4", [128, D], F32))
        cvb = E_(nc.sbuf_tensor("cvb5", [128, 8], F32))
        ident = E_(nc.sbuf_tensor("ident4", [128, 128], BF16))
        stat = E_(nc.sbuf_tensor("stat4", [128, 4, 4], F32))
        scr = E_(nc.sbuf_tensor("scr4", [128, D], BF16))
        tmpb = E_(nc.sbuf_tensor("tmpb", [128, D], F32))
        ps = E_(nc.psum_tensor("ps4", [128, 4, 512], F32))
        psB = E_(nc.psum_tensor("psB4", [128, 512], F32))
        pst = E_(nc.psum_tensor("pst4", [128, 2, 1024], BF16))
        _sd = E_(nc.semaphore("sem_d"))
        _st_ = E_(nc.semaphore("sem_t"))
        _sv = E_(nc.semaphore("sem_v"))
        _sa = E_(nc.semaphore("sem_a"))
        _sg = E_(nc.semaphore("sem_g"))
        _so = E_(nc.semaphore("sem_o"))
        sem_pool = [Sem(E_(nc.semaphore("sp%d" % _i))) for _i in range(24)]
        blk = E_(nc.Block())

        P = Prog(nc, sem_pool)
        sd, st, sv, sa, sg, so = (Sem(_sd), Sem(_st_), Sem(_sv), Sem(_sa),
                                  Sem(_sg), Sem(_so))
        Rs = lambda n: [Res() for _ in range(n)]
        R_kb, R_vb, R_qb = Rs(2), Rs(2), Rs(2)
        R_mk2, R_pt = Res(), Rs(2)
        R_rd, R_rb, R_on1 = Res(), Res(), Res()
        R_OT2 = Res()
        R_selb = Rs(4)
        R_x2 = Rs(4)
        R_gub = Rs(32)
        R_h2T = Res()
        R_hb4 = Rs(2)
        R_wt, R_wtd = Rs(2), Rs(2)
        R_ob = Rs(6)
        R_scb, R_ln, R_cv, R_ident = Res(), Res(), Res(), Res()
        R_stat = Rs(4)
        R_scr, R_tmp = Res(), Res()
        R_ps = Rs(4)
        R_psB = Res()
        R_pst = Rs(2)
        R_cast = Res()

        P.dma("gpsimd", ident[:, :], identc[:, :], sg, writes=[R_ident])
        P.dma("gpsimd", mk2[:, :], mask2c[:, :], sg, writes=[R_mk2])
        for s_, d_ in ((wo2t, wob), (wg2t, wgb), (wu2t, wub), (wd2t, wdb)):
            P.dma("gpsimd", d_[:, :], s_[:, :], sg, writes=[R_cast])
        P.dma("sync", lnb[:, :], ln22.to_broadcast([128, D]), sd, writes=[R_ln])
        P.dma("sync", cvb[:, :], constv.to_broadcast([128, 8]), sd, writes=[R_cv])
        P.dma("sync", scb[:, :], sc_in[:, :], sd, writes=[R_scb])
        for tl in range(4):
            P.dma("sync", selb[:, tl, :], sel_in[128 * tl:128 * (tl + 1), :], sd,
                  writes=[R_selb[tl]])
        P.run("vector", lambda E: E.memset(on1[:, :], 1.0), sv, writes=[R_on1])

        for u in range(2):
            for h in range(H):
                ib = (u * H + h) % 2
                bb_cols = u * KSEL
                P.dma("sync", kb[0:32, ib, :],
                      k2f[32 * h:32 * h + 32, bb_cols:bb_cols + KSEL], sd,
                      writes=[R_kb[ib]])
                P.dma("sync", kb[32:64, ib, :],
                      k2f[512 + 32 * h:512 + 32 * h + 32, bb_cols:bb_cols + KSEL],
                      sd, writes=[R_kb[ib]])
                P.dma("sync", vb[:, ib, :], vswz2[u * H + h, :, :], sd,
                      writes=[R_vb[ib]])
                P.dma("sync", qb[0:32, ib, :],
                      q2T_in[32 * h:32 * h + 32, 256 * u:256 * (u + 1)], sd,
                      writes=[R_qb[ib]])
                P.dma("sync", qb[32:64, ib, :],
                      q2T_in[512 + 32 * h:512 + 32 * h + 32, 256 * u:256 * (u + 1)],
                      sd, writes=[R_qb[ib]])
                for c in range(16):
                    sb = c % 2
                    P.run("tensor", lambda E, ib=ib, sb=sb, c=c: E.matmul(
                        ps[:, sb, 0:256], kb[:, ib, 128 * c:128 * (c + 1)],
                        qb[:, ib, :], start=True, stop=True),
                        st, reads=[R_kb[ib], R_qb[ib]], writes=[R_ps[sb]])
                    P.run("scalar", lambda E, sb=sb: E.activation(
                        pt[:, sb, :], ps[:, sb, 0:256], AF.Exp, scale=0.125,
                        bias=cvb[:, 0:1]),
                        sa, reads=[R_ps[sb], R_cv], writes=[R_pt[sb]])
                    P.run("vector", lambda E, sb=sb, u=u, c=c: E.tensor_tensor(
                        pt[:, sb, :], pt[:, sb, :],
                        mk2[:, (u * 16 + c) * 256:(u * 16 + c + 1) * 256],
                        op=ALU.mult),
                        sv, reads=[R_mk2, R_pt[sb]], writes=[R_pt[sb]])
                    P.run("tensor", lambda E, ib=ib, sb=sb, c=c: E.matmul(
                        ps[0:65, 2, 0:256], vb[:, ib, 65 * c:65 * (c + 1)],
                        pt[:, sb, :], start=(c == 0), stop=(c == 15)),
                        st, reads=[R_vb[ib], R_pt[sb]], writes=[R_ps[2]])
                P.run("vector", lambda E: E.reciprocal(
                    rd[:, :], ps[64:65, 2, 0:256]),
                    sv, reads=[R_ps[2]], writes=[R_rd])
                P.run("tensor", lambda E: E.matmul(
                    psB[0:64, 0:256], on1[:, :], rd[:, :], start=True, stop=True),
                    st, reads=[R_rd, R_on1], writes=[R_psB])
                P.run("vector", lambda E: E.tensor_copy(rb[:, :], psB[0:64, 0:256]),
                      sv, reads=[R_psB], writes=[R_rb])
                P.run("vector", lambda E, h=h, u=u: E.tensor_tensor(
                    OT2[64 * (h % 2):64 * (h % 2) + 64, h // 2, 256 * u:256 * (u + 1)],
                    ps[0:64, 2, 0:256], rb[:, :], op=ALU.mult),
                    sv, reads=[R_ps[2], R_rb], writes=[R_OT2])

        def mmchain8(wres, buf, bank, rhs_tiles, rhs_res, coloff):
            P.rd("tensor", [wres, rhs_res])
            P.wr("tensor", [R_ps[bank]])
            for c in range(8):
                f = (lambda E, c=c, buf=buf, bank=bank, coloff=coloff:
                     E.matmul(ps[:, bank, :], wt[:, buf, c, :],
                              rhs_tiles[:, c, coloff:coloff + 512],
                              start=(c == 0), stop=(c == 7)))
                if c == 7:
                    st.n += 1
                    P.ops["tensor"].append(lambda E, f=f: f(E).then_inc(_st_, 1))
                    P.done_write([R_ps[bank]], st, st.n)
                    P.done_read([wres, rhs_res], st, st.n)
                else:
                    P.ops["tensor"].append(lambda E, f=f: f(E))

        def loadw8(w_dram, oc, buf):
            P.dma("sync", wt[:, buf, :, :],
                  w_dram[128 * oc:128 * (oc + 1), :].rearrange("p (c j) -> p c j", c=8),
                  sd, writes=[R_wt[buf]])
            return R_wt[buf]

        tpb = 0
        for oc in range(8):
            wres = loadw8(wob, oc, oc % 2)
            mmchain8(wres, oc % 2, oc % 2, OT2, R_OT2, 0)
            slot = 4 + (oc % 2)
            P.run("scalar", lambda E, pb=oc % 2, slot=slot: E.activation(
                ob[:, slot, :], ps[:, pb, :], AF.Identity),
                sa, reads=[R_ps[oc % 2]], writes=[R_ob[slot]])
            for sub in range(4):
                pb2 = tpb % 2
                P.run("tensor", lambda E, slot=slot, sub=sub, pb2=pb2: E.transpose(
                    pst[:, pb2, 0:128], ob[:, slot, 128 * sub:128 * (sub + 1)],
                    ident[:, :]),
                    st, reads=[R_ob[slot], R_ident], writes=[R_pst[pb2]])
                P.run("vector", lambda E, sub=sub, oc=oc, pb2=pb2: E.tensor_tensor(
                    x2b[:, sub, 128 * oc:128 * (oc + 1)],
                    selb[:, sub, 128 * oc:128 * (oc + 1)],
                    pst[:, pb2, 0:128], op=ALU.add),
                    sv, reads=[R_pst[pb2], R_selb[sub]], writes=[R_x2[sub]])
                tpb += 1

        for sub in range(4):
            P.run("scalar", lambda E, sub=sub: E.activation(
                scr[:, :], x2b[:, sub, :], AF.Square, accum_out=stat[:, sub, 0:1]),
                sa, reads=[R_x2[sub]], writes=[R_stat[sub], R_scr])
            P.run("scalar", lambda E, sub=sub: E.activation(
                stat[:, sub, 0:1], stat[:, sub, 0:1], AF.Ln, bias=cvb[:, 1:2],
                scale=float(1.0 / D)),
                sa, reads=[R_stat[sub], R_cv], writes=[R_stat[sub]])
            P.run("scalar", lambda E, sub=sub: E.activation(
                stat[:, sub, 0:1], stat[:, sub, 0:1], AF.Exp, scale=-0.5),
                sa, reads=[R_stat[sub]], writes=[R_stat[sub]])
            hsl = sub % 2
            P.run("vector", lambda E, sub=sub, hsl=hsl: E.scalar_tensor_tensor(
                hb4[:, hsl, :], x2b[:, sub, :], stat[:, sub, 0:1], lnb[:, :],
                op0=ALU.mult, op1=ALU.mult),
                sv, reads=[R_x2[sub], R_stat[sub], R_ln], writes=[R_hb4[hsl]])
            for c in range(8):
                pb2 = tpb % 2
                P.run("tensor", lambda E, hsl=hsl, c=c, pb2=pb2: E.transpose(
                    pst[:, pb2, 0:128], hb4[:, hsl, 128 * c:128 * (c + 1)],
                    ident[:, :]),
                    st, reads=[R_hb4[hsl], R_ident], writes=[R_pst[pb2]])
                P.run("vector", lambda E, sub=sub, c=c, pb2=pb2: E.tensor_copy(
                    h2T[:, c, 128 * sub:128 * (sub + 1)], pst[:, pb2, 0:128]),
                    sv, reads=[R_pst[pb2]], writes=[R_h2T])
                tpb += 1

        for nch in range(32):
            wb = nch % 2
            P.dma("sync", wt[:, wb, :, :],
                  wgb[128 * nch:128 * (nch + 1), :].rearrange("p (c j) -> p c j", c=8),
                  sd, writes=[R_wt[wb]])
            mmchain8(R_wt[wb], wb, 2, h2T, R_h2T, 0)
            P.dma("gpsimd", wtd[:, wb, 0:8, :],
                  wub[128 * nch:128 * (nch + 1), :].rearrange("p (c j) -> p c j", c=8),
                  sg, writes=[R_wtd[wb]])
            P.rd("tensor", [R_wtd[wb], R_h2T])
            P.wr("tensor", [R_ps[3]])
            for c in range(8):
                f = (lambda E, c=c, wb=wb:
                     E.matmul(ps[:, 3, :], wtd[:, wb, c, :], h2T[:, c, :],
                              start=(c == 0), stop=(c == 7)))
                if c == 7:
                    st.n += 1
                    P.ops["tensor"].append(lambda E, f=f: f(E).then_inc(_st_, 1))
                    P.done_write([R_ps[3]], st, st.n)
                    P.done_read([R_wtd[wb], R_h2T], st, st.n)
                else:
                    P.ops["tensor"].append(lambda E, f=f: f(E))
            P.run("scalar", lambda E: E.activation(
                ob[:, 0, :], ps[:, 2, :], AF.Silu),
                sa, reads=[R_ps[2]], writes=[R_ob[0]])
            P.run("vector", lambda E, nch=nch: E.tensor_tensor(
                gub[:, nch, :], ob[:, 0, :], ps[:, 3, :], op=ALU.mult),
                sv, reads=[R_ob[0], R_ps[3]], writes=[R_gub[nch]])
        for oc in range(8):
            wb = oc % 2
            P.dma("sync", wtd[:, wb, :, :],
                  wdb[128 * oc:128 * (oc + 1), :].rearrange("p (n j) -> p n j", n=32),
                  sd, writes=[R_wtd[wb]])
            pb = oc % 2
            P.rd("tensor", [R_wtd[wb]] + R_gub)
            P.wr("tensor", [R_ps[pb]])
            for n in range(32):
                f = (lambda E, n=n, wb=wb, pb=pb:
                     E.matmul(ps[:, pb, :], wtd[:, wb, n, :], gub[:, n, :],
                              start=(n == 0), stop=(n == 31)))
                if n == 31:
                    st.n += 1
                    P.ops["tensor"].append(lambda E, f=f: f(E).then_inc(_st_, 1))
                    P.done_write([R_ps[pb]], st, st.n)
                    P.done_read([R_wtd[wb]] + R_gub, st, st.n)
                else:
                    P.ops["tensor"].append(lambda E, f=f: f(E))
            slot = 4 + (oc % 2)
            P.run("scalar", lambda E, pb=pb, slot=slot: E.activation(
                ob[:, slot, :], ps[:, pb, :], AF.Identity),
                sa, reads=[R_ps[pb]], writes=[R_ob[slot]])
            for sub in range(4):
                pb2 = tpb % 2
                P.run("tensor", lambda E, slot=slot, sub=sub, pb2=pb2: E.transpose(
                    pst[:, pb2, 0:128], ob[:, slot, 128 * sub:128 * (sub + 1)],
                    ident[:, :]),
                    st, reads=[R_ob[slot], R_ident], writes=[R_pst[pb2]])
                P.run("vector", lambda E, sub=sub, oc=oc, pb2=pb2: E.tensor_tensor(
                    x2b[:, sub, 128 * oc:128 * (oc + 1)],
                    x2b[:, sub, 128 * oc:128 * (oc + 1)],
                    pst[:, pb2, 0:128], op=ALU.add),
                    sv, reads=[R_pst[pb2]], writes=[R_x2[sub]])
                tpb += 1
        for sub in range(4):
            P.run("vector", lambda E, sub=sub: E.tensor_tensor(
                tmpb[:, :], x2b[:, sub, :], selb[:, sub, :], op=ALU.subtract),
                sv, reads=[R_x2[sub], R_selb[sub]], writes=[R_tmp])
            P.run("vector", lambda E, sub=sub: E.scalar_tensor_tensor(
                x2b[:, sub, :], tmpb[:, :], scb[:, 4 + sub:5 + sub], selb[:, sub, :],
                op0=ALU.mult, op1=ALU.add),
                sv, reads=[R_tmp, R_scb, R_selb[sub]], writes=[R_x2[sub]])
            P.dma("gpsimd", upd_out[128 * sub:128 * (sub + 1), :], x2b[:, sub, :],
                  so, reads=[R_x2[sub]])
        P.finish("gpsimd")
        P.emit(blk)
    return nc


def host_prep_L4(inputs, l3_results):
    k2f = np.zeros((D, B * KSEL), BF16NP)
    v2full = np.zeros((B * KSEL, D), BF16NP)
    for p in range(NC):
        r = l3_results[p]
        bases = [(0, 256 * p), (1, 256 * (7 - p))]
        for ui, (b, base) in enumerate(bases):
            cols = slice(256 * ui, 256 * (ui + 1))
            k2f[:, b * KSEL + base:b * KSEL + base + 256] = np.asarray(r["k2T"])[:, cols]
            v2full[b * KSEL + base:b * KSEL + base + 256, :] = np.asarray(r["v2T"])[:, cols].T
    vswz2 = np.ones((B * H, 128, 16 * 65), BF16NP)
    for b in range(B):
        for h in range(H):
            vv = v2full[b * KSEL:(b + 1) * KSEL, 64 * h:64 * h + 64].reshape(16, 128, 64)
            vs = vswz2[b * H + h].reshape(128, 16, 65)
            vs[:, :, 0:64] = vv.transpose(1, 0, 2)
    wo2t = _tile_w(inputs["L0_ffn_wo2"], 8, 8)
    wg2t = _tile_w(inputs["L0_ffn_wg2"], 8, 32)
    wu2t = _tile_w(inputs["L0_ffn_wu2"], 8, 32)
    wd2t = _tile_w(inputs["L0_ffn_wd2"], 32, 8)
    ln22 = _f32(inputs["L0_ffn_ln2_2"]).reshape(1, D)
    ident = np.eye(128, dtype=np.float32)
    consts = np.array([[-EXPB, 1e-6, 0, 0, 0, 0, 0, 0]], np.float32)
    in_maps = []
    kk = np.arange(128)
    qq = np.arange(256)
    for p in range(NC):
        ubase = [256 * p, 256 * (7 - p)]
        m = np.zeros((2, 16, 128, 256), np.float32)
        for u in range(2):
            for c in range(16):
                m[u, c] = (128 * c + kk[:, None] <= ubase[u] + qq[None, :])
        mask2c = np.ascontiguousarray(
            m.transpose(2, 0, 1, 3).reshape(128, 2 * 16 * 256)).astype(np.float32)
        r = l3_results[p]
        in_maps.append(dict(q2T=np.asarray(r["q2T"]), k2f=k2f, vswz2=vswz2,
                            seli=np.asarray(r["selo"]), sci=np.asarray(r["sco"]),
                            mask2c=mask2c, wo2t=wo2t, wg2t=wg2t, wu2t=wu2t,
                            wd2t=wd2t, ln22=ln22, identc=ident, constv=consts))
    return in_maps


def run_L4(inputs, l3_results):
    nc = _get_prog("L4", build_L4)
    in_maps = host_prep_L4(inputs, l3_results)
    res = run_bass_kernel_spmd(nc, in_maps, core_ids=list(range(NC)))
    return res.results


CCH = 128


def build_LcA():
    nc = bass.Bass("TRN2", target_bir_lowering=False)
    vch = nc.dram_tensor("vch", [B, CCH, N], BF16, kind="ExternalInput").ap()
    rpe_in = nc.dram_tensor("rpein", [128, 1], F32, kind="ExternalInput").ap()
    pos4c = nc.dram_tensor("pos4c", [128, N // 4], F32, kind="ExternalInput").ap()
    w4l = nc.dram_tensor("w4l", [3 * 128, 128], F32, kind="ExternalInput").ap()
    wog = nc.dram_tensor("wog", [4 * 128, 128], F32, kind="ExternalInput").ap()
    gb4 = nc.dram_tensor("gb4", [3 * 128, 2], F32, kind="ExternalInput").ap()
    gam = nc.dram_tensor("gam", [1, CCH], F32, kind="ExternalInput").ap()
    posr = nc.dram_tensor("posr", [1, N], F32, kind="ExternalInput").ap()
    blk1 = nc.dram_tensor("blk1", [128, 4], F32, kind="ExternalInput").ap()
    blk2 = nc.dram_tensor("blk2", [4, 128], F32, kind="ExternalInput").ap()
    dfc = nc.dram_tensor("dfc", [64, 256], F32, kind="ExternalInput").ap()
    constv = nc.dram_tensor("constv", [1, 8], F32, kind="ExternalInput").ap()

    ya_out = nc.dram_tensor("ya", [3, 2, 128, CCH * 128], BF16, kind="ExternalOutput").ap()
    tdr = nc.dram_tensor("tdr", [CCH, N], BF16).ap()

    from contextlib import ExitStack
    with ExitStack() as ctx:
        E_ = ctx.enter_context
        h4 = E_(nc.sbuf_tensor("h4", [128, N // 4], F32))
        hn4 = E_(nc.sbuf_tensor("hn4", [128, N // 4], F32))
        gs4 = E_(nc.sbuf_tensor("gs4", [4, 8, 512], F32))
        gsb = E_(nc.sbuf_tensor("gsb", [128, 2, N // 4], F32))
        w4b = E_(nc.sbuf_tensor("w4b", [128, 8, 128], BF16))
        gbb = E_(nc.sbuf_tensor("gbb", [128, 3, 2], F32))
        rpeb = E_(nc.sbuf_tensor("rpeb", [128, 1], F32))
        pos4b = E_(nc.sbuf_tensor("pos4b", [128, N // 4], F32))
        posb = E_(nc.sbuf_tensor("posb", [1, N], F32))
        blk1b = E_(nc.sbuf_tensor("blk1b", [128, 4], F32))
        blk2b = E_(nc.sbuf_tensor("blk2b", [4, 128], F32))
        gamb = E_(nc.sbuf_tensor("gamb", [128, 2], F32))
        iob = E_(nc.sbuf_tensor("iob", [128, N], F32))
        ttb = E_(nc.sbuf_tensor("ttb", [128, N // 4], BF16))
        dfb = E_(nc.sbuf_tensor("dfb", [64, 256], BF16))
        vt = E_(nc.sbuf_tensor("vt", [64, 2, 512], BF16))
        oeb = E_(nc.sbuf_tensor("oeb", [128, 4, 512], BF16))
        cvb = E_(nc.sbuf_tensor("cvb6", [128, 8], F32))
        ps = E_(nc.psum_tensor("psA", [128, 4, 512], F32))
        _sd = E_(nc.semaphore("sem_d"))
        _st_ = E_(nc.semaphore("sem_t"))
        _sv = E_(nc.semaphore("sem_v"))
        _sa = E_(nc.semaphore("sem_a"))
        _sg = E_(nc.semaphore("sem_g"))
        _so = E_(nc.semaphore("sem_o"))
        sem_pool = [Sem(E_(nc.semaphore("sp%d" % _i))) for _i in range(24)]
        blk = E_(nc.Block())
        P = Prog(nc, sem_pool)
        sd, st, sv, sa, sg, so = (Sem(_sd), Sem(_st_), Sem(_sv), Sem(_sa),
                                  Sem(_sg), Sem(_so))
        Rs = lambda n: [Res() for _ in range(n)]
        R_h4, R_hn4, R_gs4, R_gsb = Res(), Res(), Res(), Res()
        R_const, R_ttb, R_io, R_gam = Res(), Res(), Res(), Res()
        R_tdr = Res()
        R_vt = Rs(2)
        R_oeb = Rs(4)
        R_ps = Rs(4)

        P.dma("gpsimd", w4b[:, 0:3, :], w4l.rearrange("(l p) j -> p l j", p=128),
              sg, writes=[R_const])
        P.dma("gpsimd", w4b[:, 4:8, :], wog.rearrange("(g p) j -> p g j", p=128),
              sg, writes=[R_const])
        P.dma("gpsimd", dfb[:, :], dfc[:, :], sg, writes=[R_const])
        P.dma("gpsimd", gbb[:, :, :],
              gb4.rearrange("(l p) c -> p l c", p=128), sg, writes=[R_const])
        P.dma("sync", rpeb[:, :], rpe_in[:, :], sd, writes=[R_const])
        P.dma("sync", pos4b[:, :], pos4c[:, :], sd, writes=[R_const])
        P.dma("sync", posb[:, :], posr[:, :], sd, writes=[R_const])
        P.dma("sync", blk1b[:, :], blk1[:, :], sd, writes=[R_const])
        P.dma("sync", blk2b[:, :], blk2[:, :], sd, writes=[R_const])
        P.dma("sync", gamb[:, 0:1], gam.rearrange("a c -> c a"), sd, writes=[R_gam])
        P.dma("sync", cvb[:, :], constv.to_broadcast([128, 8]), sd, writes=[R_const])

        P.run("vector", lambda E: E.tensor_scalar(
            h4[:, :], pos4b[:, :], rpeb[:, 0:1], None, op0=ALU.mult),
            sv, reads=[R_const], writes=[R_h4])
        P.run("scalar", lambda E: E.activation(h4[:, :], h4[:, :], AF.Silu),
              sa, reads=[R_h4], writes=[R_h4])
        for li in range(3):
            P.run("scalar", lambda E: E.activation(
                hn4[:, :], h4[:, :], AF.Square),
                sa, reads=[R_h4], writes=[R_hn4])
            for ch in range(4):
                P.run("tensor", lambda E, ch=ch: E.matmul(
                    ps[0:4, 0, :], blk1b[:, :],
                    h4[:, 512 * ch:512 * (ch + 1)], start=True, stop=True),
                    st, reads=[R_h4, R_const], writes=[R_ps[0]])
                P.run("vector", lambda E, ch=ch: E.tensor_copy(
                    gs4[0:4, 2 * ch, :], ps[0:4, 0, :]),
                    sv, reads=[R_ps[0]], writes=[R_gs4])
                P.run("tensor", lambda E, ch=ch: E.matmul(
                    ps[0:4, 1, :], blk1b[:, :], hn4[:, 512 * ch:512 * (ch + 1)],
                    start=True, stop=True),
                    st, reads=[R_hn4, R_const], writes=[R_ps[1]])
                P.run("vector", lambda E, ch=ch: E.tensor_copy(
                    gs4[0:4, 1 + 2 * ch, :], ps[0:4, 1, :]),
                    sv, reads=[R_ps[1]], writes=[R_gs4])
            for ch in range(4):
                P.run("vector", lambda E, ch=ch: E.tensor_scalar(
                    gs4[0:4, 2 * ch, :], gs4[0:4, 2 * ch, :], float(1.0 / 32), None,
                    op0=ALU.mult),
                    sv, reads=[R_gs4], writes=[R_gs4])
                P.run("vector", lambda E, ch=ch: E.tensor_scalar(
                    gs4[0:4, 1 + 2 * ch, :], gs4[0:4, 1 + 2 * ch, :],
                    float(1.0 / 32), None, op0=ALU.mult),
                    sv, reads=[R_gs4], writes=[R_gs4])
                P.run("vector", lambda E, ch=ch: E.tensor_tensor(
                    gsb[0:4, 0, 512 * ch:512 * (ch + 1)], gs4[0:4, 2 * ch, :],
                    gs4[0:4, 2 * ch, :], op=ALU.mult),
                    sv, reads=[R_gs4], writes=[R_gsb])
                P.run("vector", lambda E, ch=ch: E.tensor_tensor(
                    gsb[0:4, 0, 512 * ch:512 * (ch + 1)],
                    gs4[0:4, 1 + 2 * ch, :],
                    gsb[0:4, 0, 512 * ch:512 * (ch + 1)], op=ALU.subtract),
                    sv, reads=[R_gs4, R_gsb], writes=[R_gsb])
            P.run("scalar", lambda E: E.activation(
                gsb[0:4, 0, :], gsb[0:4, 0, :], AF.Ln, bias=cvb[0:4, 0:1]),
                sa, reads=[R_gsb, R_const], writes=[R_gsb])
            P.run("scalar", lambda E: E.activation(
                gsb[0:4, 0, :], gsb[0:4, 0, :], AF.Exp, scale=-0.5),
                sa, reads=[R_gsb], writes=[R_gsb])
            for ch in range(4):
                P.run("tensor", lambda E, ch=ch: E.matmul(
                    ps[:, 2, :], blk2b[:, :], gs4[0:4, 2 * ch, :],
                    start=True, stop=True),
                    st, reads=[R_gs4, R_const], writes=[R_ps[2]])
                P.run("vector", lambda E, ch=ch: E.tensor_copy(
                    gsb[:, 1, 512 * ch:512 * (ch + 1)], ps[:, 2, :]),
                    sv, reads=[R_ps[2]], writes=[R_gsb])
                P.run("tensor", lambda E, ch=ch: E.matmul(
                    ps[:, 3, :], blk2b[:, :], gsb[0:4, 0, 512 * ch:512 * (ch + 1)],
                    start=True, stop=True),
                    st, reads=[R_gsb, R_const], writes=[R_ps[3]])
                P.run("vector", lambda E, ch=ch: E.tensor_tensor(
                    hn4[:, 512 * ch:512 * (ch + 1)], h4[:, 512 * ch:512 * (ch + 1)],
                    gsb[:, 1, 512 * ch:512 * (ch + 1)], op=ALU.subtract),
                    sv, reads=[R_h4, R_gsb], writes=[R_hn4])
                P.run("vector", lambda E, ch=ch: E.tensor_tensor(
                    hn4[:, 512 * ch:512 * (ch + 1)], hn4[:, 512 * ch:512 * (ch + 1)],
                    ps[:, 3, :], op=ALU.mult),
                    sv, reads=[R_hn4, R_ps[3]], writes=[R_hn4])
            P.run("vector", lambda E, li=li: E.tensor_scalar(
                hn4[:, :], hn4[:, :], gbb[:, li, 0:1], gbb[:, li, 1:2],
                op0=ALU.mult, op1=ALU.add),
                sv, reads=[R_hn4, R_const], writes=[R_hn4])
            P.run("vector", lambda E: E.tensor_copy(ttb[:, :], hn4[:, :]),
                  sv, reads=[R_hn4], writes=[R_ttb])
            for ch in range(4):
                P.run("tensor", lambda E, li=li, ch=ch: E.matmul(
                    ps[:, 0, :], w4b[:, li, :], ttb[:, 512 * ch:512 * (ch + 1)],
                    start=True, stop=True),
                    st, reads=[R_ttb, R_const], writes=[R_ps[0]])
                P.run("scalar", lambda E, ch=ch: E.activation(
                    h4[:, 512 * ch:512 * (ch + 1)], ps[:, 0, :], AF.Silu),
                    sa, reads=[R_ps[0]], writes=[R_h4])

        P.run("gpsimd", lambda E: E.iota(iob[:, :], [[1, N]], base=0,
                                         channel_multiplier=0,
                                         allow_small_or_imprecise_dtypes=True),
              sg, k=1, writes=[R_io])
        P.run("scalar", lambda E: E.activation(
            gamb[:, 1:2], gamb[:, 0:1], AF.Sigmoid),
            sa, reads=[R_gam], writes=[R_gam])
        P.run("vector", lambda E: E.tensor_scalar(
            gamb[:, 1:2], gamb[:, 1:2], 0.01, 0.99, op0=ALU.mult, op1=ALU.add),
            sv, reads=[R_gam], writes=[R_gam])
        P.run("scalar", lambda E: E.activation(
            gamb[:, 1:2], gamb[:, 1:2], AF.Ln),
            sa, reads=[R_gam], writes=[R_gam])
        P.run("vector", lambda E: E.tensor_scalar(
            iob[:, :], iob[:, :], gamb[:, 1:2], None, op0=ALU.mult),
            sv, reads=[R_io, R_gam], writes=[R_io])
        P.run("scalar", lambda E: E.activation(
            iob[:, :], iob[:, :], AF.Exp),
            sa, reads=[R_io], writes=[R_io])
        P.run("vector", lambda E: E.tensor_copy(ttb[:, :], h4[:, :]),
              sv, reads=[R_h4], writes=[R_ttb])
        for g in range(4):
            for ch in range(4):
                P.run("tensor", lambda E, g=g, ch=ch: E.matmul(
                    ps[:, 1, :], w4b[:, 4 + g, :], ttb[:, 512 * ch:512 * (ch + 1)],
                    start=True, stop=True),
                    st, reads=[R_ttb, R_const], writes=[R_ps[1]])
                P.run("vector", lambda E, g=g, ch=ch: E.tensor_tensor(
                    oeb[:, 0, :], ps[:, 1, :],
                    iob[:, 2048 * g + 512 * ch:2048 * g + 512 * (ch + 1)],
                    op=ALU.mult),
                    sv, reads=[R_ps[1], R_io], writes=[R_oeb[0]])
                P.dma("gpsimd",
                      tdr[:, 2048 * g + 512 * ch:2048 * g + 512 * (ch + 1)],
                      oeb[:, 0, :], so, reads=[R_oeb[0]], writes=[R_tdr])

        def stageA(sig_idx, src_ap, src_res):
            for chunk in range(32):
                ib = chunk % 2
                c0 = chunk * 4
                P.dma("sync", vt[:, ib, :].rearrange("p (c j) -> p c j", c=4),
                      src_ap[c0:c0 + 4, :].rearrange("c (j1 j2) -> j1 c j2", j1=64),
                      sd, reads=[src_res] if src_res else (),
                      writes=[R_vt[ib]])
                P.run("tensor", lambda E, ib=ib: E.matmul(
                    ps[:, 2, :], dfb[:, 0:128], vt[:, ib, :], start=True, stop=True),
                    st, reads=[R_vt[ib], R_const], writes=[R_ps[2]])
                P.run("tensor", lambda E, ib=ib: E.matmul(
                    ps[:, 3, :], dfb[:, 128:256], vt[:, ib, :], start=True, stop=True),
                    st, reads=[R_vt[ib], R_const], writes=[R_ps[3]])
                ob0 = (2 * chunk) % 4
                ob1 = (2 * chunk + 1) % 4
                P.run("vector", lambda E, ob0=ob0: E.tensor_copy(
                    oeb[:, ob0, :], ps[:, 2, :]),
                    sv, reads=[R_ps[2]], writes=[R_oeb[ob0]])
                P.dma("gpsimd", ya_out[sig_idx, 0, :, 512 * chunk:512 * (chunk + 1)],
                      oeb[:, ob0, :], so, reads=[R_oeb[ob0]])
                P.run("vector", lambda E, ob1=ob1: E.tensor_copy(
                    oeb[:, ob1, :], ps[:, 3, :]),
                    sv, reads=[R_ps[3]], writes=[R_oeb[ob1]])
                P.dma("gpsimd", ya_out[sig_idx, 1, :, 512 * chunk:512 * (chunk + 1)],
                      oeb[:, ob1, :], so, reads=[R_oeb[ob1]])

        stageA(0, tdr, R_tdr)
        stageA(1, vch[0], None)
        stageA(2, vch[1], None)
        P.finish("gpsimd")
        P.emit(blk)
    return nc


def build_LcB():
    nc = bass.Bass("TRN2", target_bir_lowering=False)
    yb = nc.dram_tensor("yb", [3, 2, 128, 128 * CCH], BF16, kind="ExternalInput").ap()
    mbc = nc.dram_tensor("mbc", [128, 3 * 128 * 128], BF16, kind="ExternalInput").ap()
    p_out = nc.dram_tensor("pp", [B, 2, 128, 128 * CCH], BF16, kind="ExternalOutput").ap()
    xdr = nc.dram_tensor("xdr", [3, 2, 128, 128 * CCH], BF16).ap()

    from contextlib import ExitStack
    with ExitStack() as ctx:
        E_ = ctx.enter_context
        yt = E_(nc.sbuf_tensor("yt", [128, 2, 2, 512], BF16))
        mb = E_(nc.sbuf_tensor("mb", [128, 2, 3, 128], BF16))
        xe = E_(nc.sbuf_tensor("xe", [128, 4, 512], BF16))
        pv = E_(nc.sbuf_tensor("pv", [128, 4, 512], BF16))
        pt_ = E_(nc.sbuf_tensor("pt_", [128, 4, 512], BF16))
        po = E_(nc.sbuf_tensor("po", [128, 4, 512], BF16))
        ps = E_(nc.psum_tensor("psB", [128, 6, 512], F32))
        _sd = E_(nc.semaphore("sem_d"))
        _st_ = E_(nc.semaphore("sem_t"))
        _sv = E_(nc.semaphore("sem_v"))
        _sg = E_(nc.semaphore("sem_g"))
        _so = E_(nc.semaphore("sem_o"))
        sem_pool = [Sem(E_(nc.semaphore("sp%d" % _i))) for _i in range(24)]
        blk = E_(nc.Block())
        P = Prog(nc, sem_pool)
        sd, st, sv, sg, so = Sem(_sd), Sem(_st_), Sem(_sv), Sem(_sg), Sem(_so)
        Rs = lambda n: [Res() for _ in range(n)]
        R_yt = Rs(2)
        R_mb = Rs(2)
        R_xe = Rs(4)
        R_pv, R_pt, R_po = Rs(4), Rs(4), Rs(4)
        R_ps = Rs(6)
        R_xdr = Res()

        for k1 in range(128):
            wb = k1 % 2
            P.dma("sync", mb[:, wb, :, :],
                  mbc[:, k1 * 384:(k1 + 1) * 384].rearrange("p (m j) -> p m j", m=3),
                  sd, writes=[R_mb[wb]])
            for s in range(3):
                ib = (k1 * 3 + s) % 2
                P.dma("sync", yt[:, ib, 0, 0:128], yb[s, 0, k1, :].rearrange("(j c) -> j c", j=128),
                      sd, writes=[R_yt[ib]])
                P.dma("sync", yt[:, ib, 1, 0:128], yb[s, 1, k1, :].rearrange("(j c) -> j c", j=128),
                      sd, writes=[R_yt[ib]])
                qk = k1 % 4
                br = 2 * s
                P.rd("tensor", [R_mb[wb], R_yt[ib]])
                P.wr("tensor", [R_ps[br]])
                P.wr("tensor", [R_ps[br + 1]])
                for (bank, lhs_i, pl) in ((br, 0, 0), (br, 2, 1), (br + 1, 1, 0), (br + 1, 0, 1)):
                    f = (lambda E, bank=bank, lhs_i=lhs_i, pl=pl, wb=wb, ib=ib, qk=qk:
                         E.matmul(ps[:, bank, 128 * qk:128 * (qk + 1)],
                                  mb[:, wb, lhs_i, :], yt[:, ib, pl, 0:128],
                                  start=(pl == 0), stop=(pl == 1)))
                    st.n += 1
                    P.ops["tensor"].append(lambda E, f=f: f(E).then_inc(_st_, 1))
                    P.done_write([R_ps[bank]], st, st.n)
                    P.done_read([R_mb[wb], R_yt[ib]], st, st.n)
                if qk == 3:
                    for pl, bank in ((0, br), (1, br + 1)):
                        sl = (k1 // 4 + pl) % 4
                        P.run("vector", lambda E, bank=bank, sl=sl: E.tensor_copy(
                            xe[:, sl, :], ps[:, bank, :]),
                            sv, reads=[R_ps[bank]], writes=[R_xe[sl]])
                        P.dma("gpsimd",
                              xdr[s, pl, :, 512 * (k1 // 4):512 * (k1 // 4 + 1)],
                              xe[:, sl, :], so, reads=[R_xe[sl]], writes=[R_xdr])
        for b in range(B):
            for ch in range(32):
                ib = ch % 2
                cs = slice(512 * ch, 512 * (ch + 1))
                P.dma("sync", pv[:, 2 * ib, :], xdr[1 + b, 0, :, cs], sd,
                      reads=[R_xdr], writes=[R_pv[2 * ib]])
                P.dma("sync", pv[:, 2 * ib + 1, :], xdr[1 + b, 1, :, cs], sd,
                      reads=[R_xdr], writes=[R_pv[2 * ib + 1]])
                P.dma("sync", pt_[:, 2 * ib, :], xdr[0, 0, :, cs], sd,
                      reads=[R_xdr], writes=[R_pt[2 * ib]])
                P.dma("sync", pt_[:, 2 * ib + 1, :], xdr[0, 1, :, cs], sd,
                      reads=[R_xdr], writes=[R_pt[2 * ib + 1]])
                o0, o1, o2, o3 = 2 * ib, 2 * ib + 1, 2 * (1 - ib), 2 * (1 - ib) + 1
                P.run("vector", lambda E, o0=o0, ib=ib: E.tensor_tensor(
                    po[:, o0, :], pv[:, 2 * ib, :], pt_[:, 2 * ib, :], op=ALU.mult),
                    sv, reads=[R_pv[2 * ib], R_pt[2 * ib]], writes=[R_po[o0]])
                P.run("vector", lambda E, o1=o1, ib=ib: E.tensor_tensor(
                    po[:, o1, :], pv[:, 2 * ib + 1, :], pt_[:, 2 * ib + 1, :],
                    op=ALU.mult),
                    sv, reads=[R_pv[2 * ib + 1], R_pt[2 * ib + 1]], writes=[R_po[o1]])
                P.run("vector", lambda E, o0=o0, o1=o1: E.tensor_tensor(
                    po[:, o0, :], po[:, o0, :], po[:, o1, :], op=ALU.subtract),
                    sv, reads=[R_po[o1]], writes=[R_po[o0]])
                P.dma("gpsimd", p_out[b, 0, :, cs], po[:, o0, :], so,
                      reads=[R_po[o0]])
                P.run("vector", lambda E, o1=o1, ib=ib: E.tensor_tensor(
                    po[:, o1, :], pv[:, 2 * ib, :], pt_[:, 2 * ib + 1, :],
                    op=ALU.mult),
                    sv, reads=[R_pv[2 * ib], R_pt[2 * ib + 1]], writes=[R_po[o1]])
                P.run("vector", lambda E, o1=o1, o3=o3, ib=ib: E.tensor_tensor(
                    po[:, o3, :], pv[:, 2 * ib + 1, :], pt_[:, 2 * ib, :],
                    op=ALU.mult),
                    sv, reads=[R_pv[2 * ib + 1], R_pt[2 * ib]], writes=[R_po[o3]])
                P.run("vector", lambda E, o1=o1, o3=o3: E.tensor_tensor(
                    po[:, o1, :], po[:, o1, :], po[:, o3, :], op=ALU.add),
                    sv, reads=[R_po[o3]], writes=[R_po[o1]])
                P.dma("gpsimd", p_out[b, 1, :, cs], po[:, o1, :], so,
                      reads=[R_po[o1]])
        P.finish("gpsimd")
        P.emit(blk)
    return nc


def build_LcC():
    nc = bass.Bass("TRN2", target_bir_lowering=False)
    pc = nc.dram_tensor("pc", [B, 2, 128, 128 * CCH], BF16, kind="ExternalInput").ap()
    mcc = nc.dram_tensor("mcc", [128, 3 * 128 * 128], BF16, kind="ExternalInput").ap()
    z_out = nc.dram_tensor("zz", [B, 2, 128, 128 * CCH], BF16, kind="ExternalOutput").ap()

    from contextlib import ExitStack
    with ExitStack() as ctx:
        E_ = ctx.enter_context
        yt = E_(nc.sbuf_tensor("ytC", [128, 2, 2, 128], BF16))
        mb = E_(nc.sbuf_tensor("mbC", [128, 2, 3, 128], BF16))
        xe = E_(nc.sbuf_tensor("xeC", [128, 4, 512], BF16))
        ps = E_(nc.psum_tensor("psC", [128, 4, 512], F32))
        _sd = E_(nc.semaphore("sem_d"))
        _st_ = E_(nc.semaphore("sem_t"))
        _sv = E_(nc.semaphore("sem_v"))
        _sg = E_(nc.semaphore("sem_g"))
        _so = E_(nc.semaphore("sem_o"))
        sem_pool = [Sem(E_(nc.semaphore("sp%d" % _i))) for _i in range(24)]
        blk = E_(nc.Block())
        P = Prog(nc, sem_pool)
        sd, st, sv, sg, so = Sem(_sd), Sem(_st_), Sem(_sv), Sem(_sg), Sem(_so)
        Rs = lambda n: [Res() for _ in range(n)]
        R_yt = Rs(2)
        R_mb = Rs(2)
        R_xe = Rs(4)
        R_ps = Rs(4)

        for k1 in range(128):
            wb = k1 % 2
            P.dma("sync", mb[:, wb, :, :],
                  mcc[:, k1 * 384:(k1 + 1) * 384].rearrange("p (m j) -> p m j", m=3),
                  sd, writes=[R_mb[wb]])
            for b in range(B):
                ib = (k1 * 2 + b) % 2
                P.dma("sync", yt[:, ib, 0, :], pc[b, 0, k1, :].rearrange("(j c) -> j c", j=128),
                      sd, writes=[R_yt[ib]])
                P.dma("sync", yt[:, ib, 1, :], pc[b, 1, k1, :].rearrange("(j c) -> j c", j=128),
                      sd, writes=[R_yt[ib]])
                qk = k1 % 4
                br = 2 * b
                P.rd("tensor", [R_mb[wb], R_yt[ib]])
                P.wr("tensor", [R_ps[br]])
                P.wr("tensor", [R_ps[br + 1]])
                for (bank, lhs_i, pl) in ((br, 0, 0), (br, 2, 1), (br + 1, 1, 0), (br + 1, 0, 1)):
                    f = (lambda E, bank=bank, lhs_i=lhs_i, pl=pl, wb=wb, ib=ib, qk=qk:
                         E.matmul(ps[:, bank, 128 * qk:128 * (qk + 1)],
                                  mb[:, wb, lhs_i, :], yt[:, ib, pl, :],
                                  start=(pl == 0), stop=(pl == 1)))
                    st.n += 1
                    P.ops["tensor"].append(lambda E, f=f: f(E).then_inc(_st_, 1))
                    P.done_write([R_ps[bank]], st, st.n)
                    P.done_read([R_mb[wb], R_yt[ib]], st, st.n)
                if qk == 3:
                    for pl, bank in ((0, br), (1, br + 1)):
                        sl = (k1 // 4 + pl + 2 * b) % 4
                        P.run("vector", lambda E, bank=bank, sl=sl: E.tensor_copy(
                            xe[:, sl, :], ps[:, bank, :]),
                            sv, reads=[R_ps[bank]], writes=[R_xe[sl]])
                        P.dma("gpsimd",
                              z_out[b, pl, :, 512 * (k1 // 4):512 * (k1 // 4 + 1)],
                              xe[:, sl, :], so, reads=[R_xe[sl]])
        P.finish("gpsimd")
        P.emit(blk)
    return nc


def build_LcD():
    nc = bass.Bass("TRN2", target_bir_lowering=False)
    zd = nc.dram_tensor("zd", [B, 2, 128, 128 * CCH], BF16, kind="ExternalInput").ap()
    ddc = nc.dram_tensor("ddc", [128, 128], F32, kind="ExternalInput").ap()
    o_out = nc.dram_tensor("oo", [B, N, CCH], BF16, kind="ExternalOutput").ap()

    from contextlib import ExitStack
    with ExitStack() as ctx:
        E_ = ctx.enter_context
        zt = E_(nc.sbuf_tensor("zt", [128, 2, 2, 512], BF16))
        ddb = E_(nc.sbuf_tensor("ddb", [128, 128], BF16))
        oe = E_(nc.sbuf_tensor("oeD", [64, 4, 512], BF16))
        ps = E_(nc.psum_tensor("psD", [128, 4, 512], F32))
        _sd = E_(nc.semaphore("sem_d"))
        _st_ = E_(nc.semaphore("sem_t"))
        _sv = E_(nc.semaphore("sem_v"))
        _sg = E_(nc.semaphore("sem_g"))
        _so = E_(nc.semaphore("sem_o"))
        sem_pool = [Sem(E_(nc.semaphore("sp%d" % _i))) for _i in range(24)]
        blk = E_(nc.Block())
        P = Prog(nc, sem_pool)
        sd, st, sv, sg, so = Sem(_sd), Sem(_st_), Sem(_sv), Sem(_sg), Sem(_so)
        Rs = lambda n: [Res() for _ in range(n)]
        R_zt = Rs(2)
        R_dd = Res()
        R_oe = Rs(4)
        R_ps = Rs(4)

        P.dma("gpsimd", ddb[:, :], ddc[:, :], sg, writes=[R_dd])
        for b in range(B):
            for ch in range(32):
                ib = ch % 2
                cs = slice(512 * ch, 512 * (ch + 1))
                P.dma("sync", zt[:, ib, 0, :], zd[b, 0, :, cs], sd, writes=[R_zt[ib]])
                P.dma("sync", zt[:, ib, 1, :], zd[b, 1, :, cs], sd, writes=[R_zt[ib]])
                bank = ib
                P.rd("tensor", [R_zt[ib], R_dd])
                P.wr("tensor", [R_ps[bank]])
                for pl in range(2):
                    f = (lambda E, pl=pl, ib=ib, bank=bank:
                         E.matmul(ps[0:64, bank, :], ddb[:, 64 * pl:64 * (pl + 1)],
                                  zt[:, ib, pl, :], start=(pl == 0), stop=(pl == 1)))
                    st.n += 1
                    P.ops["tensor"].append(lambda E, f=f: f(E).then_inc(_st_, 1))
                    P.done_write([R_ps[bank]], st, st.n)
                    P.done_read([R_zt[ib], R_dd], st, st.n)
                sl = ch % 4
                P.run("vector", lambda E, bank=bank, sl=sl: E.tensor_copy(
                    oe[:, sl, :], ps[0:64, bank, :]),
                    sv, reads=[R_ps[bank]], writes=[R_oe[sl]])
                m2o = 4 * ch
                P.dma("gpsimd",
                      o_out[b, :, :].rearrange("(m1 m2) c -> m1 m2 c", m2=128)[:, m2o:m2o + 4, :].rearrange("m1 m2 c -> m1 (m2 c)"),
                      oe[:, sl, :], so, reads=[R_oe[sl]])
        P.finish("gpsimd")
        P.emit(blk)
    return nc


def _conv_consts():
    j1 = np.arange(64)
    k1 = np.arange(128)
    j2 = np.arange(128)
    k2 = np.arange(128)
    m1 = np.arange(64)
    m2 = np.arange(128)
    w = np.exp(-2j * np.pi / (2 * N))
    w128 = np.exp(-2j * np.pi / 128)
    wi = np.conj(w)
    wi128 = np.conj(w128)
    DA = w128 ** np.outer(j1, k1)
    dfc = np.concatenate([DA.real, DA.imag], 1).astype(np.float32)
    MB = (w ** np.outer(j2, k1)).T[:, :, None] * (w128 ** np.outer(j2, k2))[None, :, :]
    mbc2 = np.zeros((128, 128, 3, 128), np.float32)
    mbc2[:, :, 0] = MB.real.transpose(1, 0, 2)
    mbc2[:, :, 1] = MB.imag.transpose(1, 0, 2)
    mbc2[:, :, 2] = -MB.imag.transpose(1, 0, 2)
    mbc = mbc2.reshape(128, 3 * 128 * 128)
    MC = (wi128 ** np.outer(k2, m2))[None, :, :] * (wi ** np.outer(k1, m2))[:, None, :] / (2.0 * N)
    mcc2 = np.zeros((128, 128, 3, 128), np.float32)
    mcc2[:, :, 0] = MC.real.transpose(1, 0, 2)
    mcc2[:, :, 1] = MC.imag.transpose(1, 0, 2)
    mcc2[:, :, 2] = -MC.imag.transpose(1, 0, 2)
    mcc = mcc2.reshape(128, 3 * 128 * 128)
    DD = wi128 ** np.outer(np.arange(128), m1)
    ddc = np.concatenate([DD.real, -DD.imag], 1).astype(np.float32)
    return dfc, mbc.astype(BF16NP), mcc.astype(BF16NP), ddc


def host_prep_LcA(inputs, l1_results):
    vchs = [np.zeros((B, CCH, N), BF16NP) for _ in range(NC)]
    for p in range(NC):
        posi = own_positions(p)
        cT = np.asarray(l1_results[p]["cT"])
        for q in range(NC):
            for b in range(B):
                vchs[q][b, :, posi] = cT[CCH * q:CCH * (q + 1),
                                         b * TOKB:(b + 1) * TOKB].T
    dfc, _, _, _ = _conv_consts()
    W = [np.asarray(inputs["L0_attn_rpe_hid_w"][i], np.float32) for i in range(3)]
    w4l = np.zeros((3 * 128, 128), np.float32)
    for li in range(3):
        for g in range(4):
            w4l[128 * li + 32 * g:128 * li + 32 * (g + 1),
                32 * g:32 * (g + 1)] = W[li]
    gb4 = np.zeros((3 * 128, 2), np.float32)
    for li in range(3):
        gb4[128 * li:128 * (li + 1), 0] = np.tile(np.asarray(inputs["L0_attn_rpe_ln_g"][li]), 4)
        gb4[128 * li:128 * (li + 1), 1] = np.tile(np.asarray(inputs["L0_attn_rpe_ln_b"][li]), 4)
    blk1 = np.zeros((128, 4), np.float32)
    for g in range(4):
        blk1[32 * g:32 * (g + 1), g] = 1.0
    blk2 = np.ascontiguousarray(blk1.T)
    posr = np.arange(N, dtype=np.float32).reshape(1, N)
    consts = np.array([[1e-5, 1e-6, 0, 0, 0, 0, 0, 0]], np.float32)
    rpein = np.tile(_f32(inputs["L0_attn_rpe_in_w"]).reshape(32), 4).reshape(128, 1)
    pos4c = np.zeros((128, N // 4), np.float32)
    for g in range(4):
        pos4c[32 * g:32 * (g + 1), :] = np.arange(2048 * g, 2048 * (g + 1),
                                                  dtype=np.float32)[None, :]
    wout = _f32(inputs["L0_attn_rpe_out_w"])
    in_maps = []
    for p in range(NC):
        wog = np.zeros((4 * 128, 128), np.float32)
        wsl = wout[:, CCH * p:CCH * (p + 1)]
        for g in range(4):
            wog[128 * g + 32 * g:128 * g + 32 * (g + 1), :] = wsl
        gamsl = _f32(inputs["L0_attn_decay_gamma"])[CCH * p:CCH * (p + 1)].reshape(1, CCH)
        in_maps.append(dict(vch=vchs[p], rpein=rpein, pos4c=pos4c, w4l=w4l,
                            wog=wog, gb4=gb4, gam=gamsl, posr=posr, blk1=blk1,
                            blk2=blk2, dfc=dfc, constv=consts))
    return in_maps


def run_conv(inputs, l1_results):
    dfc, mbc, mcc, ddc = _conv_consts()
    ncA = _get_prog("LcA", build_LcA)
    resA = run_bass_kernel_spmd(ncA, host_prep_LcA(inputs, l1_results),
                                core_ids=list(range(NC))).results
    ncB = _get_prog("LcB", build_LcB)
    in_mapsB = []
    for p in range(NC):
        ya = np.asarray(resA[p]["ya"]).reshape(3, 2, 128, CCH, 128)
        ybt = np.ascontiguousarray(ya.transpose(0, 1, 2, 4, 3)).reshape(3, 2, 128, 128 * CCH)
        in_mapsB.append(dict(yb=ybt, mbc=mbc))
    resB = run_bass_kernel_spmd(ncB, in_mapsB, core_ids=list(range(NC))).results
    ncC = _get_prog("LcC", build_LcC)
    in_mapsC = []
    for p in range(NC):
        pp = np.asarray(resB[p]["pp"]).reshape(B, 2, 128, 128, CCH)
        pct = np.ascontiguousarray(pp.transpose(0, 1, 3, 2, 4)).reshape(B, 2, 128, 128 * CCH)
        in_mapsC.append(dict(pc=pct, mcc=mcc))
    resC = run_bass_kernel_spmd(ncC, in_mapsC, core_ids=list(range(NC))).results
    ncD = _get_prog("LcD", build_LcD)
    in_mapsD = []
    for p in range(NC):
        zz = np.asarray(resC[p]["zz"]).reshape(B, 2, 128, 128, CCH)
        zdt = np.ascontiguousarray(zz.transpose(0, 1, 3, 2, 4)).reshape(B, 2, 128, 128 * CCH)
        in_mapsD.append(dict(zd=zdt, ddc=ddc))
    resD = run_bass_kernel_spmd(ncD, in_mapsD, core_ids=list(range(NC))).results
    return resD


def build_L5():
    nc = bass.Bass("TRN2", target_bir_lowering=False)
    x_in = nc.dram_tensor("x5", [TOKC, D], F32, kind="ExternalInput").ap()
    proc_in = nc.dram_tensor("proc5", [TOKC, D], F32, kind="ExternalInput").ap()
    updf = nc.dram_tensor("updf", [B * KSEL, D], F32, kind="ExternalInput").ap()
    rks_in = nc.dram_tensor("rks", [TOKC, 2], F32, kind="ExternalInput").ap()
    u_in = nc.dram_tensor("u5", [TOKC, D], BF16, kind="ExternalInput").ap()
    o_in = nc.dram_tensor("o5", [TOKC, D], BF16, kind="ExternalInput").ap()
    wot = nc.dram_tensor("wot", [8 * 128, 8 * 128], F32, kind="ExternalInput").ap()
    identc = nc.dram_tensor("identc", [128, 128], F32, kind="ExternalInput").ap()

    y_out = nc.dram_tensor("y", [TOKC, D], F32, kind="ExternalOutput").ap()

    from contextlib import ExitStack
    with ExitStack() as ctx:
        E_ = ctx.enter_context
        yb = E_(nc.sbuf_tensor("yb5", [128, 4, D], F32))
        pb = E_(nc.sbuf_tensor("pb5", [128, 4, D], F32))
        ub = E_(nc.sbuf_tensor("ub5", [128, 2, D], BF16))
        ob_ = E_(nc.sbuf_tensor("ob5x", [128, 2, D], BF16))
        uo = E_(nc.sbuf_tensor("uo5", [128, 2, D], BF16))
        uoT = E_(nc.sbuf_tensor("uoT", [128, 8, 512], BF16))
        rkb = E_(nc.sbuf_tensor("rkb5", [128, 4, 2], F32))
        gix = E_(nc.sbuf_tensor("gix", [128, 4], F32))
        giu = E_(nc.sbuf_tensor("giu", [128, 1], U32))
        wt = E_(nc.sbuf_tensor("wt5", [128, 2, 8, 128], BF16))
        oev = E_(nc.sbuf_tensor("oev", [128, 2, 512], BF16))
        ident = E_(nc.sbuf_tensor("ident5", [128, 128], BF16))
        ps = E_(nc.psum_tensor("ps5", [128, 2, 512], F32))
        pst = E_(nc.psum_tensor("pst5", [128, 2, 1024], BF16))
        sem_pool = [Sem(E_(nc.semaphore("sp%d" % _i))) for _i in range(24)]
        _st_ = E_(nc.semaphore("sem_t"))
        _sv = E_(nc.semaphore("sem_v"))
        _sa = E_(nc.semaphore("sem_a"))
        _so = E_(nc.semaphore("sem_o"))
        blk = E_(nc.Block())
        P = Prog(nc, sem_pool)
        st, sv, sa, so = Sem(_st_), Sem(_sv), Sem(_sa), Sem(_so)
        Rs = lambda n: [Res() for _ in range(n)]
        R_yb = Rs(4)
        R_pb = Rs(4)
        R_ub, R_ob, R_uo = Rs(2), Rs(2), Rs(2)
        R_uoT = Res()
        R_rkb = Rs(4)
        R_gix, R_giu = Res(), Res()
        R_wt = Rs(2)
        R_oev = Rs(2)
        R_ident = Res()
        R_ps = Rs(2)
        R_pst = Rs(2)

        P.dma("gpsimd", ident[:, :], identc[:, :], None, writes=[R_ident])

        for ttok in range(4):
            for sub in range(4):
                ti = 4 * ttok + sub
                bb = ti // 8
                P.dma("gpsimd", pb[:, sub, :], proc_in[128 * ti:128 * (ti + 1), :],
                      None, writes=[R_pb[sub]])
                P.dma("sync", rkb[:, sub, :], rks_in[128 * ti:128 * (ti + 1), :],
                      None, writes=[R_rkb[sub]])
                P.run("vector", lambda E, sub=sub, bb=bb: E.tensor_scalar(
                    gix[:, sub:sub + 1], rkb[:, sub, 1:2], float(-1e9), float(1e9),
                    op0=ALU.mult, op1=ALU.add),
                    sv, reads=[R_rkb[sub]], writes=[R_gix])
                P.run("vector", lambda E, sub=sub, bb=bb: E.scalar_tensor_tensor(
                    gix[:, sub:sub + 1], rkb[:, sub, 0:1], float(1.0),
                    gix[:, sub:sub + 1], op0=ALU.mult, op1=ALU.add),
                    sv, reads=[R_rkb[sub], R_gix], writes=[R_gix])
                P.run("vector", lambda E, sub=sub, bb=bb: E.tensor_scalar(
                    gix[:, sub:sub + 1], gix[:, sub:sub + 1], float(bb * KSEL), None,
                    op0=ALU.add),
                    sv, reads=[R_gix], writes=[R_gix])
                P.run("vector", lambda E, sub=sub: E.tensor_copy(
                    giu[:, :], gix[:, sub:sub + 1]),
                    sv, reads=[R_gix], writes=[R_giu])
                P.run("gpsimd", lambda E, sub=sub: E.indirect_dma_start(
                    out=pb[:, sub, :], out_offset=None, in_=updf[:, :],
                    in_offset=bass.IndirectOffsetOnAxis(ap=giu[:, :], axis=0),
                    bounds_check=B * KSEL - 1, oob_is_err=False),
                    P._dma_sem(R_pb[sub]), k=16, reads=[R_giu], writes=[R_pb[sub]])
                P.dma("sync", yb[:, sub, :], x_in[128 * ti:128 * (ti + 1), :],
                      None, writes=[R_yb[sub]])
                P.run("vector", lambda E, sub=sub: E.tensor_tensor(
                    yb[:, sub, :], yb[:, sub, :], pb[:, sub, :], op=ALU.add),
                    sv, reads=[R_pb[sub]], writes=[R_yb[sub]])
                us = sub % 2
                P.dma("sync", ub[:, us, :], u_in[128 * ti:128 * (ti + 1), :],
                      None, writes=[R_ub[us]])
                P.dma("sync", ob_[:, us, :], o_in[128 * ti:128 * (ti + 1), :],
                      None, writes=[R_ob[us]])
                P.run("vector", lambda E, us=us: E.tensor_tensor(
                    uo[:, us, :], ub[:, us, :], ob_[:, us, :], op=ALU.mult),
                    sv, reads=[R_ub[us], R_ob[us]], writes=[R_uo[us]])
                for c in range(8):
                    pb2 = (ti * 8 + c) % 2
                    P.run("tensor", lambda E, us=us, c=c, pb2=pb2: E.transpose(
                        pst[:, pb2, 0:128], uo[:, us, 128 * c:128 * (c + 1)],
                        ident[:, :]),
                        st, reads=[R_uo[us], R_ident], writes=[R_pst[pb2]])
                    P.run("vector", lambda E, sub=sub, c=c, pb2=pb2: E.tensor_copy(
                        uoT[:, c, 128 * sub:128 * (sub + 1)], pst[:, pb2, 0:128]),
                        sv, reads=[R_pst[pb2]], writes=[R_uoT])
            for oc in range(8):
                wb = oc % 2
                P.dma("gpsimd", wt[:, wb, :, :],
                      wot[128 * oc:128 * (oc + 1), :].rearrange("p (c j) -> p c j", c=8),
                      None, writes=[R_wt[wb]])
                bank = oc % 2
                P.rd("tensor", [R_wt[wb], R_uoT])
                P.wr("tensor", [R_ps[bank]])
                for c in range(8):
                    f = (lambda E, c=c, wb=wb, bank=bank:
                         E.matmul(ps[:, bank, :], wt[:, wb, c, :], uoT[:, c, :],
                                  start=(c == 0), stop=(c == 7)))
                    if c == 7:
                        st.n += 1
                        P.ops["tensor"].append(lambda E, f=f: f(E).then_inc(_st_, 1))
                        P.done_write([R_ps[bank]], st, st.n)
                        P.done_read([R_wt[wb], R_uoT], st, st.n)
                    else:
                        P.ops["tensor"].append(lambda E, f=f: f(E))
                sl = oc % 2
                P.run("scalar", lambda E, bank=bank, sl=sl: E.activation(
                    oev[:, sl, :], ps[:, bank, :], AF.Identity),
                    sa, reads=[R_ps[bank]], writes=[R_oev[sl]])
                for sub in range(4):
                    pb2 = (oc * 4 + sub) % 2
                    P.run("tensor", lambda E, sl=sl, sub=sub, pb2=pb2: E.transpose(
                        pst[:, pb2, 0:128], oev[:, sl, 128 * sub:128 * (sub + 1)],
                        ident[:, :]),
                        st, reads=[R_oev[sl], R_ident], writes=[R_pst[pb2]])
                    P.run("vector", lambda E, sub=sub, oc=oc, pb2=pb2: E.tensor_tensor(
                        yb[:, sub, 128 * oc:128 * (oc + 1)],
                        yb[:, sub, 128 * oc:128 * (oc + 1)],
                        pst[:, pb2, 0:128], op=ALU.add),
                        sv, reads=[R_pst[pb2]], writes=[R_yb[sub]])
            for sub in range(4):
                ti = 4 * ttok + sub
                P.dma("sync", y_out[128 * ti:128 * (ti + 1), :], yb[:, sub, :],
                      so, reads=[R_yb[sub]])
        P.finish("sync")
        P.emit(blk)
    return nc


def host_prep_L5(inputs, l1_results, l2b_results, l3_results, l4_results, lc_results):
    x = np.asarray(inputs["L0_attn_x"])
    updf = np.zeros((B * KSEL, D), np.float32)
    for p in range(NC):
        upd = np.asarray(l4_results[p]["updo"])
        bases = [(0, 256 * p), (0, 256 * p + 128),
                 (1, 256 * (7 - p)), (1, 256 * (7 - p) + 128)]
        for tl, (b, base) in enumerate(bases):
            updf[b * KSEL + base:b * KSEL + base + 128] = upd[128 * tl:128 * (tl + 1)]
    rkm = np.asarray(l3_results[0]["rkm"])
    rank_flat = rkm[:, 0:128].reshape(-1)
    selm_flat = rkm[:, 128:256].reshape(-1)
    wotl = _tile_w(inputs["L0_attn_W_o"], 8, 8)
    ident = np.eye(128, dtype=np.float32)
    in_maps = []
    for p in range(NC):
        posi = own_positions(p)
        x_own = _f32(np.concatenate([np.asarray(x[b])[posi] for b in range(B)], 0))
        rks = np.zeros((TOKC, 2), np.float32)
        for b in range(B):
            rks[b * TOKB:(b + 1) * TOKB, 0] = rank_flat[b * N + posi]
            rks[b * TOKB:(b + 1) * TOKB, 1] = selm_flat[b * N + posi]
        uT = np.asarray(l1_results[p]["uT"])
        u_own = np.ascontiguousarray(uT.T)
        o_own = np.zeros((TOKC, D), BF16NP)
        for q in range(NC):
            oo = np.asarray(lc_results[q]["oo"])
            for b in range(B):
                o_own[b * TOKB:(b + 1) * TOKB, 128 * q:128 * (q + 1)] = oo[b][posi]
        in_maps.append(dict(x5=x_own, proc5=np.asarray(l2b_results[p]["proc"]),
                            updf=updf, rks=rks, u5=u_own, o5=o_own, wot=wotl,
                            identc=ident))
    return in_maps


def run_L5(inputs, l1_results, l2b_results, l3_results, l4_results, lc_results):
    nc = _get_prog("L5", build_L5)
    in_maps = host_prep_L5(inputs, l1_results, l2b_results, l3_results,
                           l4_results, lc_results)
    res = run_bass_kernel_spmd(nc, in_maps, core_ids=list(range(NC)))
    return res.results

def kernel(**inputs):
    l1 = run_L1(inputs)
    l2a = run_L2a(l1)
    l2b = run_L2b(inputs, l1, l2a)
    lc = run_conv(inputs, l1)
    l3 = run_L3(inputs, l2b)
    l4 = run_L4(inputs, l3)
    l5 = run_L5(inputs, l1, l2b, l3, l4, lc)
    out = np.zeros((B, N, D), np.float32)
    for p in range(NC):
        y = np.asarray(l5[p]["y"])
        posi = own_positions(p)
        for b in range(B):
            out[b, posi] = y[b * TOKB:(b + 1) * TOKB]
    return out
